# version 07
# speedup vs baseline: 1.1501x; 1.0032x over previous
import contextlib
import numpy as np
import concourse.bass as bass
import concourse.mybir as mybir
from concourse.bass_core_rust import IndirectOffsetOnAxis

F32 = mybir.dt.float32
BF16 = mybir.dt.bfloat16
I32 = mybir.dt.int32
I16 = mybir.dt.int16
from concourse.library_config import mlp as MLP_LIB
U32 = mybir.dt.uint32
AF = mybir.ActivationFunctionType
ALU = mybir.AluOpType
AX = mybir.AxisListType

T = 4096
D = 1024
NTILE = T // 128


class Buf:
    __slots__ = ("name", "writer", "readers")
    def __init__(self, name=""):
        self.name = name
        self.writer = None
        self.readers = []


class Eng:
    def __init__(self, P, name, has_q=False):
        self.P = P
        self.name = name
        self.n = 0
        self.qn = 0
        self.seen = {}
        self.items = []
        self.semkey = name
        self.qsemkey = name + "_q"

    def _wait(self, semkey, val):
        if semkey == self.semkey and val > self.n:
            return
        if self.seen.get(semkey, 0) >= val:
            return
        self.seen[semkey] = val
        self.items.append(("wait", semkey, val))

    def _pre(self, reads, writes):
        for r in reads:
            if r.writer is not None:
                self._wait(*r.writer)
        for w in writes:
            if w.writer is not None:
                self._wait(*w.writer)
            for rd in w.readers:
                self._wait(*rd)

    def _post(self, reads, writes, tok):
        for w in writes:
            w.writer = tok
            w.readers = []
        for r in reads:
            r.readers.append(tok)

    def op(self, fn, reads=(), writes=()):
        return self.ops([fn], reads, writes)

    def ops(self, fns, reads=(), writes=()):
        self._pre(reads, writes)
        self.n += 1
        k, nn = self.semkey, self.n
        def run(b, S, inc, fns=tuple(fns), k=k):
            inst = None
            for f in fns:
                inst = f(b)
            if inc:
                inst.then_inc(S[k], inc)
        self.items.append(("op", k, nn, run))
        tok = (k, nn)
        self._post(reads, writes, tok)
        return tok

    def dma(self, out, in_, reads=(), writes=(), **kw):
        self._pre(reads, writes)
        self.qn += 16
        k, nn = self.qsemkey, self.qn
        self.items.append(("dma", k, lambda b, S: b.dma_start(out=out, in_=in_, **kw).then_inc(S[k], 16)))
        tok = (k, nn)
        self._post(reads, writes, tok)
        return tok

    def dma_transpose(self, out, in_, reads=(), writes=()):
        self._pre(reads, writes)
        self.qn += 16
        k, nn = self.qsemkey, self.qn
        self.items.append(("dma", k, lambda b, S: b.dma_start_transpose(out, in_).then_inc(S[k], 16)))
        tok = (k, nn)
        self._post(reads, writes, tok)
        return tok

    def indirect(self, out, out_off, in_, in_off, reads=(), writes=(), bound=None):
        self._pre(reads, writes)
        self.qn += 16
        k, nn = self.qsemkey, self.qn
        import os as _os
        tag = None
        if _os.environ.get("MK_DBG"):
            import traceback as _tb
            tag = "".join(_tb.format_stack()[-4:-1])
        def _run(b, S, k=k, out=out, out_off=out_off, in_=in_, in_off=in_off,
                 bound=bound, tag=tag):
            try:
                return b.indirect_dma_start(
                    out, out_off, in_, in_off, bounds_check=bound,
                    oob_is_err=False).then_inc(S[k], 16)
            except Exception:
                if tag:
                    print("INDIRECT FAIL AT:\n" + tag, flush=True)
                    print("OUT:", repr(out), flush=True)
                    print("IN_:", repr(in_), flush=True)
                    print("OUT_OFF:", repr(out_off), flush=True)
                    print("IN_OFF:", repr(in_off), flush=True)
                    print("BOUND:", bound, flush=True)
                raise
        self.items.append(("dma", k, _run))
        tok = (k, nn)
        self._post(reads, writes, tok)
        return tok

    def load_lib(self, lib=None, reads=(), writes=()):
        if lib is None:
            lib = MLP_LIB
        self._pre(reads, writes)
        self.n += 1
        k, nn = self.semkey, self.n
        self.items.append(("op", k, nn, lambda b, S, inc: b.load_library(
            lib).then_inc(S[k], max(inc, 1))))
        tok = (k, nn)
        self._post(reads, writes, tok)
        return tok

    def gather_ant(self, out, in_, idxs, num_idxs, num_valid, elem_size,
                   elem_step=None, reads=(), writes=(), transpose=False,
                   queue_num=0):
        self._pre(reads, writes)
        self.qn += 16
        k, nn = self.qsemkey, self.qn
        self.items.append(("dma", k, lambda b, S: b.dma_gather(
            out, in_, idxs, num_idxs, num_valid, elem_size,
            elem_step=elem_step, transpose=transpose,
            queue_num=queue_num).then_inc(S[k], 16)))
        tok = (k, nn)
        self._post(reads, writes, tok)
        return tok

    def scatter_add_ant(self, out, in_, idxs, num_idxs, num_valid, elem_size,
                        elem_step=None, reads=(), writes=(), queue_num=0):
        self._pre(reads, writes)
        self.qn += 16
        k, nn = self.qsemkey, self.qn
        self.items.append(("dma", k, lambda b, S: b.dma_scatter_add(
            out, in_, idxs, num_idxs, num_valid, elem_size,
            elem_step=elem_step, queue_num=queue_num).then_inc(S[k], 16)))
        tok = (k, nn)
        self._post(reads, writes, tok)
        return tok

    def collective(self, kind, op, groups, ins, outs, reads=(), writes=()):
        self._pre(reads, writes)
        self.n += 1
        k, nn = self.semkey, self.n
        self.items.append(("op", k, nn, lambda b, S, inc: b.collective_compute(
            kind, op, replica_groups=groups, ins=ins, outs=outs).then_inc(S[k], max(inc, 1))))
        tok = (k, nn)
        self._post(reads, writes, tok)
        return tok

    def matmul_acc(self, psum_aps, pairs, reads=(), writes=(), flags=None):
        self._pre(reads, writes)
        self.n += 1
        k, nn = self.semkey, self.n
        n = len(pairs)
        if not isinstance(psum_aps, (list, tuple)):
            psum_aps = [psum_aps] * n
        if flags is None:
            flags = [(i == 0, i == n - 1) for i in range(n)]
        def run(b, S, inc, psum_aps=tuple(psum_aps), pairs=tuple(pairs), flags=tuple(flags)):
            inst = None
            for pa, (lhsT, rhs), (st, sp) in zip(psum_aps, pairs, flags):
                inst = b.matmul(pa, lhsT, rhs, start=st, stop=sp)
            if inc:
                inst.then_inc(S[k], inc)
        self.items.append(("op", k, nn, run))
        tok = (k, nn)
        self._post(reads, writes, tok)
        return tok

    def transpose(self, out_psum, in_sbuf, ident, reads=(), writes=()):
        self._pre(reads, writes)
        self.n += 1
        k, nn = self.semkey, self.n
        self.items.append(("op", k, nn, lambda b, S, inc: b.transpose(out_psum, in_sbuf, ident).then_inc(S[k], inc) if inc else b.transpose(out_psum, in_sbuf, ident)))
        tok = (k, nn)
        self._post(reads, writes, tok)
        return tok


class Prog:
    def __init__(self, name="mk", dyn_dma=16384):
        self.nc = bass.Bass("TRN2", target_bir_lowering=False,
                            dynamic_dma_scratch_size=dyn_dma)
        self.es = contextlib.ExitStack()
        self._consts = {}
        self._sb_names = 0
        self.sync = Eng(self, "sync")
        self.vector = Eng(self, "vector")
        self.scalar = Eng(self, "scalar")
        self.tensor = Eng(self, "tensor")
        self.gpsimd = Eng(self, "gpsimd")
        self.sync.semkey = self.sync.qsemkey

    def din(self, name, shape, dtype=F32):
        return self.nc.dram_tensor(name, list(shape), dtype, kind="ExternalInput").ap()

    def dout(self, name, shape, dtype=F32):
        return self.nc.dram_tensor(name, list(shape), dtype, kind="ExternalOutput").ap()

    def dscratch(self, name, shape, dtype=F32):
        return self.nc.dram_tensor(name, list(shape), dtype, kind="Internal").ap()

    def const(self, name, arr):
        self._consts[name] = arr
        if arr.dtype == np.float32:
            dt = F32
        elif arr.dtype == np.int32:
            dt = I32
        else:
            import ml_dtypes
            assert arr.dtype == ml_dtypes.bfloat16, arr.dtype
            dt = BF16
        return self.nc.dram_tensor(name, list(arr.shape), dt, kind="ExternalInput").ap()

    def sb(self, shape, dtype=F32, name=None):
        if name is None:
            self._sb_names += 1
            name = f"sb{self._sb_names}"
        st = self._phase_stacks[-1] if getattr(self, "_phase_stacks", None) else self.es
        return st.enter_context(self.nc.sbuf_tensor(name, list(shape), dtype))

    def barrier(self):
        engines = (self.sync, self.vector, self.scalar, self.tensor, self.gpsimd)
        snap = [(e, e.semkey, e.n, e.qsemkey, e.qn) for e in engines]
        for e in engines:
            for (o, sk, n, qk, qn) in snap:
                if o is not e and n:
                    e._wait(sk, n)
                if qn:
                    e._wait(qk, qn)

    import contextlib as _ctxlib

    @_ctxlib.contextmanager
    def phase(self, name=""):
        import os
        if not hasattr(self, "_phase_stacks"):
            self._phase_stacks = []
        st = contextlib.ExitStack()
        self._phase_stacks.append(st)
        try:
            yield st
        finally:
            self.barrier()
            self._phase_stacks.pop()
            if os.environ.get("MK_NO_FREE"):
                self.es.enter_context(st.pop_all())
            else:
                st.close()

    def psum(self, shape, dtype=F32, name=None):
        if name is None:
            self._sb_names += 1
            name = f"ps{self._sb_names}"
        return self.es.enter_context(self.nc.psum_tensor(name, list(shape), dtype))

    def final_barrier(self):
        for e in (self.sync, self.vector, self.scalar, self.tensor, self.gpsimd):
            if e.qn:
                self.gpsimd.items.append(("wait", e.qsemkey, e.qn))
            if e.n and e is not self.gpsimd:
                self.gpsimd.items.append(("wait", e.semkey, e.n))

    def finalize(self, sim=False):
        import bisect, os
        if os.environ.get("MK_HW_DENSE"):
            sim = True
        dense_inc = bool(os.environ.get("MK_T_DENSEINC"))
        keepsame = True
        engines = (self.sync, self.vector, self.scalar, self.tensor, self.gpsimd)
        semnames = set()
        for e in engines:
            semnames.add(e.semkey)
            semnames.add(e.qsemkey)
        import os
        same_edges = True
        enames = {e.semkey for e in engines} - {e.qsemkey for e in engines}
        waited = {k: set() for k in enames}
        for e in engines:
            for it in e.items:
                if it[0] == "wait" and it[1] in enames and (same_edges or it[1] != e.semkey):
                    waited[it[1]].add(it[2])
        remap = {}
        for k, vs in waited.items():
            sv = sorted(vs)
            remap[k] = ({v: i + 1 for i, v in enumerate(sv)}, sv)
        bounds = {}
        for e in engines:
            q = e.qsemkey
            cum = 0
            bl = []
            for it in e.items:
                if it[0] == "dma" and it[1] == q:
                    cum += 16
                elif it[0] == "wait":
                    bl.append(cum)
            bl.append(cum)
            bounds[q] = sorted(set(bl))
        S = {n: self.es.enter_context(self.nc.semaphore(n)) for n in sorted(semnames)}
        block = self.es.enter_context(self.nc.Block())
        for eng, opener in ((self.sync, block.sync), (self.vector, block.vector),
                            (self.scalar, block.scalar), (self.tensor, block.tensor),
                            (self.gpsimd, block.gpsimd)):
            items = eng.items
            if not items:
                continue
            def body(b, items=items, eng=eng):
                cum = {}
                for it in items:
                    if it[0] == "wait":
                        _, k, v = it
                        if k in enames:
                            if sim:
                                b.wait_ge(S[k], v)
                            else:
                                if k == eng.semkey and not same_edges:
                                    continue
                                if dense_inc:
                                    b.wait_ge(S[k], v)
                                else:
                                    m, sv = remap[k]
                                    b.wait_ge(S[k], m[v])
                        else:
                            if sim:
                                b.wait_ge(S[k], v)
                            else:
                                bl = bounds.get(k)
                                if bl:
                                    i = bisect.bisect_left(bl, v)
                                    v = bl[i] if i < len(bl) else bl[-1]
                                b.wait_ge(S[k], v)
                    elif it[0] == "dma":
                        it[-1](b, S)
                        if sim or os.environ.get("MK_DMA_FENCE"):
                            k = it[1]
                            cum[k] = cum.get(k, 0) + 16
                            b.wait_ge(S[k], cum[k])
                    else:
                        _, k, nn, run = it
                        if sim or dense_inc:
                            run(b, S, 1)
                        else:
                            run(b, S, 1 if nn in waited[k] else 0)
            opener(body)
        return S




import numpy as np
import ml_dtypes


BF = ml_dtypes.bfloat16


def host_constants():
    C = {}
    C["ident_bf"] = np.eye(128, dtype=BF)
    C["ident_f32"] = np.eye(128, dtype=np.float32)
    inv = 1.0 / 10000.0 ** (np.arange(0, 64, 2, dtype=np.float32) / 64)
    fr = np.outer(inv, np.arange(T, dtype=np.float32))
    C["rope_cos"] = np.cos(fr).astype(BF)
    C["rope_sin"] = np.sin(fr).astype(BF)
    tk = np.arange(128)[:, None]; tq = np.arange(128)[None, :]
    C["mask_diag"] = (tk <= tq).astype(BF)
    q = np.arange(128)[:, None]; p = np.arange(128)[None, :]
    C["tri_strict"] = (q < p).astype(BF)
    C["ones_col"] = np.ones((128, 1), dtype=BF)
    C["ones_row"] = np.ones((1, 128), dtype=BF)
    C["iota_tok"] = (np.arange(128)[:, None] + 128.0 * np.arange(NTILE)[None, :]).astype(np.float32)
    C["iota8"] = np.broadcast_to(np.arange(8, dtype=np.float32)[None, :], (128, 8)).copy()
    C["eps_cols"] = np.array([[1e-5, 1e-3, -8.0, 0.0]], np.float32).repeat(128, 0)
    kc = np.arange(128)[:, None]; qq = np.arange(64)[None, :]
    C["mask_bucket"] = ((kc < 64) | (kc - 64 <= qq)).astype(BF)
    C["mask_bucket0"] = ((kc < 64) & (kc <= qq)).astype(BF)
    import math
    gam = (1.0 - np.exp(np.linspace(math.log(1 / 32), math.log(1 / 512), 8))).astype(np.float64)
    pp = np.arange(128)[:, None]
    C["gamma_pow"] = (gam[None, :] ** pp).astype(np.float32)
    C["gamma_npow"] = (gam[None, :] ** (-pp)).astype(np.float32)
    C["tie64"] = np.broadcast_to(np.linspace(0, 1e-6, 16, dtype=np.float32)[None, :],
                                 (64, 16)).copy()
    bd = np.zeros((128, 8), dtype=BF)
    for tt_ in range(128):
        bd[tt_, tt_ // 16] = 1
    C["blockdiag16"] = bd
    t32 = np.arange(32)[:, None]; s32 = np.arange(32)[None, :]
    C["mask_strict32"] = (s32 < t32).astype(BF)
    hs = np.zeros((2, 128), dtype=BF)
    hs[0, 0:64] = 1; hs[1, 64:128] = 1
    C["headsel"] = hs
    return C


class PsumTile:
    def __init__(self, ap, name):
        self.ap = ap
        self.buf = Buf(name)


def make_ctx(P):
    ctx = {}
    consts = host_constants()
    ctx["c"] = {}
    for k, v in consts.items():
        ap = P.const("c_" + k, v)
        sb = P.sb(list(v.shape), {np.dtype(np.float32): F32}.get(v.dtype, BF16), name="sb_" + k)
        P.sync.dma(sb[:], ap)
        ctx["c"][k] = (sb, Buf("c_" + k))
        ctx["c"][k][1].writer = (P.sync.qsemkey, P.sync.qn)
    ctx["pf"] = [PsumTile(P.psum([128, 512], F32, name=f"psf{i}"), f"pf{i}") for i in range(6)]
    ctx["pb"] = [PsumTile(P.psum([128, 1024], BF16, name=f"psb{i}"), f"pb{i}") for i in range(2)]
    return ctx


def cref(ctx, name):
    return ctx["c"][name]


class WeightFM:
    def __init__(self, P, name, d_in, d_out, alloc=True):
        self.d_in, self.d_out = d_in, d_out
        self.nc_in = d_in // 128
        if alloc:
            self.sb = P.sb([128, self.nc_in, d_out], BF16, name=name)
        self.buf = Buf(name)

    def load(self, P, src_ap):
        P.gpsimd.dma(self.sb[:], src_ap.rearrange("(c p) o -> p c o", p=128),
                     writes=[self.buf])
        return self

    def lhsT(self, dc, o0, o1):
        return self.sb[:, dc, o0:o1]


def load_weight_T_cols(P, name, src_ap, d_out, d_in, c0, c1):
    assert c0 % 128 == 0 and (c1 - c0) == 128
    tr = WeightFM(P, name, d_in, 128)
    with P.phase(name + "_ld"):
        nat = P.sb([128, d_in], BF16, name=name + "_nat"); nat_b = Buf()
        P.gpsimd.dma(nat[:], src_ap[c0:c1, :], writes=[nat_b])
        for dc in range(d_in // 128):
            P.sync.dma_transpose(tr.sb[:, dc, 0:128],
                                 nat[:, dc * 128:(dc + 1) * 128],
                                 reads=[nat_b], writes=[tr.buf])
    return tr


def load_weight_T(P, name, src_ap, d_out, d_in):
    tr = WeightFM(P, name, d_in, d_out)
    with P.phase(name + "_ld"):
        nat = WeightFM(P, name + "_nat", d_out, d_in).load(P, src_ap)
        for oc in range(d_out // 128):
            for dc in range(d_in // 128):
                P.sync.dma_transpose(tr.sb[:, dc, oc * 128:(oc + 1) * 128],
                                     nat.sb[:, oc, dc * 128:(dc + 1) * 128],
                                     reads=[nat.buf], writes=[tr.buf])
    return tr


def emit_proj_fm(P, ctx, w, xn_fm_sb, xn_fm_buf, out_cb, o_chunks, t_tiles=None,
                 n_t=512, extra_reads=()):
    tt_list = t_tiles if t_tiles is not None else range(T // n_t)
    slot = 0
    for oc in o_chunks:
        for tt in tt_list:
            pt = ctx["pf"][slot % 2]
            slot += 1
            t0 = tt * n_t
            pairs = [(w.lhsT(dc, oc * 128, (oc + 1) * 128),
                      xn_fm_sb[:, dc, t0:t0 + n_t]) for dc in range(w.nc_in)]
            P.tensor.matmul_acc(pt.ap[:, 0:n_t], pairs,
                                reads=[w.buf, xn_fm_buf] + list(extra_reads),
                                writes=[pt.buf])
            out_cb(P, ctx, oc, tt, pt)


def emit_pre_pass(P, ctx, x_src, adds, out_x, xn_tok, xn_fm_sb, xn_fm_buf,
                  router_hook=None, eps=1e-5, final=False, dbg=None,
                  xn_tok2=None):
    v, s, t, sy = P.vector, P.scalar, P.tensor, P.sync
    x_src, x_src_b = x_src
    adds_b = [a[1] for a in adds]; adds = [a[0] for a in adds]
    out_x, out_x_b = out_x if out_x is not None else (None, None)
    xn_tok, xn_tok_b = xn_tok if xn_tok is not None else (None, None)
    xn_tok2, xn_tok2_b = xn_tok2 if xn_tok2 is not None else (None, None)
    idf_sb, idf_buf = cref(ctx, "ident_f32")
    xt = [P.sb([128, D], F32) for _ in range(2)]
    at = [P.sb([128, D], F32) for _ in range(2)]
    bt = [P.sb([128, D], F32) for _ in range(2)]
    xnb = [P.sb([128, D], BF16) for _ in range(2)]
    xnf = [P.sb([128, D], F32) for _ in range(2)]
    sq = [P.sb([128, D], F32) for _ in range(2)]
    fchunk = [P.sb([128, 128], F32) for _ in range(2)]
    stat = [[P.sb([128, 1], F32) for _ in range(4)] for _ in range(2)]
    xtb = [Buf() for _ in range(2)]; atb = [Buf() for _ in range(2)]
    btb = [Buf() for _ in range(2)]; xnbb = [Buf() for _ in range(2)]
    xnfb = [Buf() for _ in range(2)]; fcb = [Buf() for _ in range(2)]
    sqb = [Buf() for _ in range(2)]
    statb = [Buf() for _ in range(2)]
    for i in range(NTILE):
        b = i % 2
        r = slice(i * 128, (i + 1) * 128)
        sy.dma(xt[b][:], x_src[r, :], reads=[x_src_b], writes=[xtb[b]])
        for j, ap in enumerate(adds):
            dst = (at, bt)[j]; dstb = (atb, btb)[j]
            sy.dma(dst[b][:], ap[r, :], reads=[adds_b[j]], writes=[dstb[b]])
        rd = [xtb[b]] + [(atb, btb)[j][b] for j in range(len(adds))]
        if adds:
            ops = [lambda bb, b=b: bb.tensor_add(xt[b][:], xt[b][:], at[b][:])]
            if len(adds) > 1:
                ops.append(lambda bb, b=b: bb.tensor_add(xt[b][:], xt[b][:], bt[b][:]))
            v.ops(ops, reads=rd, writes=[xtb[b]])
        if out_x is not None:
            sy.dma(out_x[r, :], xt[b][:], reads=[xtb[b]], writes=[out_x_b])
        if final:
            continue
        v.op(lambda bb, b=b: bb.tensor_reduce(stat[b][0][:], xt[b][:], AX.X, ALU.add),
             reads=[xtb[b]], writes=[statb[b]])
        v.op(lambda bb, b=b: bb.tensor_scalar_mul(stat[b][0][:], stat[b][0][:], 1.0 / D),
             reads=[statb[b]], writes=[statb[b]])
        v.op(lambda bb, b=b: bb.tensor_scalar(xnf[b][:], xt[b][:], stat[b][0][:], None,
                                              ALU.subtract),
             reads=[xtb[b], statb[b]], writes=[xnfb[b]])
        s.op(lambda bb, b=b: bb.activation(sq[b][:], xnf[b][:], AF.Square),
             reads=[xnfb[b]], writes=[sqb[b]])
        v.op(lambda bb, b=b: bb.tensor_reduce(stat[b][1][:], sq[b][:], AX.X, ALU.add),
             reads=[sqb[b]], writes=[statb[b]])
        epssb, epsbuf = cref(ctx, "eps_cols")
        epscol = epssb[:, (0 if eps == 1e-5 else 1):(1 if eps == 1e-5 else 2)]
        s.op(lambda bb, b=b: bb.activation(stat[b][2][:], stat[b][1][:], AF.Ln,
                                           scale=1.0 / D, bias=epscol),
             reads=[statb[b], epsbuf], writes=[statb[b]])
        s.op(lambda bb, b=b: bb.activation(stat[b][3][:], stat[b][2][:], AF.Exp,
                                           scale=-0.5),
             reads=[statb[b]], writes=[statb[b]])
        v.op(lambda bb, b=b: bb.tensor_scalar(xnf[b][:], xnf[b][:], stat[b][3][:], None,
                                              ALU.mult),
             reads=[statb[b]], writes=[xnfb[b]])
        v.op(lambda bb, b=b: bb.tensor_copy(xnb[b][:], xnf[b][:]),
             reads=[xnfb[b]], writes=[xnbb[b]])
        if xn_tok is not None:
            sy.dma(xn_tok[r, :], xnb[b][:], reads=[xnbb[b]], writes=[xn_tok_b])
        if xn_tok2 is not None:
            sy.dma(xn_tok2[r, :], xnb[b][:], reads=[xnbb[b]], writes=[xn_tok2_b])
        if dbg is not None and i == 0:
            for jj in range(4):
                sy.dma(dbg[:, jj:jj+1], stat[b][jj][:], reads=[statb[b]], allow_slow_non_contiguous=True)
        for dc in range(8):
            pt = ctx["pf"][2 + (dc % 2)]
            P.tensor.transpose(pt.ap[:, 0:128], xnf[b][:, dc * 128:(dc + 1) * 128],
                               idf_sb[:], reads=[xnfb[b], idf_buf], writes=[pt.buf])
            v.op(lambda bb, b=b, dc=dc, pt=pt, i=i: bb.tensor_copy(
                    xn_fm_sb[:, dc, i * 128:(i + 1) * 128], pt.ap[:, 0:128]),
                 reads=[pt.buf], writes=[xn_fm_buf])
            if router_hook is not None:
                s.op(lambda bb, b=b, pt=pt: bb.copy(fchunk[b][:], pt.ap[:, 0:128]),
                     reads=[pt.buf], writes=[fcb[b]])
                router_hook(P, ctx, i, dc, fchunk[b], fcb[b])


def emit_l0_router_hook(P, ctx, wr_sb, wr_buf, rlog_sb, rlog_buf):
    def hook(P, ctx, i, dc, fchunk, fcb):
        pt = ctx["pf"][5]
        P.tensor.matmul_acc(pt.ap[0:128, 0:8],
                            [(fchunk[:], wr_sb[:, dc, :])],
                            reads=[fcb, wr_buf], writes=[pt.buf])
        if dc == 0:
            P.vector.op(lambda bb, pt=pt, i=i: bb.tensor_copy(
                rlog_sb[:, i, :], pt.ap[0:128, 0:8]),
                reads=[pt.buf], writes=[rlog_buf])
        else:
            P.vector.op(lambda bb, pt=pt, i=i: bb.tensor_add(
                rlog_sb[:, i, :], rlog_sb[:, i, :], pt.ap[0:128, 0:8]),
                reads=[pt.buf, rlog_buf], writes=[rlog_buf])
    return hook


def emit_l0_moe(P, ctx, wts, xn_tok, ffn_out, rlog_sb, rlog_buf,
                collective=True, cap=5120, seg=768, warena=None,
                stage="all", disp_io=None):
    v, s, t, sy, g = P.vector, P.scalar, P.tensor, P.sync, P.gpsimd
    NT = NTILE
    Tt = NT * 128
    E = 8
    xn_tok, xn_tok_b = xn_tok
    ffn_out, ffn_out_b = ffn_out if ffn_out is not None else (None, None)
    idb_sb, idb_buf = cref(ctx, "ident_bf")
    tri_sb, tri_buf = cref(ctx, "tri_strict")
    onesr_sb, onesr_buf = cref(ctx, "ones_row")
    onesc_sb, onesc_buf = cref(ctx, "ones_col")
    iota8_sb, iota8_buf = cref(ctx, "iota8")
    iotat_sb, iotat_buf = cref(ctx, "iota_tok")
    pf = ctx["pf"]; pb = ctx["pb"]
    nslot = E * seg

    eout = P.dscratch("l0_eout", [nslot, D], F32)
    eout_b = Buf()
    cc_in = P.dscratch("l0_ccin", [1, 8], F32)
    cc_out = P.dscratch("l0_ccout", [8, 8], F32)

    probs = P.sb([128, 8], F32); probs_b = Buf()
    mcol = P.sb([128, 1], F32); mcol_b = Buf()
    scol = P.sb([128, 1], F32); scol_b = Buf()
    eqt = P.sb([128, 8], F32); eqt_b = Buf()
    pmaxm = P.sb([128, NT], F32); pmaxm_b = Buf()
    idxm = P.sb([128, NT], F32); idxm_b = Buf()
    inds = P.sb([128, E * NT], BF16); inds_b = Buf()
    indsf = P.sb([128, E * NT], F32); indsf_b = Buf()
    for i in range(NT):
        v.op(lambda bb, i=i: bb.reduce_max(mcol[:], rlog_sb[:, i, :], AX.X),
             reads=[rlog_buf], writes=[mcol_b])
        s.op(lambda bb, i=i: bb.activation(probs[:], rlog_sb[:, i, :], AF.Exp,
                                           scale=-1.0, bias=mcol[:]),
             reads=[rlog_buf, mcol_b], writes=[probs_b])
        v.op(lambda bb, i=i: bb.tensor_scalar(probs[:], rlog_sb[:, i, :], mcol[:], None,
                                              ALU.subtract),
             reads=[rlog_buf, mcol_b], writes=[probs_b])
        s.op(lambda bb: bb.activation(probs[:], probs[:], AF.Exp),
             reads=[probs_b], writes=[probs_b])
        v.op(lambda bb: bb.tensor_reduce(scol[:], probs[:], AX.X, ALU.add),
             reads=[probs_b], writes=[scol_b])
        v.op(lambda bb: bb.reciprocal(scol[:], scol[:]),
             reads=[scol_b], writes=[scol_b])
        v.op(lambda bb: bb.tensor_scalar(probs[:], probs[:], scol[:], None, ALU.mult),
             reads=[probs_b, scol_b], writes=[probs_b])
        v.op(lambda bb, i=i: bb.reduce_max(pmaxm[:, i:i + 1], probs[:], AX.X),
             reads=[probs_b], writes=[pmaxm_b])
        v.op(lambda bb, i=i: bb.tensor_scalar(eqt[:], probs[:], pmaxm[:, i:i + 1], None,
                                              ALU.is_ge),
             reads=[probs_b, pmaxm_b], writes=[eqt_b])
        v.op(lambda bb, i=i: bb.tensor_copy(
            inds[:].rearrange("p (e j) -> p e j", e=E)[:, :, i], eqt[:]),
            reads=[eqt_b], writes=[inds_b])
        v.op(lambda bb, i=i: bb.tensor_copy(
            indsf[:].rearrange("p (e j) -> p e j", e=E)[:, :, i], eqt[:]),
            reads=[eqt_b], writes=[indsf_b])
        v.op(lambda bb: bb.tensor_mul(eqt[:], eqt[:], iota8_sb[:]),
             reads=[eqt_b, iota8_buf], writes=[eqt_b])
        v.op(lambda bb, i=i: bb.tensor_reduce(idxm[:, i:i + 1], eqt[:], AX.X, ALU.add),
             reads=[eqt_b], writes=[idxm_b])
    colsum_p = pf[0]
    t.matmul_acc(colsum_p.ap[0:1, 0:E * NT], [(onesc_sb[:], inds[:])],
                 reads=[inds_b, onesc_buf], writes=[colsum_p.buf])
    colrow = P.sb([1, E * NT], F32); colrow_b = Buf()
    s.op(lambda bb: bb.copy(colrow[:], colsum_p.ap[0:1, 0:E * NT]),
         reads=[colsum_p.buf], writes=[colrow_b])
    excl = P.sb([1, E, NT + 1], F32); excl_b = Buf()
    v.op(lambda bb: bb.memset(excl[:, :, 0:1], 0.0), writes=[excl_b])
    zrow = P.sb([1, NT], F32); zrow_b = Buf()
    v.op(lambda bb: bb.memset(zrow[:], 0.0), writes=[zrow_b])
    for e in range(E):
        v.op(lambda bb, e=e: bb.tensor_tensor_scan(
            excl[:, e, 1:NT + 1], colrow[:, e * NT:(e + 1) * NT], zrow[:],
            0.0, ALU.add, ALU.add),
            reads=[colrow_b, zrow_b], writes=[excl_b])
    exclb = P.sb([1, E * NT], BF16); exclb_b = Buf()
    v.op(lambda bb: bb.tensor_copy(exclb[:].rearrange("p (e j) -> p e j", e=E),
                                   excl[:, :, 0:NT]),
         reads=[excl_b], writes=[exclb_b])
    exhf = P.sb([1, E * NT], F32); exhf_b = Buf()
    v.op(lambda bb: bb.tensor_copy(exhf[:], exclb[:]),
         reads=[exclb_b], writes=[exhf_b])
    exlo = P.sb([1, E * NT], F32); exlo_b = Buf()
    v.op(lambda bb: bb.tensor_sub(exlo[:].rearrange("p (e j) -> p e j", e=E),
                                  excl[:, :, 0:NT],
                                  exhf[:].rearrange("p (e j) -> p e j", e=E)),
         reads=[excl_b, exhf_b], writes=[exlo_b])
    exclo = P.sb([1, E * NT], BF16); exclo_b = Buf()
    v.op(lambda bb: bb.tensor_copy(exclo[:], exlo[:]),
         reads=[exlo_b], writes=[exclo_b])
    counts = P.sb([1, 8], F32); counts_b = Buf()
    v.op(lambda bb: bb.tensor_copy(
        counts[:].rearrange("p (e o) -> p e o", o=1), excl[:, :, NT:NT + 1]),
        reads=[excl_b], writes=[counts_b])
    cpfxr = None
    if collective:
        cmask_ap = P.din("coremask", [8, 1], BF16)
        cmask_sb = P.sb([8, 1], BF16); cmask_b = Buf()
        sy.dma(cmask_sb[:], cmask_ap, writes=[cmask_b])
        ccin_b = Buf(); ccout_b = Buf()
        g.dma(cc_in, counts[:], reads=[counts_b], writes=[ccin_b])
        g.collective("AllGather", ALU.bypass, [[0, 1, 2, 3, 4, 5, 6, 7]],
                     [cc_in.opt()], [cc_out.opt()], reads=[ccin_b], writes=[ccout_b])
        ag = P.sb([8, 8], F32); ag_b = Buf()
        g.dma(ag[:], cc_out, reads=[ccout_b], writes=[ag_b])
        agb = P.sb([8, 8], BF16); agb_b = Buf()
        v.op(lambda bb: bb.tensor_copy(agb[:], ag[:]), reads=[ag_b], writes=[agb_b])
        cpfx_p = pf[2]
        t.matmul_acc(cpfx_p.ap[0:1, 0:8], [(cmask_sb[:], agb[:])],
                     reads=[cmask_b, agb_b], writes=[cpfx_p.buf])
        cpfx = P.sb([1, 8], F32); cpfx_b = Buf()
        s.op(lambda bb: bb.copy(cpfx[:], cpfx_p.ap[0:1, 0:8]),
             reads=[cpfx_p.buf], writes=[cpfx_b])
        cpfxr = P.sb([1, E * NT], BF16); cpfxr_b = Buf()
        v.op(lambda bb: bb.tensor_copy(
            cpfxr[:].rearrange("p (e j) -> p e j", e=E),
            cpfx[:].rearrange("p (e o) -> p e o", o=1).broadcast_to([1, E, NT])),
            reads=[cpfx_b], writes=[cpfxr_b])
    Cp = pf[1]
    lpairs = [(tri_sb[:], inds[:]), (onesr_sb[:, 0:128], exclb[:]),
              (onesr_sb[:, 0:128], exclo[:])]
    lreads = [inds_b, tri_buf, exclb_b, exclo_b, onesr_buf]
    lflags = [(True, False), (False, False), (False, True)]
    t.matmul_acc(Cp.ap[:, 0:E * NT], lpairs, reads=lreads, writes=[Cp.buf],
                 flags=lflags)
    tmp = P.sb([128, E * NT], F32); tmp_b = Buf()
    lpos = P.sb([128, NT], F32); lpos_b = Buf()
    v.op(lambda bb: bb.tensor_tensor(tmp[:], Cp.ap[:, 0:E * NT], indsf[:], ALU.mult),
         reads=[Cp.buf, indsf_b], writes=[tmp_b])
    v.op(lambda bb: bb.tensor_reduce(
        lpos[:], tmp[:].rearrange("p (e j) -> p j e", e=E), AX.X, ALU.add),
        reads=[tmp_b], writes=[lpos_b])
    pos = P.sb([128, NT], F32); pos_b = Buf()
    if collective:
        Cp2 = pf[3]
        t.matmul_acc(Cp2.ap[:, 0:E * NT],
                     [(tri_sb[:], inds[:]), (onesr_sb[:, 0:128], exclb[:]),
                      (onesr_sb[:, 0:128], exclo[:]),
                      (onesr_sb[:, 0:128], cpfxr[:])],
                     reads=[inds_b, tri_buf, exclb_b, exclo_b, onesr_buf,
                            cpfxr_b],
                     writes=[Cp2.buf],
                     flags=[(True, False), (False, False), (False, False),
                            (False, True)])
        v.op(lambda bb: bb.tensor_tensor(tmp[:], Cp2.ap[:, 0:E * NT], indsf[:],
                                         ALU.mult),
             reads=[Cp2.buf, indsf_b], writes=[tmp_b])
    else:
        v.op(lambda bb: bb.tensor_tensor(tmp[:], Cp.ap[:, 0:E * NT], indsf[:],
                                         ALU.mult),
             reads=[Cp.buf, indsf_b], writes=[tmp_b])
    v.op(lambda bb: bb.tensor_reduce(
        pos[:], tmp[:].rearrange("p (e j) -> p j e", e=E), AX.X, ALU.add),
        reads=[tmp_b], writes=[pos_b])
    kept = P.sb([128, NT], F32); kept_b = Buf()
    nkept = P.sb([128, NT], F32); nkept_b = Buf()
    factor = P.sb([128, NT], F32); factor_b = Buf()
    slot = P.sb([128, NT], F32); slot_b = Buf()
    slot_i = P.sb([128, NT], I32); slot_i_b = Buf()
    gidx = P.sb([128, NT], F32); gidx_b = Buf()
    gidx_i = P.sb([128, NT], I32); gidx_i_b = Buf()
    tokid_i = P.sb([128, NT], I32); tokid_i_b = Buf()
    v.op(lambda bb: bb.tensor_single_scalar(kept[:], pos[:], float(cap), ALU.is_lt),
         reads=[pos_b], writes=[kept_b])
    v.op(lambda bb: bb.tensor_scalar(nkept[:], kept[:], -1.0, -1.0, ALU.mult,
                                     ALU.subtract),
         reads=[kept_b], writes=[nkept_b])
    v.op(lambda bb: bb.tensor_mul(factor[:], kept[:], pmaxm[:]),
         reads=[kept_b, pmaxm_b], writes=[factor_b])
    v.op(lambda bb: bb.tensor_add(factor[:], factor[:], nkept[:]),
         reads=[factor_b, nkept_b], writes=[factor_b])
    v.op(lambda bb: bb.tensor_single_scalar(slot[:], idxm[:], float(seg), ALU.mult),
         reads=[idxm_b], writes=[slot_b])
    v.op(lambda bb: bb.tensor_add(slot[:], slot[:], lpos[:]),
         reads=[slot_b, lpos_b], writes=[slot_b])
    ovf = P.sb([128, NT], F32); ovf_b = Buf()
    v.op(lambda bb: bb.tensor_single_scalar(ovf[:], lpos[:], float(seg), ALU.is_ge),
         reads=[lpos_b], writes=[ovf_b])
    v.op(lambda bb: bb.scalar_tensor_tensor(slot[:], ovf[:], 1e9, slot[:],
                                            ALU.mult, ALU.add),
         reads=[ovf_b, slot_b], writes=[slot_b])
    v.op(lambda bb: bb.tensor_copy(slot_i[:], slot[:]),
         reads=[slot_b], writes=[slot_i_b])
    v.op(lambda bb: bb.scalar_tensor_tensor(gidx[:], nkept[:], 1e9, slot[:],
                                            ALU.mult, ALU.add),
         reads=[nkept_b, slot_b], writes=[gidx_b])
    v.op(lambda bb: bb.tensor_copy(gidx_i[:], gidx[:]),
         reads=[gidx_b], writes=[gidx_i_b])
    v.op(lambda bb: bb.tensor_copy(tokid_i[:], iotat_sb[:, 0:NT]),
         reads=[iotat_buf], writes=[tokid_i_b])
    disp_ap, disp_b = disp_io
    if stage in ("all", "disp"):
        xtile = [P.sb([128, D], BF16) for _ in range(2)]
        xtb = [Buf() for _ in range(2)]
        for j in range(NT):
            b2 = j % 2
            sy.dma(xtile[b2][:], xn_tok[j * 128:(j + 1) * 128, :],
                   reads=[xn_tok_b], writes=[xtb[b2]])
            g.indirect(disp_ap, IndirectOffsetOnAxis(slot_i[:, j:j + 1], 0),
                       xtile[b2][:], None, reads=[slot_i_b, xtb[b2]],
                       writes=[disp_b], bound=nslot - 1)
    if stage == "disp":
        return
    with P.phase("l0exp"):
        if warena is None:
            warena = P.sb([128, 32768], BF16, name="l0warena")
        w1v = warena[:].rearrange("p (c f) -> p c f", c=8)
        w2v = warena[:].rearrange("p (c f) -> p c f", c=32)
        warena_b = Buf()
        sub = min(384, seg)
        nsub = seg // sub
        nst = sub // 128
        harena = P.sb([128, 32, sub], BF16, name="l0harena")
        harena_b = Buf()
        dispfm = P.sb([128, 8, sub], BF16, name="l0disp")
        dispfm_b = Buf()
        smcol = P.sb([128, 1], I32); smcol_b = Buf()
        dtok = [P.sb([128, D], BF16) for _ in range(2)]
        dtok_b = [Buf() for _ in range(2)]
        yev = [P.sb([128, 512], F32) for _ in range(2)]
        yev_b = [Buf() for _ in range(2)]
        for e in range(E):
          for sh in range(nsub):
            s0 = e * seg + sh * sub
            g.dma(w1v, wts["L0_ffn_W1"][e].rearrange("(c p) f -> p c f", p=128),
                  writes=[warena_b])
            for st in range(nst):
                db = st % 2
                sy.dma(dtok[db][:], disp_ap[s0 + st * 128:s0 + (st + 1) * 128, :],
                       reads=[disp_b], writes=[dtok_b[db]])
                for dc in range(8):
                    pt = pb[dc % 2]
                    t.transpose(pt.ap[:, 0:128], dtok[db][:, dc * 128:(dc + 1) * 128],
                                idb_sb[:], reads=[dtok_b[db], idb_buf], writes=[pt.buf])
                    v.op(lambda bb, pt=pt, dc=dc, st=st: bb.tensor_copy(
                        dispfm[:, dc, st * 128:(st + 1) * 128], pt.ap[:, 0:128]),
                        reads=[pt.buf], writes=[dispfm_b])
            for fc in range(32):
                ptc = pf[fc % 2]
                pairs = [(w1v[:, dc, fc * 128:(fc + 1) * 128],
                          dispfm[:, dc, :]) for dc in range(8)]
                t.matmul_acc(ptc.ap[:, 0:sub], pairs,
                             reads=[warena_b, dispfm_b], writes=[ptc.buf])
                s.op(lambda bb, ptc=ptc, fc=fc: bb.activation(
                    harena[:, fc, :], ptc.ap[:, 0:sub], AF.Relu),
                    reads=[ptc.buf], writes=[harena_b])
            g.dma(w2v, wts["L0_ffn_W2"][e].rearrange("(c p) f -> p c f", p=128),
                  writes=[warena_b])
            for st in range(nst):
                for dh in range(2):
                    ptc = pf[2 + dh]
                    pairs = [(harena[:, fc, st * 128:(st + 1) * 128],
                              w2v[:, fc, dh * 512:(dh + 1) * 512]) for fc in range(32)]
                    t.matmul_acc(ptc.ap[:, 0:512], pairs,
                                 reads=[warena_b, harena_b], writes=[ptc.buf])
                    yb = dh
                    v.op(lambda bb, ptc=ptc, yb=yb: bb.tensor_copy(yev[yb][:],
                                                                   ptc.ap[:, 0:512]),
                         reads=[ptc.buf], writes=[yev_b[yb]])
                    sy.dma(eout[s0 + st * 128:s0 + (st + 1) * 128,
                                dh * 512:(dh + 1) * 512], yev[yb][:],
                           reads=[yev_b[yb]], writes=[eout_b])
    xnt = [P.sb([128, D], BF16) for _ in range(2)]
    xnt_b = [Buf() for _ in range(2)]
    geo = [P.sb([128, D], F32) for _ in range(2)]
    geo_b = [Buf() for _ in range(2)]
    out_t = [P.sb([128, D], F32) for _ in range(2)]
    out_t_b = [Buf() for _ in range(2)]
    for i in range(NT):
        b2 = i % 2
        sy.dma(xnt[b2][:], xn_tok[i * 128:(i + 1) * 128, :],
               reads=[xn_tok_b], writes=[xnt_b[b2]])
        v.op(lambda bb, b2=b2: bb.memset(geo[b2][:], 0.0), writes=[geo_b[b2]])
        g.indirect(geo[b2][:], None, eout, IndirectOffsetOnAxis(gidx_i[:, i:i + 1], 0),
                   reads=[gidx_i_b, eout_b], writes=[geo_b[b2]], bound=nslot - 1)
        v.op(lambda bb, b2=b2, i=i: bb.tensor_scalar(
            out_t[b2][:], geo[b2][:], factor[:, i:i + 1], None, ALU.mult),
            reads=[geo_b[b2], factor_b], writes=[out_t_b[b2]])
        v.op(lambda bb, b2=b2, i=i: bb.tensor_scalar(
            xnt[b2][:], xnt[b2][:], nkept[:, i:i + 1], None, ALU.mult),
            reads=[xnt_b[b2], nkept_b], writes=[xnt_b[b2]])
        v.op(lambda bb, b2=b2: bb.tensor_add(out_t[b2][:], out_t[b2][:], xnt[b2][:]),
             reads=[xnt_b[b2], out_t_b[b2]], writes=[out_t_b[b2]])
        sy.dma(ffn_out[i * 128:(i + 1) * 128, :], out_t[b2][:],
               reads=[out_t_b[b2]], writes=[ffn_out_b])


def emit_l0_attn(P, ctx, wts, xn_fm_sb, xn_fm_buf, attn_out, attn_out_b):
    v, s, t, sy, g = P.vector, P.scalar, P.tensor, P.sync, P.gpsimd
    NT = NTILE
    Tt = NT * 128
    nq = Tt // 512
    idf_sb, idf_buf = cref(ctx, "ident_f32")
    cos_sb, cos_buf = cref(ctx, "rope_cos")
    sin_sb, sin_buf = cref(ctx, "rope_sin")
    mask_sb, mask_buf = cref(ctx, "mask_diag")
    ones_sb, ones_buf = cref(ctx, "ones_row")
    epssb, epsbuf = cref(ctx, "eps_cols")
    pf = ctx["pf"]
    def _group(grp):
        with P.phase(f"l0attn{grp}"):
            qk = {}
            for nm in ("q", "k"):
                for hp in range(2):
                    qk[(nm, hp)] = (P.sb([128, Tt], BF16), Buf())
            vtok = P.sb([128, NT, 4 * 65], BF16)
            vtok_buf = Buf()
            att_ar = P.sb([128, 2, 512], BF16)
            att_ar_buf = Buf()
            for wkey, nm in (("L0_attn_wq", "q"), ("L0_attn_wk", "k")):
              with P.phase(f"l0{nm}p{grp}"):
                wT = load_weight_T(P, f"l0{nm}{grp}", wts[wkey], D, D)
                for hp in range(2):
                    oc0 = (grp * 4 + hp * 2) * 64
                    sb, bf = qk[(nm, hp)]
                    for tt in range(Tt // 512):
                        pt = pf[tt % 2]
                        pairs = [(wT.lhsT(dc, oc0, oc0 + 128),
                                  xn_fm_sb[:, dc, tt * 512:(tt + 1) * 512]) for dc in range(8)]
                        t.matmul_acc(pt.ap[:, 0:512], pairs,
                                     reads=[wT.buf, xn_fm_buf], writes=[pt.buf])
                        v.op(lambda bb, pt=pt, sb=sb, tt=tt: bb.tensor_copy(
                            sb[:, tt * 512:(tt + 1) * 512], pt.ap[:, 0:512]),
                            reads=[pt.buf], writes=[bf])
            vph = P.phase(f"l0vp{grp}"); vph.__enter__()
            wvT = load_weight_T(P, f"l0v{grp}", wts["L0_attn_wv"], D, D)
            for ti in range(NT):
                pt = pf[ti % 2]
                oc0 = grp * 256
                pairs = [(xn_fm_sb[:, dc, ti * 128:(ti + 1) * 128],
                          wvT.sb[:, dc, oc0:oc0 + 256]) for dc in range(8)]
                t.matmul_acc(pt.ap[:, 0:256], pairs,
                             reads=[wvT.buf, xn_fm_buf], writes=[pt.buf])
                v.op(lambda bb, pt=pt, ti=ti: bb.tensor_copy(
                    vtok[:, ti, 0:260].rearrange("p (h c) -> p h c", h=4)[:, :, 0:64],
                    pt.ap[:, 0:256].rearrange("p (h c) -> p h c", h=4)),
                    reads=[pt.buf], writes=[vtok_buf])
            v.op(lambda bb: bb.memset(
                vtok[:].rearrange("p n (h c) -> p n h c", h=4)[:, :, :, 64:65], 1.0),
                writes=[vtok_buf])
            vph.__exit__(None, None, None)
            rope_ph = P.phase(f"l0rope{grp}"); rope_ph.__enter__()
            rscr = P.sb([64, Tt], BF16); rscr_b = Buf()
            rope_t = [P.sb([32, Tt], BF16) for _ in range(3)]
            rope_tb = [Buf() for _ in range(3)]
            cosv = cos_sb[:, 0:Tt]; sinv = sin_sb[:, 0:Tt]
            x2s = P.sb([32, Tt], BF16); x2s_b = Buf()
            def rope_rows(eng, x1, bf):
                eng.op(lambda bb: bb.tensor_mul(rope_t[0][:], x1, cosv),
                       reads=[bf, cos_buf], writes=[rope_tb[0]])
                eng.op(lambda bb: bb.tensor_mul(rope_t[1][:], x1, sinv),
                       reads=[bf, sin_buf], writes=[rope_tb[1]])
                eng.op(lambda bb: bb.tensor_mul(rope_t[2][:], x2s[:], sinv),
                       reads=[x2s_b, sin_buf], writes=[rope_tb[2]])
                eng.op(lambda bb: bb.tensor_sub(x1, rope_t[0][:], rope_t[2][:]),
                       reads=[rope_tb[0], rope_tb[2]], writes=[bf])
                eng.op(lambda bb: bb.tensor_mul(rope_t[0][:], x2s[:], cosv),
                       reads=[x2s_b, cos_buf], writes=[rope_tb[0]])
                eng.op(lambda bb: bb.tensor_add(x2s[:], rope_t[1][:], rope_t[0][:]),
                       reads=[rope_tb[0], rope_tb[1]], writes=[x2s_b])
            for nm in ("q", "k"):
                for hp in range(2):
                    sb, bf = qk[(nm, hp)]
                    v.op(lambda bb, sb=sb: bb.tensor_copy(x2s[:], sb[32:64, :]),
                         reads=[bf], writes=[x2s_b])
                    rope_rows(v, sb[0:32, :], bf)
                    v.op(lambda bb, sb=sb: bb.tensor_copy(sb[32:64, :], x2s[:]),
                         reads=[x2s_b], writes=[bf])
                    g.op(lambda bb, sb=sb: bb.tensor_copy(rscr[:], sb[64:128, :]),
                         reads=[bf], writes=[rscr_b])
                    g.op(lambda bb: bb.tensor_copy(x2s[:], rscr[32:64, :]),
                         reads=[rscr_b], writes=[x2s_b])
                    rope_rows(g, rscr[0:32, :], rscr_b)
                    g.op(lambda bb: bb.tensor_copy(rscr[32:64, :], x2s[:]),
                         reads=[x2s_b], writes=[rscr_b])
                    g.op(lambda bb, sb=sb: bb.tensor_copy(sb[64:128, :], rscr[:]),
                         reads=[rscr_b], writes=[bf])
            rope_ph.__exit__(None, None, None)
            ptile = [P.sb([128, 512], BF16) for _ in range(2)]
            ptile_b = [Buf() for _ in range(2)]
            rrf = P.sb([1, 512], F32); rrf_b = Buf()
            rrow = P.sb([1, 512], BF16); rrow_b = Buf()
            recb = P.sb([64, 512], F32); recb_b = Buf()
            evq = [P.sb([128, 512], F32) for _ in range(2)]
            evq_b = [Buf() for _ in range(2)]
            prev = [P.sb([128, D], F32) for _ in range(2)]
            prev_b = [Buf() for _ in range(2)]
            atok = [P.sb([128, D], F32) for _ in range(4)]
            atok_b = [Buf() for _ in range(4)]
            woT = load_weight_T(P, f"l0o{grp}", wts["L0_attn_wo"], D, D)
            for tq in range(nq):
                for hh in range(4):
                    qsb, qbf = qk[("q", hh // 2)]
                    ksb, kbf = qk[("k", hh // 2)]
                    r0 = (hh % 2) * 64
                    patt = pf[2]
                    nk = tq * 4 + 4
                    for tkc in range(nk):
                        psc = pf[3 + (tkc % 2)]
                        t.matmul_acc(psc.ap[0:128, 0:512],
                                     [(ksb[r0:r0 + 64, tkc * 128:(tkc + 1) * 128],
                                       qsb[r0:r0 + 64, tq * 512:(tq + 1) * 512])],
                                     reads=[kbf, qbf], writes=[psc.buf])
                        pbi = tkc % 2
                        s.op(lambda bb, psc=psc, pbi=pbi: bb.activation(
                            ptile[pbi][:], psc.ap[0:128, 0:512], AF.Exp,
                            scale=0.125, bias=epssb[:, 2:3]),
                            reads=[psc.buf, epsbuf], writes=[ptile_b[pbi]])
                        dsub = tkc - tq * 4
                        if dsub >= 1:
                            v.op(lambda bb, pbi=pbi, dsub=dsub: bb.memset(
                                ptile[pbi][:, 0:dsub * 128], 0.0), writes=[ptile_b[pbi]])
                        if dsub >= 0:
                            v.op(lambda bb, pbi=pbi, dsub=dsub: bb.tensor_mul(
                                ptile[pbi][:, dsub * 128:(dsub + 1) * 128],
                                ptile[pbi][:, dsub * 128:(dsub + 1) * 128], mask_sb[:]),
                                reads=[mask_buf], writes=[ptile_b[pbi]])
                        t.matmul_acc(patt.ap[0:65, 0:512],
                                     [(vtok[:, tkc, hh * 65:hh * 65 + 65], ptile[pbi][:])],
                                     reads=[vtok_buf, ptile_b[pbi]], writes=[patt.buf],
                                     flags=[(tkc == 0, tkc == nk - 1)])
                    v.op(lambda bb, patt=patt: bb.reciprocal(rrf[:], patt.ap[64:65, 0:512]),
                         reads=[patt.buf], writes=[rrf_b])
                    v.op(lambda bb: bb.tensor_copy(rrow[:], rrf[:]),
                         reads=[rrf_b], writes=[rrow_b])
                    pn = pf[5]
                    t.matmul_acc(pn.ap[0:64, 0:512], [(ones_sb[:, 0:64], rrow[:])],
                                 reads=[rrow_b, ones_buf], writes=[pn.buf])
                    s.op(lambda bb, pn=pn: bb.copy(recb[:], pn.ap[0:64, 0:512]),
                         reads=[pn.buf], writes=[recb_b])
                    v.op(lambda bb, patt=patt, hh=hh: bb.tensor_tensor(
                        att_ar[(hh % 2) * 64:(hh % 2) * 64 + 64, hh // 2, :],
                        patt.ap[0:64, 0:512], recb[:], ALU.mult),
                        reads=[patt.buf, recb_b], writes=[att_ar_buf])
                for oc in range(8):
                    pt = pf[3]
                    pairs = [(woT.lhsT(grp * 2 + dc, oc * 128, (oc + 1) * 128),
                              att_ar[:, dc, :]) for dc in range(2)]
                    t.matmul_acc(pt.ap[:, 0:512], pairs,
                                 reads=[woT.buf, att_ar_buf], writes=[pt.buf])
                    eb = oc % 2
                    s.op(lambda bb, pt=pt, eb=eb: bb.copy(evq[eb][:], pt.ap[:, 0:512]),
                         reads=[pt.buf], writes=[evq_b[eb]])
                    for tsub in range(4):
                        pt2 = pf[4]
                        t.transpose(pt2.ap[:, 0:128], evq[eb][:, tsub * 128:(tsub + 1) * 128],
                                    idf_sb[:], reads=[evq_b[eb], idf_buf], writes=[pt2.buf])
                        v.op(lambda bb, pt2=pt2, oc=oc, tsub=tsub: bb.tensor_copy(
                            atok[tsub][:, oc * 128:(oc + 1) * 128], pt2.ap[:, 0:128]),
                            reads=[pt2.buf], writes=[atok_b[tsub]])
                        if oc == 7:
                            ti = tq * 4 + tsub
                            if grp == 0:
                                sy.dma(attn_out[ti * 128:(ti + 1) * 128, :], atok[tsub][:],
                                       reads=[atok_b[tsub]], writes=[attn_out_b])
                            else:
                                sy.dma(prev[tsub % 2][:], attn_out[ti * 128:(ti + 1) * 128, :],
                                       reads=[attn_out_b], writes=[prev_b[tsub % 2]])
                                v.op(lambda bb, tsub=tsub: bb.tensor_add(
                                    atok[tsub][:], atok[tsub][:], prev[tsub % 2][:]),
                                    reads=[prev_b[tsub % 2]], writes=[atok_b[tsub]])
                                sy.dma(attn_out[ti * 128:(ti + 1) * 128, :], atok[tsub][:],
                                       reads=[atok_b[tsub]], writes=[attn_out_b])
    for grp in range(4):
        _group(grp)


def emit_ln_pass(P, ctx, src, src_dtype, h_tok, h_fm_sb, h_fm_buf, gain=None,
                 beta=None, eps=1e-5):
    v, s, t, sy = P.vector, P.scalar, P.tensor, P.sync
    NT = NTILE
    idb_sb, idb_buf = cref(ctx, "ident_bf")
    src, src_b = src
    h_tok, h_tok_b = h_tok if h_tok is not None else (None, None)
    epssb, epsbuf = cref(ctx, "eps_cols")
    xt = [P.sb([128, D], src_dtype) for _ in range(2)]
    xtb = [Buf() for _ in range(2)]
    xnf = [P.sb([128, D], F32) for _ in range(2)]
    xnfb = [Buf() for _ in range(2)]
    xnb = [P.sb([128, D], BF16) for _ in range(2)]
    xnbb = [Buf() for _ in range(2)]
    sq = [P.sb([128, D], F32) for _ in range(2)]
    sqb = [Buf() for _ in range(2)]
    stat = [[P.sb([128, 1], F32) for _ in range(4)] for _ in range(2)]
    statb = [Buf() for _ in range(2)]
    for i in range(NT):
        b = i % 2
        r = slice(i * 128, (i + 1) * 128)
        sy.dma(xt[b][:], src[r, :], reads=[src_b], writes=[xtb[b]])
        v.op(lambda bb, b=b: bb.tensor_reduce(stat[b][0][:], xt[b][:], AX.X, ALU.add),
             reads=[xtb[b]], writes=[statb[b]])
        v.op(lambda bb, b=b: bb.tensor_scalar_mul(stat[b][0][:], stat[b][0][:], 1.0 / D),
             reads=[statb[b]], writes=[statb[b]])
        v.op(lambda bb, b=b: bb.tensor_scalar(xnf[b][:], xt[b][:], stat[b][0][:], None,
                                              ALU.subtract),
             reads=[xtb[b], statb[b]], writes=[xnfb[b]])
        s.op(lambda bb, b=b: bb.activation(sq[b][:], xnf[b][:], AF.Square),
             reads=[xnfb[b]], writes=[sqb[b]])
        v.op(lambda bb, b=b: bb.tensor_reduce(stat[b][1][:], sq[b][:], AX.X, ALU.add),
             reads=[sqb[b]], writes=[statb[b]])
        s.op(lambda bb, b=b: bb.activation(stat[b][2][:], stat[b][1][:], AF.Ln,
                                           scale=1.0 / D, bias=epssb[:, 0:1]),
             reads=[statb[b], epsbuf], writes=[statb[b]])
        s.op(lambda bb, b=b: bb.activation(stat[b][3][:], stat[b][2][:], AF.Exp,
                                           scale=-0.5),
             reads=[statb[b]], writes=[statb[b]])
        v.op(lambda bb, b=b: bb.tensor_scalar(xnf[b][:], xnf[b][:], stat[b][3][:], None,
                                              ALU.mult),
             reads=[statb[b]], writes=[xnfb[b]])
        if gain is not None:
            v.op(lambda bb, b=b: bb.tensor_mul(xnf[b][:], xnf[b][:], gain[0][:]),
                 reads=[gain[1], xnfb[b]], writes=[xnfb[b]])
            v.op(lambda bb, b=b: bb.tensor_add(xnf[b][:], xnf[b][:], beta[0][:]),
                 reads=[beta[1], xnfb[b]], writes=[xnfb[b]])
        v.op(lambda bb, b=b: bb.tensor_copy(xnb[b][:], xnf[b][:]),
             reads=[xnfb[b]], writes=[xnbb[b]])
        if h_tok is not None:
            sy.dma(h_tok[r, :], xnb[b][:], reads=[xnbb[b]], writes=[h_tok_b])
        if h_fm_sb is not None:
            for dc in range(8):
                pt = ctx["pb"][dc % 2]
                t.transpose(pt.ap[:, 0:128], xnb[b][:, dc * 128:(dc + 1) * 128],
                            idb_sb[:], reads=[xnbb[b], idb_buf], writes=[pt.buf])
                v.op(lambda bb, b=b, dc=dc, pt=pt, i=i: bb.tensor_copy(
                    h_fm_sb[:, dc, i * 128:(i + 1) * 128], pt.ap[:, 0:128]),
                    reads=[pt.buf], writes=[h_fm_buf])


def bias_cols(P, ctx, name, src_ap, n):
    sb = P.sb([128, n // 128], F32, name=name)
    b = Buf(name)
    P.sync.dma(sb[:], src_ap.rearrange("(c p) -> p c", p=128), writes=[b],
               allow_slow_non_contiguous=True)
    return sb, b


def repl_row(P, ctx, name, src_ap, n):
    row = P.sb([1, n], BF16, name=name + "_r")
    rowb = Buf()
    P.gpsimd.dma(row[:], src_ap.rearrange("(o n) -> o n", o=1), writes=[rowb])
    out = P.sb([128, n], F32, name=name)
    ob = Buf(name)
    ones_sb, ones_buf = cref(ctx, "ones_row")
    for c in range(n // 512):
        pt = ctx["pf"][5]
        P.tensor.matmul_acc(pt.ap[:, 0:512], [(ones_sb[:, 0:128], row[:, c * 512:(c + 1) * 512])],
                            reads=[rowb, ones_buf], writes=[pt.buf])
        P.vector.op(lambda bb, pt=pt, c=c: bb.tensor_copy(out[:, c * 512:(c + 1) * 512],
                                                          pt.ap[:, 0:512]),
                    reads=[pt.buf], writes=[ob])
    return out, ob


def emit_l1_attn(P, ctx, wts, xn_tok, attn_out, attn_out_b, stop_after=None):
    v, s, t, sy, g = P.vector, P.scalar, P.tensor, P.sync, P.gpsimd
    NT = NTILE
    Tt = NT * 128
    idf_sb, idf_buf = cref(ctx, "ident_f32")
    idb_sb, idb_buf = cref(ctx, "ident_bf")
    onesr_sb, onesr_buf = cref(ctx, "ones_row")
    epssb, epsbuf = cref(ctx, "eps_cols")
    mbk_sb, mbk_buf = cref(ctx, "mask_bucket")
    mbk0_sb, mbk0_buf = cref(ctx, "mask_bucket0")
    pf = ctx["pf"]; pbk = ctx["pb"]
    h_tok = P.dscratch("l1_h", [Tt, D], BF16); h_tok_b = Buf()
    hfm_hbm = P.dscratch("l1_hfm", [D, Tt], BF16); hfm_b = Buf()
    x1 = P.dscratch("l1_x1", [Tt, D], F32); x1_b = Buf()
    attfm_hbm = P.dscratch("l1_attfm", [D, Tt], BF16); attfm_b = Buf()
    gfm_hbm = P.dscratch("l1_gfm", [4 * D, Tt], BF16); gfm_b = Buf()
    with P.phase("l1ln1"):
        g1 = repl_row(P, ctx, "l1g1", wts["L1_attn_ln1_g"], D)
        b1 = repl_row(P, ctx, "l1b1", wts["L1_attn_ln1_b"], D)
        hfm_sb = P.sb([128, 8, Tt], BF16); hfm_sb_b = Buf()
        emit_ln_pass(P, ctx, xn_tok, BF16, (h_tok, h_tok_b), hfm_sb, hfm_sb_b,
                     gain=g1, beta=b1)
        sy.dma(hfm_hbm.rearrange("(c p) t -> p c t", p=128), hfm_sb[:],
               reads=[hfm_sb_b], writes=[hfm_b])
    for grp in range(4):
        _l1_attn_group(P, ctx, wts, grp, hfm_hbm, hfm_b, attfm_hbm, attfm_b)
    if stop_after == "groups":
        return {"attfm": attfm_hbm, "attfm_b": attfm_b, "h_tok": h_tok, "h_tok_b": h_tok_b}
    with P.phase("l1wout"):
        wout = WeightFM(P, "l1wout", D, D).load(P, wts["L1_attn_w_out"])
        bout_sb, bout_b = bias_cols(P, ctx, "l1bout", wts["L1_attn_b_out"], D)
        afm = P.sb([128, 8, 512], BF16, name="l1afm"); afm_b = Buf()
        evq = P.sb([128, 512], F32, name="l1evqw"); evq_b = Buf()
        xn_t = [P.sb([128, D], BF16) for _ in range(2)]
        xn_tb = [Buf() for _ in range(2)]
        ot = [P.sb([128, D], F32) for _ in range(4)]
        ot_b = [Buf() for _ in range(4)]
        for tt in range(Tt // 512):
            sy.dma(afm[:], attfm_hbm.rearrange("(c p) t -> p c t", p=128)[:, :, tt * 512:(tt + 1) * 512],
                   reads=[attfm_b], writes=[afm_b])
            for oc in range(8):
                pt = pf[oc % 2]
                pairs = [(wout.lhsT(dc, oc * 128, (oc + 1) * 128), afm[:, dc, :])
                         for dc in range(8)]
                t.matmul_acc(pt.ap[:, 0:512], pairs, reads=[wout.buf, afm_b],
                             writes=[pt.buf])
                v.op(lambda bb, pt=pt, oc=oc: bb.tensor_scalar(
                    evq[:], pt.ap[:, 0:512], bout_sb[:, oc:oc + 1], None, ALU.add),
                    reads=[pt.buf, bout_b], writes=[evq_b])
                for tsub in range(4):
                    pt2 = pf[4]
                    t.transpose(pt2.ap[:, 0:128], evq[:, tsub * 128:(tsub + 1) * 128],
                                idf_sb[:], reads=[evq_b, idf_buf], writes=[pt2.buf])
                    v.op(lambda bb, pt2=pt2, oc=oc, tsub=tsub: bb.tensor_copy(
                        ot[tsub][:, oc * 128:(oc + 1) * 128], pt2.ap[:, 0:128]),
                        reads=[pt2.buf], writes=[ot_b[tsub]])
                    if oc == 7:
                        ti = tt * 4 + tsub
                        b2 = tsub % 2
                        sy.dma(xn_t[b2][:], xn_tok[0][ti * 128:(ti + 1) * 128, :],
                               reads=[xn_tok[1]], writes=[xn_tb[b2]])
                        v.op(lambda bb, tsub=tsub, b2=b2: bb.tensor_add(
                            ot[tsub][:], ot[tsub][:], xn_t[b2][:]),
                            reads=[xn_tb[b2]], writes=[ot_b[tsub]])
                        sy.dma(x1[ti * 128:(ti + 1) * 128, :], ot[tsub][:],
                               reads=[ot_b[tsub]], writes=[x1_b])
    h2fm_hbm = P.dscratch("l1_h2fm", [D, Tt], BF16); h2fm_hb = Buf()
    with P.phase("l1ffn1"):
        with P.phase("l1ln2"):
            g2 = repl_row(P, ctx, "l1g2", wts["L1_attn_ln2_g"], D)
            b2r = repl_row(P, ctx, "l1b2r", wts["L1_attn_ln2_b"], D)
            h2fm_sb = P.sb([128, 8, Tt], BF16); h2fm_b = Buf()
            emit_ln_pass(P, ctx, (x1, x1_b), F32, None, h2fm_sb, h2fm_b,
                         gain=g2, beta=b2r)
            sy.dma(h2fm_hbm.rearrange("(c p) t -> p c t", p=128), h2fm_sb[:],
                   reads=[h2fm_b], writes=[h2fm_hb])
        w1 = WeightFM(P, "l1w1", D, 4 * D).load(P, wts["L1_attn_w1"])
        b1f_sb, b1f_b = bias_cols(P, ctx, "l1b1f", wts["L1_attn_b1"], 4 * D)
        h2ch = P.sb([128, 8, 512], BF16); h2ch_b = Buf()
        gout = P.sb([128, 512], BF16); gout_b = Buf()
        for tt in range(Tt // 512):
            sy.dma(h2ch[:], h2fm_hbm.rearrange("(c p) t -> p c t", p=128)
                   [:, :, tt * 512:(tt + 1) * 512], reads=[h2fm_hb], writes=[h2ch_b])
            for oc in range(32):
                pt = pf[oc % 2]
                pairs = [(w1.lhsT(dc, oc * 128, (oc + 1) * 128),
                          h2ch[:, dc, :]) for dc in range(8)]
                t.matmul_acc(pt.ap[:, 0:512], pairs, reads=[w1.buf, h2ch_b],
                             writes=[pt.buf])
                s.op(lambda bb, pt=pt, oc=oc: bb.activation(
                    gout[:], pt.ap[:, 0:512], AF.Gelu, bias=b1f_sb[:, oc:oc + 1]),
                    reads=[pt.buf, b1f_b], writes=[gout_b])
                sy.dma(gfm_hbm[oc * 128:(oc + 1) * 128, tt * 512:(tt + 1) * 512],
                       gout[:], reads=[gout_b], writes=[gfm_b])
    with P.phase("l1ffn2"):
        w2 = WeightFM(P, "l1w2", 4 * D, D).load(P, wts["L1_attn_w2"])
        b2f_sb, b2f_b = bias_cols(P, ctx, "l1b2f", wts["L1_attn_b2"], D)
        gch = P.sb([128, 32, 256], BF16); gch_b = Buf()
        evq2 = P.sb([128, 256], F32); evq2_b = Buf()
        x1t = [P.sb([128, D], F32) for _ in range(2)]
        x1tb = [Buf() for _ in range(2)]
        ot2 = [P.sb([128, D], F32) for _ in range(2)]
        ot2_b = [Buf() for _ in range(2)]
        for tt in range(Tt // 256):
            sy.dma(gch[:], gfm_hbm.rearrange("(c p) t -> p c t", p=128)[:, :, tt * 256:(tt + 1) * 256],
                   reads=[gfm_b], writes=[gch_b])
            for oc in range(8):
                pt = pf[oc % 2]
                pairs = [(w2.lhsT(fc, oc * 128, (oc + 1) * 128), gch[:, fc, :])
                         for fc in range(32)]
                t.matmul_acc(pt.ap[:, 0:256], pairs, reads=[w2.buf, gch_b],
                             writes=[pt.buf])
                v.op(lambda bb, pt=pt, oc=oc: bb.tensor_scalar(
                    evq2[:], pt.ap[:, 0:256], b2f_sb[:, oc:oc + 1], None, ALU.add),
                    reads=[pt.buf, b2f_b], writes=[evq2_b])
                for tsub in range(2):
                    pt2 = pf[4]
                    t.transpose(pt2.ap[:, 0:128], evq2[:, tsub * 128:(tsub + 1) * 128],
                                idf_sb[:], reads=[evq2_b, idf_buf], writes=[pt2.buf])
                    v.op(lambda bb, pt2=pt2, oc=oc, tsub=tsub: bb.tensor_copy(
                        ot2[tsub][:, oc * 128:(oc + 1) * 128], pt2.ap[:, 0:128]),
                        reads=[pt2.buf], writes=[ot2_b[tsub]])
                    if oc == 7:
                        ti = tt * 2 + tsub
                        b2 = tsub % 2
                        sy.dma(x1t[b2][:], x1[ti * 128:(ti + 1) * 128, :],
                               reads=[x1_b], writes=[x1tb[b2]])
                        v.op(lambda bb, tsub=tsub, b2=b2: bb.tensor_add(
                            ot2[tsub][:], ot2[tsub][:], x1t[b2][:]),
                            reads=[x1tb[b2]], writes=[ot2_b[tsub]])
                        sy.dma(attn_out[ti * 128:(ti + 1) * 128, :], ot2[tsub][:],
                               reads=[ot2_b[tsub]], writes=[attn_out_b])


def _l1_attn_group(P, ctx, wts, grp, hfm_hbm, hfm_b, attfm_hbm, attfm_b):
    v, s, t, sy, g = P.vector, P.scalar, P.tensor, P.sync, P.gpsimd
    NT = NTILE
    Tt = NT * 128
    NB = Tt // 64
    onesr_sb, onesr_buf = cref(ctx, "ones_row")
    mbk_sb, mbk_buf = cref(ctx, "mask_bucket")
    mbk0_sb, mbk0_buf = cref(ctx, "mask_bucket0")
    pf = ctx["pf"]
    with P.phase(f"l1att{grp}"):
        hch = P.sb([128, 8, 512], BF16); hch_b = Buf()
        wsl = P.sb([128, 8, 1024], BF16); wsl_b = Buf()
        bq_sb = P.sb([128, 24], F32); bq_b = Buf()
        bvrow = P.sb([1, 512], BF16); bvrow_b = Buf()
        qf = P.sb([128, 2, Tt], BF16); qf_b = Buf()
        kf = P.sb([128, 2, Tt], BF16); kf_b = Buf()
        vtok = P.sb([128, NT, 4 * 65], BF16); vtok_b = Buf()
        vtok_s = P.sb([128, NT, 4 * 65], BF16); vtok_s_b = Buf()
        ptile = P.sb([128, 512], BF16); ptile_b = Buf()
        rrf = P.sb([1, 512], F32); rrf_b = Buf()
        rrow = P.sb([1, 512], BF16); rrow_b = Buf()
        recb = P.sb([128, 512], F32); recb_b = Buf()
        aev = P.sb([128, 512], BF16); aev_b = Buf()
        qc0 = grp * 256
        vc0 = grp * 256
        g.dma(wsl[:, :, 0:256], wts["L1_attn_w_qkv"][:, qc0:qc0 + 256]
              .rearrange("(c p) o -> p c o", p=128), writes=[wsl_b])
        g.dma(wsl[:, :, 256:512], wts["L1_attn_w_qkv"][:, 1024 + qc0:1024 + qc0 + 256]
              .rearrange("(c p) o -> p c o", p=128), writes=[wsl_b])
        g.dma(wsl[:, :, 512:768], wts["L1_attn_w_qkv"][:, 2048 + vc0:2048 + vc0 + 256]
              .rearrange("(c p) o -> p c o", p=128), writes=[wsl_b])
        sy.dma(bq_sb[:], wts["L1_attn_b_qkv"].rearrange("(c p) -> p c", p=128),
               writes=[bq_b], allow_slow_non_contiguous=True)
        g.dma(bvrow[:, 0:256], wts["L1_attn_b_qkv"][2048 + vc0:2048 + vc0 + 256]
              .rearrange("(o n) -> o n", o=1), writes=[bvrow_b])
        for base, dst, dstb, bcbase in ((0, qf, qf_b, grp * 2), (256, kf, kf_b, 8 + grp * 2)):
            for op2 in range(2):
                for tt in range(Tt // 512):
                    pt = pf[tt % 2]
                    sy.dma(hch[:], hfm_hbm.rearrange("(c p) t -> p c t", p=128)
                           [:, :, tt * 512:(tt + 1) * 512], reads=[hfm_b], writes=[hch_b])
                    pairs = [(wsl[:, dc, base + op2 * 128:base + (op2 + 1) * 128],
                              hch[:, dc, :]) for dc in range(8)]
                    t.matmul_acc(pt.ap[:, 0:512], pairs,
                                 reads=[wsl_b, hch_b], writes=[pt.buf])
                    v.op(lambda bb, pt=pt, dst=dst, op2=op2, tt=tt, bc=bcbase + op2:
                         bb.tensor_scalar(dst[:, op2, tt * 512:(tt + 1) * 512],
                                          pt.ap[:, 0:512], bq_sb[:, bc:bc + 1], None,
                                          ALU.add),
                         reads=[pt.buf, bq_b], writes=[dstb])
        for tt in range(Tt // 512):
            sy.dma(hch[:], hfm_hbm.rearrange("(c p) t -> p c t", p=128)
                   [:, :, tt * 512:(tt + 1) * 512], reads=[hfm_b], writes=[hch_b])
            for ts4 in range(4):
                ti = tt * 4 + ts4
                pt = pf[ts4 % 2]
                pairs = [(hch[:, dc, ts4 * 128:(ts4 + 1) * 128],
                          wsl[:, dc, 512:768]) for dc in range(8)]
                flags = [(i == 0, False) for i in range(8)] + [(False, True)]
                t.matmul_acc([pt.ap[:, 0:256]] * 8 + [pt.ap[:, 0:256]],
                             pairs + [(onesr_sb[0:1, :], bvrow[:, 0:256])],
                             reads=[hch_b, wsl_b, bvrow_b, onesr_buf],
                             writes=[pt.buf], flags=flags)
                v.op(lambda bb, pt=pt, ti=ti: bb.tensor_copy(
                    vtok[:, ti, 0:260].rearrange("p (h c) -> p h c", h=4)[:, :, 0:64],
                    pt.ap[:, 0:256].rearrange("p (h c) -> p h c", h=4)),
                    reads=[pt.buf], writes=[vtok_b])
        v.op(lambda bb: bb.memset(
            vtok[:].rearrange("p n (h c) -> p n h c", h=4)[:, :, :, 64:65], 1.0),
            writes=[vtok_b])
        for tj in range(1, NT):
            sy.dma(vtok_s[0:64, tj, :], vtok[64:128, tj - 1, :],
                   reads=[vtok_b], writes=[vtok_s_b])
            sy.dma(vtok_s[64:128, tj, :], vtok[0:64, tj, :],
                   reads=[vtok_b], writes=[vtok_s_b])
        for hh in range(4):
            for bg in range(NB // 8):
                psc = pf[2]
                patt = pf[3]
                for j in range(8):
                    b = bg * 8 + j
                    if b == 0:
                        kl = kf[(hh % 2) * 64:(hh % 2) * 64 + 64, hh // 2, 0:128]
                    else:
                        kl = kf[(hh % 2) * 64:(hh % 2) * 64 + 64, hh // 2,
                                (b - 1) * 64:(b + 1) * 64]
                    ql = qf[(hh % 2) * 64:(hh % 2) * 64 + 64, hh // 2,
                            b * 64:(b + 1) * 64]
                    t.matmul_acc(psc.ap[0:128, j * 64:(j + 1) * 64], [(kl, ql)],
                                 reads=[kf_b, qf_b], writes=[psc.buf],
                                 flags=[(True, True)])
                s.op(lambda bb, psc=psc: bb.activation(ptile[:], psc.ap[0:128, 0:512],
                                                       AF.Exp, scale=0.125),
                     reads=[psc.buf], writes=[ptile_b])
                if bg == 0:
                    v.op(lambda bb: bb.tensor_mul(ptile[:, 0:64], ptile[:, 0:64],
                                                  mbk0_sb[:]),
                         reads=[mbk0_buf], writes=[ptile_b])
                    v.op(lambda bb: bb.tensor_tensor(
                        ptile[:, 64:512].rearrange("p (g q) -> p g q", q=64),
                        ptile[:, 64:512].rearrange("p (g q) -> p g q", q=64),
                        mbk_sb[:].rearrange("p (g q) -> p g q", g=1)
                        .broadcast_to([128, 7, 64]), ALU.mult),
                        reads=[mbk_buf], writes=[ptile_b])
                else:
                    v.op(lambda bb: bb.tensor_tensor(
                        ptile[:].rearrange("p (g q) -> p g q", q=64),
                        ptile[:].rearrange("p (g q) -> p g q", q=64),
                        mbk_sb[:].rearrange("p (g q) -> p g q", g=1)
                        .broadcast_to([128, 8, 64]), ALU.mult),
                        reads=[mbk_buf], writes=[ptile_b])
                for j in range(8):
                    b = bg * 8 + j
                    pcol = ptile[:, j * 64:(j + 1) * 64]
                    if b == 0:
                        t.matmul_acc(patt.ap[0:65, j * 64:(j + 1) * 64],
                                     [(vtok[:, 0, hh * 65:(hh + 1) * 65], pcol)],
                                     reads=[vtok_b, ptile_b], writes=[patt.buf],
                                     flags=[(True, True)])
                    elif b % 2 == 1:
                        ti0 = (b - 1) // 2
                        t.matmul_acc(patt.ap[0:65, j * 64:(j + 1) * 64],
                                     [(vtok[:, ti0, hh * 65:(hh + 1) * 65], pcol)],
                                     reads=[vtok_b, ptile_b], writes=[patt.buf],
                                     flags=[(True, True)])
                    else:
                        t.matmul_acc(patt.ap[0:65, j * 64:(j + 1) * 64],
                                     [(vtok_s[:, b // 2, hh * 65:(hh + 1) * 65], pcol)],
                                     reads=[vtok_s_b, ptile_b], writes=[patt.buf],
                                     flags=[(True, True)])
                v.op(lambda bb, patt=patt: bb.tensor_copy(rrf[:], patt.ap[64:65, 0:512]),
                     reads=[patt.buf], writes=[rrf_b])
                if bg == 0:
                    v.op(lambda bb: bb.tensor_scalar_add(rrf[:, 0:64], rrf[:, 0:64], 64.0),
                         reads=[rrf_b], writes=[rrf_b])
                v.op(lambda bb: bb.reciprocal(rrf[:], rrf[:]),
                     reads=[rrf_b], writes=[rrf_b])
                v.op(lambda bb: bb.tensor_copy(rrow[:], rrf[:]),
                     reads=[rrf_b], writes=[rrow_b])
                pn = pf[5]
                t.matmul_acc(pn.ap[0:64, 0:512], [(onesr_sb[:, 0:64], rrow[:])],
                             reads=[rrow_b, onesr_buf], writes=[pn.buf])
                s.op(lambda bb, pn=pn: bb.copy(recb[0:64, :], pn.ap[0:64, 0:512]),
                     reads=[pn.buf], writes=[recb_b])
                v.op(lambda bb, patt=patt: bb.tensor_tensor(
                    aev[0:64, :], patt.ap[0:64, 0:512], recb[0:64, :], ALU.mult),
                    reads=[patt.buf, recb_b], writes=[aev_b])
                sy.dma(attfm_hbm[grp * 256 + hh * 64:grp * 256 + (hh + 1) * 64,
                                 bg * 512:(bg + 1) * 512], aev[0:64, :],
                       reads=[aev_b], writes=[attfm_b])


def emit_l1_moe(P, ctx, wts, xn_fm_sb, xn_fm_buf, xn_tok, ffn_out, seg=1280,
                rlog=None, stage="all", disp_io=None, disp_in=None,
                eout_io=None, part_io=None):
    v, s, t, sy, g = P.vector, P.scalar, P.tensor, P.sync, P.gpsimd
    NT = NTILE
    Tt = NT * 128
    E = 8
    xn_tok, xn_tok_b = xn_tok
    ffn_out, ffn_out_b = ffn_out if ffn_out is not None else (None, None)
    idb_sb, idb_buf = cref(ctx, "ident_bf")
    tri_sb, tri_buf = cref(ctx, "tri_strict")
    onesr_sb, onesr_buf = cref(ctx, "ones_row")
    onesc_sb, onesc_buf = cref(ctx, "ones_col")
    iota8_sb, iota8_buf = cref(ctx, "iota8")
    iotat_sb, iotat_buf = cref(ctx, "iota_tok")
    pf = ctx["pf"]; pbk = ctx["pb"]
    nslot = E * seg
    nst = seg // 128
    if eout_io is not None:
        eout, eout_b = eout_io
    else:
        eout = P.dscratch("l1_eout", [nslot, D], F32); eout_b = Buf()
    with P.phase("l1moe"):
        if rlog is None:
            wg_sb = P.sb([128, 8, 8], BF16); wg_b = Buf()
            g.dma(wg_sb[:], wts["L1_ffn_Wg"].rearrange("(c p) e -> p c e", p=128),
                  writes=[wg_b])
        else:
            rlog_sb, rlog_buf = rlog
        probs = P.sb([128, 8], F32); probs_b = Buf()
        mcol = P.sb([128, 1], F32); mcol_b = Buf()
        m2col = P.sb([128, 1], F32); m2col_b = Buf()
        scol = P.sb([128, 1], F32); scol_b = Buf()
        eq1 = P.sb([128, 8], F32); eq1_b = Buf()
        eq2 = P.sb([128, 8], F32); eq2_b = Buf()
        a2 = P.sb([128, 8], F32); a2_b = Buf()
        w1m = P.sb([128, NT], F32); w1m_b = Buf()
        w2m = P.sb([128, NT], F32); w2m_b = Buf()
        idx1 = P.sb([128, NT], F32); idx1_b = Buf()
        idx2 = P.sb([128, NT], F32); idx2_b = Buf()
        inds = P.sb([128, E * NT], BF16); inds_b = Buf()
        ind1f = P.sb([128, E * NT], F32); ind1f_b = Buf()
        ind2f = P.sb([128, E * NT], F32); ind2f_b = Buf()
        for i in range(NT):
            if rlog is None:
                pt = pf[5]
                pairs = [(xn_fm_sb[:, dc, i * 128:(i + 1) * 128], wg_sb[:, dc, :])
                         for dc in range(8)]
                t.matmul_acc(pt.ap[0:128, 0:8], pairs, reads=[xn_fm_buf, wg_b],
                             writes=[pt.buf])
                s.op(lambda bb, pt=pt: bb.copy(probs[:], pt.ap[0:128, 0:8]),
                     reads=[pt.buf], writes=[probs_b])
            else:
                v.op(lambda bb, i=i: bb.tensor_copy(probs[:], rlog_sb[:, i, :]),
                     reads=[rlog_buf], writes=[probs_b])
            v.op(lambda bb: bb.reduce_max(mcol[:], probs[:], AX.X),
                 reads=[probs_b], writes=[mcol_b])
            v.op(lambda bb: bb.tensor_scalar(probs[:], probs[:], mcol[:], None,
                                             ALU.subtract),
                 reads=[probs_b, mcol_b], writes=[probs_b])
            s.op(lambda bb: bb.activation(probs[:], probs[:], AF.Exp),
                 reads=[probs_b], writes=[probs_b])
            v.op(lambda bb: bb.tensor_reduce(scol[:], probs[:], AX.X, ALU.add),
                 reads=[probs_b], writes=[scol_b])
            v.op(lambda bb: bb.reciprocal(scol[:], scol[:]),
                 reads=[scol_b], writes=[scol_b])
            v.op(lambda bb: bb.tensor_scalar(probs[:], probs[:], scol[:], None, ALU.mult),
                 reads=[probs_b, scol_b], writes=[probs_b])
            v.op(lambda bb: bb.reduce_max(mcol[:], probs[:], AX.X),
                 reads=[probs_b], writes=[mcol_b])
            v.op(lambda bb: bb.tensor_scalar(eq1[:], probs[:], mcol[:], None, ALU.is_ge),
                 reads=[probs_b, mcol_b], writes=[eq1_b])
            v.op(lambda bb: bb.scalar_tensor_tensor(a2[:], eq1[:], -2.0, probs[:],
                                                    ALU.mult, ALU.add),
                 reads=[eq1_b, probs_b], writes=[a2_b])
            v.op(lambda bb: bb.reduce_max(m2col[:], a2[:], AX.X),
                 reads=[a2_b], writes=[m2col_b])
            v.op(lambda bb: bb.tensor_scalar(eq2[:], a2[:], m2col[:], None, ALU.is_ge),
                 reads=[a2_b, m2col_b], writes=[eq2_b])
            v.op(lambda bb: bb.tensor_add(scol[:], mcol[:], m2col[:]),
                 reads=[mcol_b, m2col_b], writes=[scol_b])
            v.op(lambda bb: bb.reciprocal(scol[:], scol[:]),
                 reads=[scol_b], writes=[scol_b])
            v.op(lambda bb, i=i: bb.tensor_tensor(w1m[:, i:i + 1], mcol[:], scol[:],
                                                  ALU.mult),
                 reads=[mcol_b, scol_b], writes=[w1m_b])
            v.op(lambda bb, i=i: bb.tensor_tensor(w2m[:, i:i + 1], m2col[:], scol[:],
                                                  ALU.mult),
                 reads=[m2col_b, scol_b], writes=[w2m_b])
            v.op(lambda bb: bb.tensor_mul(a2[:], eq1[:], iota8_sb[:]),
                 reads=[eq1_b, iota8_buf], writes=[a2_b])
            v.op(lambda bb, i=i: bb.tensor_reduce(idx1[:, i:i + 1], a2[:], AX.X, ALU.add),
                 reads=[a2_b], writes=[idx1_b])
            v.op(lambda bb: bb.tensor_mul(a2[:], eq2[:], iota8_sb[:]),
                 reads=[eq2_b, iota8_buf], writes=[a2_b])
            v.op(lambda bb, i=i: bb.tensor_reduce(idx2[:, i:i + 1], a2[:], AX.X, ALU.add),
                 reads=[a2_b], writes=[idx2_b])
            iv = inds[:].rearrange("p (e j) -> p e j", e=E)
            v.op(lambda bb, i=i: bb.tensor_add(a2[:], eq1[:], eq2[:]),
                 reads=[eq1_b, eq2_b], writes=[a2_b])
            v.op(lambda bb, i=i, iv=iv: bb.tensor_copy(iv[:, :, i], a2[:]),
                 reads=[a2_b], writes=[inds_b])
            v.op(lambda bb, i=i: bb.tensor_copy(
                ind1f[:].rearrange("p (e j) -> p e j", e=E)[:, :, i], eq1[:]),
                reads=[eq1_b], writes=[ind1f_b])
            v.op(lambda bb, i=i: bb.tensor_copy(
                ind2f[:].rearrange("p (e j) -> p e j", e=E)[:, :, i], eq2[:]),
                reads=[eq2_b], writes=[ind2f_b])
        colsum_p = pf[0]
        t.matmul_acc(colsum_p.ap[0:1, 0:E * NT], [(onesc_sb[:], inds[:])],
                     reads=[inds_b, onesc_buf], writes=[colsum_p.buf])
        colrow = P.sb([1, E * NT], F32); colrow_b = Buf()
        s.op(lambda bb: bb.copy(colrow[:], colsum_p.ap[0:1, 0:E * NT]),
             reads=[colsum_p.buf], writes=[colrow_b])
        excl = P.sb([1, E, NT + 1], F32); excl_b = Buf()
        v.op(lambda bb: bb.memset(excl[:, :, 0:1], 0.0), writes=[excl_b])
        zrow = P.sb([1, NT], F32); zrow_b = Buf()
        v.op(lambda bb: bb.memset(zrow[:], 0.0), writes=[zrow_b])
        for e in range(E):
            v.op(lambda bb, e=e: bb.tensor_tensor_scan(
                excl[:, e, 1:NT + 1], colrow[:, e * NT:(e + 1) * NT], zrow[:],
                0.0, ALU.add, ALU.add),
                reads=[colrow_b, zrow_b], writes=[excl_b])
        exclb = P.sb([1, E * NT], BF16); exclb_b = Buf()
        v.op(lambda bb: bb.tensor_copy(exclb[:].rearrange("p (e j) -> p e j", e=E),
                                       excl[:, :, 0:NT]),
             reads=[excl_b], writes=[exclb_b])
        exhf = P.sb([1, E * NT], F32); exhf_b = Buf()
        v.op(lambda bb: bb.tensor_copy(exhf[:], exclb[:]),
             reads=[exclb_b], writes=[exhf_b])
        exlo = P.sb([1, E * NT], F32); exlo_b = Buf()
        v.op(lambda bb: bb.tensor_sub(exlo[:].rearrange("p (e j) -> p e j", e=E),
                                      excl[:, :, 0:NT],
                                      exhf[:].rearrange("p (e j) -> p e j", e=E)),
             reads=[excl_b, exhf_b], writes=[exlo_b])
        exclo = P.sb([1, E * NT], BF16); exclo_b = Buf()
        v.op(lambda bb: bb.tensor_copy(exclo[:], exlo[:]),
             reads=[exlo_b], writes=[exclo_b])
        Cp = pf[1]
        t.matmul_acc(Cp.ap[:, 0:E * NT],
                     [(tri_sb[:], inds[:]), (onesr_sb[:, 0:128], exclb[:]),
                      (onesr_sb[:, 0:128], exclo[:])],
                     reads=[inds_b, tri_buf, exclb_b, exclo_b, onesr_buf],
                     writes=[Cp.buf],
                     flags=[(True, False), (False, False), (False, True)])
        tmp = P.sb([128, E * NT], F32); tmp_b = Buf()
        slot1 = P.sb([128, NT], F32); slot1_b = Buf()
        slot2 = P.sb([128, NT], F32); slot2_b = Buf()
        ovf = P.sb([128, NT], F32); ovf_b = Buf()
        s1i = P.sb([128, NT], I32); s1i_b = Buf()
        s2i = P.sb([128, NT], I32); s2i_b = Buf()
        tokid_i = P.sb([128, NT], I32); tokid_i_b = Buf()
        for (indf, indf_b, idxm, idxm_b, slot, slot_b, si, si_b) in (
                (ind1f, ind1f_b, idx1, idx1_b, slot1, slot1_b, s1i, s1i_b),
                (ind2f, ind2f_b, idx2, idx2_b, slot2, slot2_b, s2i, s2i_b)):
            v.op(lambda bb, indf=indf: bb.tensor_tensor(tmp[:], Cp.ap[:, 0:E * NT],
                                                        indf[:], ALU.mult),
                 reads=[Cp.buf, indf_b], writes=[tmp_b])
            v.op(lambda bb, slot=slot: bb.tensor_reduce(
                slot[:], tmp[:].rearrange("p (e j) -> p j e", e=E), AX.X, ALU.add),
                reads=[tmp_b], writes=[slot_b])
            v.op(lambda bb, slot=slot: bb.tensor_single_scalar(ovf[:], slot[:],
                                                               float(seg), ALU.is_ge),
                 reads=[slot_b], writes=[ovf_b])
            v.op(lambda bb, slot=slot, idxm=idxm: bb.scalar_tensor_tensor(
                slot[:], idxm[:], float(seg), slot[:], ALU.mult, ALU.add),
                reads=[idxm_b, slot_b], writes=[slot_b])
            v.op(lambda bb, slot=slot: bb.scalar_tensor_tensor(
                slot[:], ovf[:], 1e9, slot[:], ALU.mult, ALU.add),
                reads=[ovf_b, slot_b], writes=[slot_b])
            v.op(lambda bb, si=si, slot=slot: bb.tensor_copy(si[:], slot[:]),
                 reads=[slot_b], writes=[si_b])
        v.op(lambda bb: bb.tensor_copy(tokid_i[:], iotat_sb[:, 0:NT]),
             reads=[iotat_buf], writes=[tokid_i_b])
        if stage in ("all", "disp1", "disp2exp"):
            if stage == "disp2exp" and disp_in is not None:
                disp_ap, disp_b = disp_in
            else:
                disp_ap, disp_b = disp_io
            si_use = s1i if stage in ("all", "disp1") else s2i
            si_use_b = s1i_b if stage in ("all", "disp1") else s2i_b
            xtile = [P.sb([128, D], BF16) for _ in range(2)]
            xtb = [Buf() for _ in range(2)]
            for j in range(NT):
                bq = j % 2
                sy.dma(xtile[bq][:], xn_tok[j * 128:(j + 1) * 128, :],
                       reads=[xn_tok_b], writes=[xtb[bq]])
                g.indirect(disp_ap, IndirectOffsetOnAxis(si_use[:, j:j + 1], 0),
                           xtile[bq][:], None, reads=[si_use_b, xtb[bq]],
                           writes=[disp_b], bound=nslot - 1)
                if stage == "all":
                    g.indirect(disp_ap, IndirectOffsetOnAxis(s2i[:, j:j + 1], 0),
                               xtile[bq][:], None, reads=[s2i_b, xtb[bq]],
                               writes=[disp_b], bound=nslot - 1)
        if stage == "disp1":
            return
        if stage in ("comb1", "comb2"):
            _l1_combine_stage(P, ctx, stage, NT, nslot, eout, eout_b,
                              s1i, s1i_b, s2i, s2i_b, w1m, w1m_b, w2m, w2m_b,
                              ffn_out, ffn_out_b, part_io)
            return
        with P.phase("l1exp"):
            warena = P.sb([128, 32768], BF16, name="l1warena")
            w1v = warena[:].rearrange("p (c f) -> p c f", c=8)
            w2v = warena[:].rearrange("p (c f) -> p c f", c=32)
            warena_b = Buf()
            sub = min(384, seg)
            nsub = seg // sub
            nst2 = sub // 128
            harena = P.sb([128, 32, sub], BF16, name="l1harena"); harena_b = Buf()
            dispfm = P.sb([128, 8, sub], BF16, name="l1disp"); dispfm_b = Buf()
            b1sb = P.sb([128, 8, 32], F32); b1sb_b = Buf()
            sy.dma(b1sb[:], wts["L1_ffn_b1"].rearrange("e (c p) -> p e c", p=128),
                   writes=[b1sb_b], allow_slow_non_contiguous=True)
            b2row = P.sb([1, 1024], BF16); b2row_b = Buf()
            smcol = P.sb([128, 1], I32); smcol_b = Buf()
            dtok = [P.sb([128, D], BF16) for _ in range(1)]
            dtok_b = [Buf() for _ in range(1)]
            yev = [P.sb([128, 512], F32) for _ in range(1)]
            yev_b = [Buf() for _ in range(1)]
            for e in range(E):
              for sh in range(nsub):
                s0 = e * seg + sh * sub
                if sh == 0:
                    g.dma(b2row[:], wts["L1_ffn_b2"][e:e + 1, :],
                          reads=[b2row_b], writes=[b2row_b])
                g.dma(w1v, wts["L1_ffn_W1"][e].rearrange("(c p) f -> p c f", p=128),
                      writes=[warena_b])
                for st in range(nst2):
                    db = 0
                    sy.dma(dtok[db][:], disp_ap[s0 + st * 128:s0 + (st + 1) * 128, :],
                           reads=[disp_b], writes=[dtok_b[db]])
                    for dc in range(8):
                        pt = pbk[dc % 2]
                        t.transpose(pt.ap[:, 0:128], dtok[db][:, dc * 128:(dc + 1) * 128],
                                    idb_sb[:], reads=[dtok_b[db], idb_buf], writes=[pt.buf])
                        v.op(lambda bb, pt=pt, dc=dc, st=st: bb.tensor_copy(
                            dispfm[:, dc, st * 128:(st + 1) * 128], pt.ap[:, 0:128]),
                            reads=[pt.buf], writes=[dispfm_b])
                for fc in range(32):
                    ptc = pf[fc % 2]
                    pairs = [(w1v[:, dc, fc * 128:(fc + 1) * 128],
                              dispfm[:, dc, :]) for dc in range(8)]
                    t.matmul_acc(ptc.ap[:, 0:sub], pairs,
                                 reads=[warena_b, dispfm_b], writes=[ptc.buf])
                    s.op(lambda bb, ptc=ptc, fc=fc, e=e: bb.activation(
                        harena[:, fc, :], ptc.ap[:, 0:sub], AF.Gelu,
                        bias=b1sb[:, e, fc:fc + 1]),
                        reads=[ptc.buf, b1sb_b], writes=[harena_b])
                g.dma(w2v, wts["L1_ffn_W2"][e].rearrange("(c p) d -> p c d", p=128),
                      writes=[warena_b])
                for st in range(nst2):
                    for dh in range(2):
                        ptc = pf[2 + dh]
                        pairs = [(harena[:, fc, st * 128:(st + 1) * 128],
                                  w2v[:, fc, dh * 512:(dh + 1) * 512]) for fc in range(32)]
                        pairs.append((onesr_sb[0:1, :], b2row[:, dh * 512:(dh + 1) * 512]))
                        flags = [(i2 == 0, False) for i2 in range(32)] + [(False, True)]
                        t.matmul_acc(ptc.ap[:, 0:512], pairs,
                                     reads=[warena_b, harena_b, b2row_b, onesr_buf],
                                     writes=[ptc.buf], flags=flags)
                        yb = 0
                        v.op(lambda bb, ptc=ptc, yb=yb: bb.tensor_copy(yev[yb][:],
                                                                       ptc.ap[:, 0:512]),
                             reads=[ptc.buf], writes=[yev_b[yb]])
                        sy.dma(eout[s0 + st * 128:s0 + (st + 1) * 128,
                                    dh * 512:(dh + 1) * 512], yev[yb][:],
                               reads=[yev_b[yb]], writes=[eout_b])
        if stage == "disp2exp":
            return
        geo = [P.sb([128, D], F32) for _ in range(2)]
        geo_b = [Buf() for _ in range(2)]
        out_t = [P.sb([128, D], F32) for _ in range(2)]
        out_t_b = [Buf() for _ in range(2)]
        for i in range(NT):
            b2 = i % 2
            v.op(lambda bb, b2=b2: bb.memset(geo[b2][:], 0.0), writes=[geo_b[b2]])
            g.indirect(geo[b2][:], None, eout, IndirectOffsetOnAxis(s1i[:, i:i + 1], 0),
                       reads=[s1i_b, eout_b], writes=[geo_b[b2]], bound=nslot - 1)
            v.op(lambda bb, b2=b2, i=i: bb.tensor_scalar(
                out_t[b2][:], geo[b2][:], w1m[:, i:i + 1], None, ALU.mult),
                reads=[geo_b[b2], w1m_b], writes=[out_t_b[b2]])
            v.op(lambda bb, b2=b2: bb.memset(geo[b2][:], 0.0), writes=[geo_b[b2]])
            g.indirect(geo[b2][:], None, eout, IndirectOffsetOnAxis(s2i[:, i:i + 1], 0),
                       reads=[s2i_b, eout_b], writes=[geo_b[b2]], bound=nslot - 1)
            v.op(lambda bb, b2=b2, i=i: bb.scalar_tensor_tensor(
                out_t[b2][:], geo[b2][:], w2m[:, i:i + 1], out_t[b2][:],
                ALU.mult, ALU.add),
                reads=[geo_b[b2], w2m_b, out_t_b[b2]], writes=[out_t_b[b2]])
            sy.dma(ffn_out[i * 128:(i + 1) * 128, :], out_t[b2][:],
                   reads=[out_t_b[b2]], writes=[ffn_out_b])


def _l1_combine_stage(P, ctx, stage, NT, nslot, eout, eout_b, s1i, s1i_b,
                      s2i, s2i_b, w1m, w1m_b, w2m, w2m_b, ffn_out, ffn_out_b,
                      part_io):
    v, sy, g = P.vector, P.sync, P.gpsimd
    part_ap, part_b = part_io
    si, si_b = (s1i, s1i_b) if stage == "comb1" else (s2i, s2i_b)
    wm, wm_b = (w1m, w1m_b) if stage == "comb1" else (w2m, w2m_b)
    geo = [P.sb([128, D], F32) for _ in range(2)]
    geo_b = [Buf() for _ in range(2)]
    out_t = [P.sb([128, D], F32) for _ in range(2)]
    out_t_b = [Buf() for _ in range(2)]
    pt_t = [P.sb([128, D], F32) for _ in range(2)]
    pt_b = [Buf() for _ in range(2)]
    for i in range(NT):
        b2 = i % 2
        v.op(lambda bb, b2=b2: bb.memset(geo[b2][:], 0.0), writes=[geo_b[b2]])
        g.indirect(geo[b2][:], None, eout, IndirectOffsetOnAxis(si[:, i:i + 1], 0),
                   reads=[si_b, eout_b], writes=[geo_b[b2]], bound=nslot - 1)
        v.op(lambda bb, b2=b2, i=i: bb.tensor_scalar(
            out_t[b2][:], geo[b2][:], wm[:, i:i + 1], None, ALU.mult),
            reads=[geo_b[b2], wm_b], writes=[out_t_b[b2]])
        if stage == "comb2":
            sy.dma(pt_t[b2][:], part_ap[i * 128:(i + 1) * 128, :],
                   reads=[part_b], writes=[pt_b[b2]])
            v.op(lambda bb, b2=b2: bb.tensor_add(out_t[b2][:], out_t[b2][:],
                                                 pt_t[b2][:]),
                 reads=[pt_b[b2], out_t_b[b2]], writes=[out_t_b[b2]])
            sy.dma(ffn_out[i * 128:(i + 1) * 128, :], out_t[b2][:],
                   reads=[out_t_b[b2]], writes=[ffn_out_b])
        else:
            sy.dma(part_ap[i * 128:(i + 1) * 128, :], out_t[b2][:],
                   reads=[out_t_b[b2]], writes=[part_b])


def emit_rlog_fm(P, ctx, w_sb, w_buf, xn_fm_sb, xn_fm_buf, rlog_sb, rlog_buf):
    NT = NTILE
    pf = ctx["pf"]
    with P.phase("rlogfm"):
        for i in range(NT):
            pt = pf[i % 2]
            pairs = [(xn_fm_sb[:, dc, i * 128:(i + 1) * 128], w_sb[:, dc, :])
                     for dc in range(8)]
            P.tensor.matmul_acc(pt.ap[0:128, 0:8], pairs,
                                reads=[xn_fm_buf, w_buf], writes=[pt.buf])
            P.vector.op(lambda bb, pt=pt, i=i: bb.tensor_copy(
                rlog_sb[:, i, :], pt.ap[0:128, 0:8]),
                reads=[pt.buf], writes=[rlog_buf])


def emit_l2log_fm(P, ctx, ctl_sb, ctl_buf, xn_fm_sb, xn_fm_buf,
                  l2log_hbm, l2log_buf):
    NT = NTILE
    pf = ctx["pf"]
    with P.phase("l2logfm"):
        stg = P.sb([64, 128], F32); stg_b = Buf()
        for i in range(NT):
            pt = pf[i % 2]
            pairs = [(ctl_sb[:, dc, :], xn_fm_sb[:, dc, i * 128:(i + 1) * 128])
                     for dc in range(8)]
            P.tensor.matmul_acc(pt.ap[0:64, 0:128], pairs,
                                reads=[xn_fm_buf, ctl_buf], writes=[pt.buf])
            P.vector.op(lambda bb, pt=pt: bb.tensor_copy(stg[:], pt.ap[0:64, 0:128]),
                        reads=[pt.buf], writes=[stg_b])
            P.sync.dma(l2log_hbm[:, i * 128:(i + 1) * 128], stg[:],
                       reads=[stg_b], writes=[l2log_buf])


def emit_l2_hook(P, ctx, ctl_sb, ctl_buf, l2log_hbm, l2log_buf, stg, stg_b):
    def hook(P, ctx, i, dc, fchunk, fcb):
        pt = ctx["pf"][4]
        P.tensor.matmul_acc(pt.ap[0:64, 0:128], [(ctl_sb[:, dc, :], fchunk[:])],
                            reads=[fcb, ctl_buf], writes=[pt.buf])
        if dc == 0:
            P.vector.op(lambda bb, pt=pt: bb.tensor_copy(stg[:], pt.ap[0:64, 0:128]),
                        reads=[pt.buf], writes=[stg_b])
        else:
            P.vector.op(lambda bb, pt=pt: bb.tensor_add(
                stg[:], stg[:], pt.ap[0:64, 0:128]),
                reads=[pt.buf, stg_b], writes=[stg_b])
        if dc == 7:
            P.sync.dma(l2log_hbm[:, i * 128:(i + 1) * 128], stg[:],
                       reads=[stg_b], writes=[l2log_buf])
    return hook


def emit_l2_attn(P, ctx, wts, xn_fm_sb, xn_fm_buf, attn_out, attn_out_b):
    import math
    v, s, t, sy, g = P.vector, P.scalar, P.tensor, P.sync, P.gpsimd
    NT = NTILE
    Tt = NT * 128
    gam = (1.0 - np.exp(np.linspace(math.log(1 / 32), math.log(1 / 512), 8)))
    idf_sb, idf_buf = cref(ctx, "ident_f32")
    idb_sb, idb_buf = cref(ctx, "ident_bf")
    gp_sb, gp_buf = cref(ctx, "gamma_pow")
    gn_sb, gn_buf = cref(ctx, "gamma_npow")
    mask_sb, mask_buf = cref(ctx, "mask_diag")
    onesr_sb, onesr_buf = cref(ctx, "ones_row")
    onesc_sb, onesc_buf = cref(ctx, "ones_col")
    epssb, epsbuf = cref(ctx, "eps_cols")
    pf = ctx["pf"]; pbk = ctx["pb"]
    with P.phase("l2attn"):
        gnw_sb, gnw_b = bias_cols(P, ctx, "l2gnw", wts["L2_attn_gn_weight"], D)
        gnb_sb, gnb_b = bias_cols(P, ctx, "l2gnb", wts["L2_attn_gn_bias"], D)
        ysb = []
        for h in range(8):
            ysb.append((P.sb([128, NT, 128], BF16, name=f"l2y{h}"), Buf()))
        for h in range(8):
            _l2_head(P, ctx, wts, h, float(gam[h]), xn_fm_sb, xn_fm_buf, ysb[h],
                     gnw_sb, gnw_b, gnb_sb, gnb_b)
        with P.phase("l2wo"):
            woT = load_weight_T(P, "l2wo", wts["L2_attn_Wo"], D, D)
            evq = P.sb([128, 512], F32); evq_b = Buf()
            ot = [P.sb([128, D], F32) for _ in range(4)]
            ot_b = [Buf() for _ in range(4)]
            for tt in range(Tt // 512):
                for oc in range(8):
                    pt = pf[oc % 2]
                    pairs = [(woT.lhsT(hc, oc * 128, (oc + 1) * 128),
                              ysb[hc][0][:].rearrange("p n c -> p (n c)")
                              [:, tt * 512:(tt + 1) * 512]) for hc in range(8)]
                    t.matmul_acc(pt.ap[:, 0:512], pairs,
                                 reads=[woT.buf] + [ysb[hc][1] for hc in range(8)],
                                 writes=[pt.buf])
                    s.op(lambda bb, pt=pt: bb.copy(evq[:], pt.ap[:, 0:512]),
                         reads=[pt.buf], writes=[evq_b])
                    for tsub in range(4):
                        pt2 = pf[4]
                        t.transpose(pt2.ap[:, 0:128], evq[:, tsub * 128:(tsub + 1) * 128],
                                    idf_sb[:], reads=[evq_b, idf_buf], writes=[pt2.buf])
                        v.op(lambda bb, pt2=pt2, oc=oc, tsub=tsub: bb.tensor_copy(
                            ot[tsub][:, oc * 128:(oc + 1) * 128], pt2.ap[:, 0:128]),
                            reads=[pt2.buf], writes=[ot_b[tsub]])
                        if oc == 7:
                            ti = tt * 4 + tsub
                            sy.dma(attn_out[ti * 128:(ti + 1) * 128, :], ot[tsub][:],
                                   reads=[ot_b[tsub]], writes=[attn_out_b])


def _l2_head(P, ctx, wts, h, gamh, xn_fm_sb, xn_fm_buf, ysbh, gnw_sb, gnw_b,
             gnb_sb, gnb_b):
    v, s, t, sy, g = P.vector, P.scalar, P.tensor, P.sync, P.gpsimd
    NT = NTILE
    Tt = NT * 128
    idb_sb, idb_buf = cref(ctx, "ident_bf")
    gp_sb, gp_buf = cref(ctx, "gamma_pow")
    gn_sb, gn_buf = cref(ctx, "gamma_npow")
    mask_sb, mask_buf = cref(ctx, "mask_diag")
    onesr_sb, onesr_buf = cref(ctx, "ones_row")
    onesc_sb, onesc_buf = cref(ctx, "ones_col")
    epssb, epsbuf = cref(ctx, "eps_cols")
    pf = ctx["pf"]; pbk = ctx["pb"]
    ysb_t, ysb_b = ysbh
    with P.phase(f"l2h{h}"):
        qtok = P.sb([128, NT, 128], BF16); qtok_b = Buf()
        ktok = P.sb([128, NT, 128], BF16); ktok_b = Buf()
        vtok = P.sb([128, NT, 128], BF16); vtok_b = Buf()
        qfm = P.sb([128, Tt], BF16); qfm_b = Buf()
        kfm = P.sb([128, Tt], BF16); kfm_b = Buf()
        k2s = P.sb([128, 128], BF16); k2s_b = Buf()
        ptl = P.sb([128, 128], BF16); ptl_b = Buf()
        Sf = P.sb([128, 128], F32); Sf_b = Buf()
        Sbf = P.sb([128, 128], BF16); Sbf_b = Buf()
        sq = P.sb([128, 512], BF16); sq_b = Buf()
        stat = [P.sb([1, 1], F32) for _ in range(4)]
        stat_b = Buf()
        mcol = P.sb([128, 1], F32); mcol_b = Buf()
        rcol = P.sb([128, 1], F32); rcol_b = Buf()
        for wkey, dst, dstb, scale_ap, scale_buf in (
                ("L2_attn_Wq", qtok, qtok_b, gp_sb, gp_buf),
                ("L2_attn_Wk", ktok, ktok_b, gn_sb, gn_buf),
                ("L2_attn_Wv", vtok, vtok_b, None, None)):
            wT = load_weight_T_cols(P, f"l2{wkey[-2:]}{h}", wts[wkey], D, D,
                                    h * 128, (h + 1) * 128)
            for ti in range(NT):
                pt = pf[ti % 2]
                pairs = [(xn_fm_sb[:, dc, ti * 128:(ti + 1) * 128],
                          wT.sb[:, dc, 0:128]) for dc in range(8)]
                t.matmul_acc(pt.ap[:, 0:128], pairs, reads=[wT.buf, xn_fm_buf],
                             writes=[pt.buf])
                if scale_ap is not None:
                    v.op(lambda bb, pt=pt, dst=dst, ti=ti, scale_ap=scale_ap:
                         bb.tensor_scalar(dst[:, ti, :], pt.ap[:, 0:128],
                                          scale_ap[:, h:h + 1], None, ALU.mult),
                         reads=[pt.buf, scale_buf], writes=[dstb])
                else:
                    v.op(lambda bb, pt=pt, dst=dst, ti=ti: bb.tensor_copy(
                        dst[:, ti, :], pt.ap[:, 0:128]),
                        reads=[pt.buf], writes=[dstb])
        for ti in range(NT):
            for src_t, src_b2, dst, dstb in ((qtok, qtok_b, qfm, qfm_b),
                                             (ktok, ktok_b, kfm, kfm_b)):
                pt = pbk[ti % 2]
                t.transpose(pt.ap[:, 0:128], src_t[:, ti, :], idb_sb[:],
                            reads=[src_b2, idb_buf], writes=[pt.buf])
                v.op(lambda bb, pt=pt, dst=dst, ti=ti: bb.tensor_copy(
                    dst[:, ti * 128:(ti + 1) * 128], pt.ap[:, 0:128]),
                    reads=[pt.buf], writes=[dstb])
        v.op(lambda bb: bb.memset(Sf[:], 0.0), writes=[Sf_b])
        v.op(lambda bb: bb.memset(Sbf[:], 0.0), writes=[Sbf_b])
        g128 = float(gamh ** 128)
        g1 = float(gamh)
        gLm1 = float(gamh ** 127)
        for c in range(NT):
            pA = pf[2]
            t.matmul_acc(pA.ap[0:128, 0:128],
                         [(kfm[:, c * 128:(c + 1) * 128], qfm[:, c * 128:(c + 1) * 128])],
                         reads=[kfm_b, qfm_b], writes=[pA.buf])
            v.op(lambda bb, pA=pA: bb.tensor_tensor(ptl[:], pA.ap[0:128, 0:128],
                                                    mask_sb[:], ALU.mult),
                 reads=[pA.buf, mask_buf], writes=[ptl_b])
            py = pf[3]
            t.matmul_acc(py.ap[0:128, 0:128],
                         [(vtok[:, c, :], ptl[:]), (Sbf[:], qfm[:, c * 128:(c + 1) * 128])],
                         reads=[vtok_b, ptl_b, Sbf_b, qfm_b], writes=[py.buf],
                         flags=[(True, False), (False, True)])
            v.op(lambda bb, py=py, c=c: bb.tensor_copy(ysb_t[:, c, :], py.ap[0:128, 0:128]),
                 reads=[py.buf], writes=[ysb_b])
            v.op(lambda bb, c=c: bb.tensor_scalar_mul(k2s[:], ktok[:, c, :], gLm1),
                 reads=[ktok_b], writes=[k2s_b])
            pS = pf[5]
            t.matmul_acc(pS.ap[0:128, 0:128], [(k2s[:], vtok[:, c, :])],
                         reads=[k2s_b, vtok_b], writes=[pS.buf])
            v.op(lambda bb: bb.tensor_scalar_mul(Sf[:], Sf[:], g128),
                 reads=[Sf_b], writes=[Sf_b])
            v.op(lambda bb, pS=pS: bb.tensor_add(Sf[:], Sf[:], pS.ap[0:128, 0:128]),
                 reads=[pS.buf, Sf_b], writes=[Sf_b])
            v.op(lambda bb: bb.tensor_scalar_mul(Sbf[:], Sf[:], g1),
                 reads=[Sf_b], writes=[Sbf_b])
        yview = ysb_t[:].rearrange("p n c -> p (n c)")
        pst = pf[2]
        for cc in range(Tt // 512):
            t.matmul_acc(pst.ap[0:1, 0:512], [(onesc_sb[:], yview[:, cc * 512:(cc + 1) * 512])],
                         reads=[ysb_b, onesc_buf], writes=[pst.buf],
                         flags=[(cc == 0, cc == Tt // 512 - 1)])
        srow = P.sb([1, 512], F32); srow_b = Buf()
        s.op(lambda bb: bb.copy(srow[:], pst.ap[0:1, 0:512]),
             reads=[pst.buf], writes=[srow_b])
        v.op(lambda bb: bb.tensor_reduce(stat[0][:], srow[:], AX.X, ALU.add),
             reads=[srow_b], writes=[stat_b])
        psq = pf[3]
        for cc in range(Tt // 512):
            s.op(lambda bb, cc=cc: bb.activation(sq[:], yview[:, cc * 512:(cc + 1) * 512],
                                                 AF.Square),
                 reads=[ysb_b], writes=[sq_b])
            t.matmul_acc(psq.ap[0:1, 0:512], [(onesc_sb[:], sq[:])],
                         reads=[sq_b, onesc_buf], writes=[psq.buf],
                         flags=[(cc == 0, cc == Tt // 512 - 1)])
        s.op(lambda bb: bb.copy(srow[:], psq.ap[0:1, 0:512]),
             reads=[psq.buf], writes=[srow_b])
        v.op(lambda bb: bb.tensor_reduce(stat[1][:], srow[:], AX.X, ALU.add),
             reads=[srow_b], writes=[stat_b])
        n = float(128 * Tt)
        v.op(lambda bb: bb.tensor_scalar_mul(stat[0][:], stat[0][:], 1.0 / n),
             reads=[stat_b], writes=[stat_b])
        v.op(lambda bb: bb.tensor_scalar(stat[2][:], stat[1][:], 1.0 / n, None, ALU.mult),
             reads=[stat_b], writes=[stat_b])
        v.op(lambda bb: bb.tensor_tensor(stat[3][:], stat[0][:], stat[0][:], ALU.mult),
             reads=[stat_b], writes=[stat_b])
        v.op(lambda bb: bb.tensor_sub(stat[2][:], stat[2][:], stat[3][:]),
             reads=[stat_b], writes=[stat_b])
        s.op(lambda bb: bb.activation(stat[2][:], stat[2][:], AF.Ln,
                                      bias=epssb[0:1, 0:1]),
             reads=[stat_b, epsbuf], writes=[stat_b])
        s.op(lambda bb: bb.activation(stat[2][:], stat[2][:], AF.Exp, scale=-0.5),
             reads=[stat_b], writes=[stat_b])
        for st_i, col, colb in ((0, mcol, mcol_b), (2, rcol, rcol_b)):
            pc = pf[4]
            srow2 = P.sb([1, 1], BF16)
            srow2_b = Buf()
            v.op(lambda bb, st_i=st_i, srow2=srow2: bb.tensor_copy(srow2[:], stat[st_i][:]),
                 reads=[stat_b], writes=[srow2_b])
            t.matmul_acc(pc.ap[0:128, 0:1], [(onesr_sb[:, 0:128], srow2[:])],
                         reads=[srow2_b, onesr_buf], writes=[pc.buf])
            s.op(lambda bb, pc=pc, col=col: bb.copy(col[:], pc.ap[0:128, 0:1]),
                 reads=[pc.buf], writes=[colb])
        for cc in range(Tt // 512):
            yv = yview[:, cc * 512:(cc + 1) * 512]
            v.op(lambda bb, yv=yv: bb.tensor_scalar(yv, yv, mcol[:], None, ALU.subtract),
                 reads=[mcol_b], writes=[ysb_b])
            v.op(lambda bb, yv=yv: bb.tensor_scalar(yv, yv, rcol[:], None, ALU.mult),
                 reads=[rcol_b], writes=[ysb_b])
            v.op(lambda bb, yv=yv, h=h: bb.tensor_scalar(yv, yv, gnw_sb[:, h:h + 1],
                                                         None, ALU.mult),
                 reads=[gnw_b], writes=[ysb_b])
            v.op(lambda bb, yv=yv, h=h: bb.tensor_scalar(yv, yv, gnb_sb[:, h:h + 1],
                                                         None, ALU.add),
                 reads=[gnb_b], writes=[ysb_b])


def emit_l2_ffn(P, ctx, wts, xn_tok, ffn_out, l2log_hbm, l2log_hbm_b):
    v, s, t, sy, g = P.vector, P.scalar, P.tensor, P.sync, P.gpsimd
    NT = NTILE
    Tt = NT * 128
    NG = Tt // 16
    GB = 32
    NBAT = NG // GB
    xn_tok, xn_tok_b = xn_tok
    ffn_out, ffn_out_b = ffn_out
    idb_sb, idb_buf = cref(ctx, "ident_bf")
    tie_sb, tie_buf = cref(ctx, "tie64")
    bd_sb, bd_buf = cref(ctx, "blockdiag16")
    pf = ctx["pf"]; pbk = ctx["pb"]
    inter_hbm = P.dscratch("l2_inter", [NG * 64, D], BF16); inter_b = Buf()
    with P.phase("l2ffn"):
        perm = P.sb([64, Tt], BF16); perm_b = Buf()
        with P.phase("l2perm"):
            l2log_sb = P.sb([64, Tt], F32, name="l2log_loc"); l2log_buf = Buf()
            sy.dma(l2log_sb[:], l2log_hbm, reads=[l2log_hbm_b], writes=[l2log_buf])
            mx = P.sb([64, NG], F32); mx_b = Buf()
            lv = l2log_sb[:].rearrange("p (g k) -> p g k", k=16)
            v.op(lambda bb: bb.tensor_tensor(
                lv, lv, tie_sb[:].rearrange("p (o k) -> p o k", o=1)
                .broadcast_to([64, NG, 16]), ALU.add),
                reads=[tie_buf, l2log_buf], writes=[l2log_buf])
            v.op(lambda bb: bb.tensor_reduce(mx[:], lv, AX.X, ALU.max),
                 reads=[l2log_buf], writes=[mx_b])
            v.op(lambda bb: bb.tensor_tensor(
                perm[:].rearrange("p (g k) -> p g k", k=16), lv,
                mx[:].rearrange("p (g o) -> p g o", o=1).broadcast_to([64, NG, 16]),
                ALU.is_ge),
                reads=[l2log_buf, mx_b], writes=[perm_b])
        bat_ph = P.phase("l2bat"); bat_ph.__enter__()
        f1sb = P.sb([128, 8, 4096], BF16, name="l2f1"); f1_b = Buf()
        g.dma(f1sb[:], wts["L2_ffn_f1"].rearrange("(c p) e s f -> p c (e s f)", p=128),
              writes=[f1_b])
        bsb = P.sb([64, 64], F32); bsb_b = Buf()
        sy.dma(bsb[:], wts["L2_ffn_bias"].rearrange("e s f -> f (e s)"),
               writes=[bsb_b], allow_slow_non_contiguous=True)
        f2sl = P.sb([64, 1024], BF16); f2sl_b = Buf()
        selb = P.sb([128, 8, 2048], BF16, name="l2selb"); selb_b = Buf()
        inner = P.sb([64, 2048], BF16, name="l2inner"); inner_b = Buf()
        xt = P.sb([128, D], BF16); xt_b = Buf()
        phat = [P.sb([128, 512], BF16) for _ in range(2)]
        phat_b = [Buf() for _ in range(2)]
        permT = P.sb([128, 64], BF16); permT_b = Buf()
        iev = P.sb([32, 1024], F32, name="l2iev"); iev_b = Buf()
        for bat in range(NBAT):
            for tc in range(4):
                ti = bat * 4 + tc
                db = tc % 2
                pt = pbk[0]
                t.transpose(pt.ap[:, 0:64], perm[:, ti * 128:(ti + 1) * 128],
                            idb_sb[0:64, 0:64],
                            reads=[perm_b, idb_buf], writes=[pt.buf])
                v.op(lambda bb, pt=pt: bb.tensor_copy(permT[:], pt.ap[0:128, 0:64]),
                     reads=[pt.buf], writes=[permT_b])
                v.op(lambda bb, db=db: bb.tensor_tensor(
                    phat[db][:].rearrange("p (g e) -> p g e", g=8),
                    permT[:].rearrange("p (o e) -> p o e", o=1).broadcast_to([128, 8, 64]),
                    bd_sb[:].rearrange("p (g o) -> p g o", o=1).broadcast_to([128, 8, 64]),
                    ALU.mult),
                    reads=[permT_b, bd_buf], writes=[phat_b[db]])
                sy.dma(xt[:], xn_tok[ti * 128:(ti + 1) * 128, :],
                       reads=[xn_tok_b], writes=[xt_b])
                for dc in range(8):
                    pt2 = pf[dc % 2]
                    t.matmul_acc(pt2.ap[:, 0:512],
                                 [(xt[:, dc * 128:(dc + 1) * 128], phat[db][:])],
                                 reads=[xt_b, phat_b[db]], writes=[pt2.buf])
                    v.op(lambda bb, pt2=pt2, dc=dc, tc=tc: bb.tensor_copy(
                        selb[:, dc, :].rearrange("p (e c) -> p e c", e=64)
                        [:, :, tc * 8:tc * 8 + 8],
                        pt2.ap[:, 0:512].rearrange("p (g e) -> p e g", g=8)),
                        reads=[pt2.buf], writes=[selb_b])
            for es in range(64):
                pt3 = pf[2 + es % 2]
                pairs = [(f1sb[:, dc, es * 64:(es + 1) * 64],
                          selb[:, dc, es * 32:(es + 1) * 32]) for dc in range(8)]
                t.matmul_acc(pt3.ap[0:64, 0:32], pairs, reads=[f1_b, selb_b],
                             writes=[pt3.buf])
                s.op(lambda bb, pt3=pt3, es=es: bb.activation(
                    inner[:, es * 32:(es + 1) * 32], pt3.ap[0:64, 0:32], AF.Relu,
                    bias=bsb[:, es:es + 1]),
                    reads=[pt3.buf, bsb_b], writes=[inner_b])
            for es in range(64):
                g.dma(f2sl[:], wts["L2_ffn_f2"].rearrange("e s f d -> (e s) f d")[es],
                      writes=[f2sl_b])
                pt4a, pt4b = pf[4], pf[5]
                t.matmul_acc(pt4a.ap[0:32, 0:512],
                             [(inner[:, es * 32:(es + 1) * 32], f2sl[:, 0:512])],
                             reads=[inner_b, f2sl_b], writes=[pt4a.buf])
                t.matmul_acc(pt4b.ap[0:32, 0:512],
                             [(inner[:, es * 32:(es + 1) * 32], f2sl[:, 512:1024])],
                             reads=[inner_b, f2sl_b], writes=[pt4b.buf])
                s.op(lambda bb, pt4a=pt4a: bb.copy(iev[:, 0:512], pt4a.ap[0:32, 0:512]),
                     reads=[pt4a.buf], writes=[iev_b])
                s.op(lambda bb, pt4b=pt4b: bb.copy(iev[:, 512:1024], pt4b.ap[0:32, 0:512]),
                     reads=[pt4b.buf], writes=[iev_b])
                g.dma(inter_hbm.rearrange("(r s) d -> r s d", s=64)
                      [bat * 32:(bat + 1) * 32, es, :], iev[:],
                      reads=[iev_b], writes=[inter_b])
        bat_ph.__exit__(None, None, None)
        shat = P.sb([128, 128], BF16); shat_b = Buf()
        irows = P.sb([128, 4, D], BF16, name="l2irows"); irows_b = Buf()
        oev = P.sb([128, D], F32); oev_b = Buf()
        for ti in range(NT):
            pt = pbk[0]
            t.transpose(pt.ap[:, 0:64], perm[:, ti * 128:(ti + 1) * 128],
                        idb_sb[0:64, 0:64],
                        reads=[perm_b, idb_buf], writes=[pt.buf])
            v.op(lambda bb, pt=pt: bb.tensor_copy(permT[:], pt.ap[0:128, 0:64]),
                 reads=[pt.buf], writes=[permT_b])
            v.op(lambda bb: bb.tensor_tensor(
                phat[0][:].rearrange("p (g e) -> p g e", g=8),
                permT[:].rearrange("p (o e) -> p o e", o=1).broadcast_to([128, 8, 64]),
                bd_sb[:].rearrange("p (g o) -> p g o", o=1).broadcast_to([128, 8, 64]),
                ALU.mult),
                reads=[permT_b, bd_buf], writes=[phat_b[0]])
            sy.dma(irows[:], inter_hbm[ti * 512:(ti + 1) * 512, :]
                   .rearrange("(c p) d -> p c d", p=128), reads=[inter_b],
                   writes=[irows_b])
            for dh in range(2):
                pt5 = pf[dh]
                pairs = []
                for kc in range(4):
                    pt6 = pbk[1]
                    t.transpose(pt6.ap[:, 0:128], phat[0][:, kc * 128:(kc + 1) * 128],
                                idb_sb[:], reads=[phat_b[0], idb_buf], writes=[pt6.buf])
                    v.op(lambda bb, pt6=pt6: bb.tensor_copy(shat[:], pt6.ap[0:128, 0:128]),
                         reads=[pt6.buf], writes=[shat_b])
                    t.matmul_acc(pt5.ap[:, 0:512],
                                 [(shat[:], irows[:, kc, dh * 512:(dh + 1) * 512])],
                                 reads=[shat_b, irows_b], writes=[pt5.buf],
                                 flags=[(kc == 0, kc == 3)])
                v.op(lambda bb, pt5=pt5, dh=dh: bb.tensor_copy(
                    oev[:, dh * 512:(dh + 1) * 512], pt5.ap[:, 0:512]),
                    reads=[pt5.buf], writes=[oev_b])
            sy.dma(ffn_out[ti * 128:(ti + 1) * 128, :], oev[:],
                   reads=[oev_b], writes=[ffn_out_b])


def emit_l3_attn(P, ctx, wts, xn_fm_sb, xn_fm_buf, xn_tok, attn_out, attn_out_b):
    v, s, t, sy, g = P.vector, P.scalar, P.tensor, P.sync, P.gpsimd
    NT = NTILE
    Tt = NT * 128
    NC = Tt // 32
    idf_sb, idf_buf = cref(ctx, "ident_f32")
    idb_sb, idb_buf = cref(ctx, "ident_bf")
    onesr_sb, onesr_buf = cref(ctx, "ones_row")
    onesc_sb, onesc_buf = cref(ctx, "ones_col")
    epssb, epsbuf = cref(ctx, "eps_cols")
    m32_sb, m32_buf = cref(ctx, "mask_strict32")
    hsel_sb, hsel_buf = cref(ctx, "headsel")
    pf = ctx["pf"]; pbk = ctx["pb"]
    xn_tok_ap, xn_tok_b = xn_tok
    dd_hbm = P.dscratch("l3_dd", [6, D, Tt], BF16); dd_b = Buf()
    kvrwu = P.dscratch("l3_proj", [5, D, Tt], BF16); kvrwu_b = Buf()
    rwkv_hbm = P.dscratch("l3_rwkv", [D, Tt], BF16); rwkv_b = Buf()
    gate_hbm = P.dscratch("l3_gate", [D, Tt], BF16); gate_b = Buf()
    a_hbm = P.dscratch("l3_abuf", [64, 32], BF16); a_hbm_b = Buf()
    zfm_hbm = P.dscratch("l3_zfm", [D, Tt], BF16); zfm_b = Buf()
    dfm_hbm = P.dscratch("l3_dfm", [D, Tt], BF16); dfm_b = Buf()
    with P.phase("l3shift"):
        mur = repl_row(P, ctx, "l3mu", wts["L3_attn_mu_x"], D)
        xt = P.sb([128, D], BF16); xtb = Buf()
        lt = P.sb([128, D], BF16); ltb = Buf()
        df = P.sb([128, D], F32); df_b = Buf()
        mt = P.sb([128, D], F32); mt_b = Buf()
        zt = P.sb([128, D], BF16); zt_b = Buf()
        dfb = P.sb([128, D], BF16); dfb_b = Buf()
        stg = P.sb([128, 128], BF16); stg_b = Buf()
        for i in range(NT):
            sy.dma(xt[:], xn_tok_ap[i * 128:(i + 1) * 128, :],
                   reads=[xn_tok_b], writes=[xtb])
            if i == 0:
                v.op(lambda bb: bb.memset(lt[0:64, :], 0.0), writes=[ltb])
                sy.dma(lt[1:128, :], xn_tok_ap[0:127, :],
                       reads=[xn_tok_b], writes=[ltb])
            else:
                sy.dma(lt[:], xn_tok_ap[i * 128 - 1:(i + 1) * 128 - 1, :],
                       reads=[xn_tok_b], writes=[ltb])
            v.op(lambda bb: bb.tensor_sub(df[:], lt[:], xt[:]),
                 reads=[xtb, ltb], writes=[df_b])
            v.op(lambda bb: bb.tensor_mul(mt[:], df[:], mur[0][:]),
                 reads=[df_b, mur[1]], writes=[mt_b])
            v.op(lambda bb: bb.tensor_add(mt[:], mt[:], xt[:]),
                 reads=[mt_b, xtb], writes=[mt_b])
            v.op(lambda bb: bb.tensor_mul(mt[:], mt[:], df[:]),
                 reads=[mt_b, df_b], writes=[mt_b])
            v.op(lambda bb: bb.tensor_add(zt[:], mt[:], xt[:]),
                 reads=[mt_b, xtb], writes=[zt_b])
            v.op(lambda bb: bb.tensor_copy(dfb[:], df[:]),
                 reads=[df_b], writes=[dfb_b])
            for src_t, src_b2, dsthbm, dsthbm_b in ((zt, zt_b, zfm_hbm, zfm_b),
                                                    (dfb, dfb_b, dfm_hbm, dfm_b)):
                for dc in range(8):
                    pt = pbk[dc % 2]
                    t.transpose(pt.ap[:, 0:128], src_t[:, dc * 128:(dc + 1) * 128],
                                idb_sb[:], reads=[src_b2, idb_buf], writes=[pt.buf])
                    v.op(lambda bb, pt=pt: bb.tensor_copy(stg[:], pt.ap[:, 0:128]),
                         reads=[pt.buf], writes=[stg_b])
                    sy.dma(dsthbm[dc * 128:(dc + 1) * 128, i * 128:(i + 1) * 128],
                           stg[:], reads=[stg_b], writes=[dsthbm_b])
    with P.phase("l3lora"):
        aw_sb = P.sb([128, 8, 6, 32], BF16); aw_b = Buf()
        for gi in range(6):
            g.dma(aw_sb[:, :, gi, :], wts["L3_attn_lora_A_w"][gi]
                  .rearrange("(c p) r -> p c r", p=128), writes=[aw_b])
        bw_sb = P.sb([32, 6, D], BF16); bw_b = Buf()
        g.dma(bw_sb[:], wts["L3_attn_lora_B_w"].rearrange("g r e -> r g e"),
              writes=[bw_b], allow_slow_non_contiguous=True)
        ab_sb = P.sb([32, 6], F32); ab_b = Buf()
        sy.dma(ab_sb[:], wts["L3_attn_lora_A_b"].rearrange("g r -> r g"),
               writes=[ab_b], allow_slow_non_contiguous=True)
        lam_sb = P.sb([128, 6, 8], F32); lam_b = Buf()
        sy.dma(lam_sb[:], wts["L3_attn_lora_lambda"].rearrange("g (c p) -> p g c", p=128),
               writes=[lam_b], allow_slow_non_contiguous=True)
        bb_sb = P.sb([128, 6, 8], F32); bbias_b = Buf()
        sy.dma(bb_sb[:], wts["L3_attn_lora_B_b"].rearrange("g (c p) -> p g c", p=128),
               writes=[bbias_b], allow_slow_non_contiguous=True)
        zch = P.sb([128, 8, 512], BF16); zch_b = Buf()
        dch = P.sb([128, 8, 512], BF16); dch_b = Buf()
        hdn = P.sb([32, 512], BF16); hdn_b = Buf()
        ddt = P.sb([128, 512], F32); ddt_b = Buf()
        for ts in range(Tt // 512):
            sy.dma(zch[:], zfm_hbm.rearrange("(c p) t -> p c t", p=128)
                   [:, :, ts * 512:(ts + 1) * 512], reads=[zfm_b], writes=[zch_b])
            sy.dma(dch[:], dfm_hbm.rearrange("(c p) t -> p c t", p=128)
                   [:, :, ts * 512:(ts + 1) * 512], reads=[dfm_b], writes=[dch_b])
            for gi in range(6):
                ph = pf[4]
                pairs = [(aw_sb[:, dc, gi, :], zch[:, dc, :])
                         for dc in range(8)]
                t.matmul_acc(ph.ap[0:32, 0:512], pairs, reads=[aw_b, zch_b],
                             writes=[ph.buf])
                s.op(lambda bb, ph=ph, gi=gi: bb.activation(
                    hdn[:], ph.ap[0:32, 0:512], AF.Tanh, bias=ab_sb[:, gi:gi + 1]),
                    reads=[ph.buf, ab_b], writes=[hdn_b])
                for ec in range(8):
                    pl = pf[ec % 2]
                    t.matmul_acc(pl.ap[:, 0:512],
                                 [(bw_sb[:, gi, ec * 128:(ec + 1) * 128], hdn[:])],
                                 reads=[bw_b, hdn_b], writes=[pl.buf])
                    v.op(lambda bb, pl=pl, gi=gi, ec=ec: bb.tensor_scalar(
                        ddt[:], pl.ap[:, 0:512], lam_sb[:, gi, ec:ec + 1],
                        bb_sb[:, gi, ec:ec + 1], ALU.add, ALU.add),
                        reads=[pl.buf, lam_b, bbias_b], writes=[ddt_b])
                    v.op(lambda bb, ec=ec: bb.tensor_mul(ddt[:], ddt[:], dch[:, ec, :]),
                         reads=[ddt_b, dch_b], writes=[ddt_b])
                    v.op(lambda bb, ec=ec, ts=ts: bb.tensor_tensor(
                        ddt[:], ddt[:], xn_fm_sb[:, ec, ts * 512:(ts + 1) * 512],
                        ALU.add),
                        reads=[ddt_b, xn_fm_buf], writes=[ddt_b])
                    g.dma(dd_hbm[gi, ec * 128:(ec + 1) * 128, ts * 512:(ts + 1) * 512],
                          ddt[:], reads=[ddt_b], writes=[dd_b])
    with P.phase("l3proj"):
        mhd_sb = P.sb([64, 5, 16, 64], BF16); mhd_b = Buf()
        g.dma(mhd_sb[:], wts["L3_attn_mhd_w"].rearrange("g h d e -> d g h e"),
              writes=[mhd_b], allow_slow_non_contiguous=True)
        ddh = P.sb([64, 512], BF16); ddh_b = Buf()
        pev = P.sb([64, 512], BF16); pev_b = Buf()
        for gi in range(5):
            for h in range(16):
                for ts in range(Tt // 512):
                    sy.dma(ddh[:], dd_hbm[gi, h * 64:(h + 1) * 64,
                                          ts * 512:(ts + 1) * 512],
                           reads=[dd_b], writes=[ddh_b])
                    pt = pf[ts % 2]
                    t.matmul_acc(pt.ap[0:64, 0:512],
                                 [(mhd_sb[:, gi, h, :], ddh[:])],
                                 reads=[mhd_b, ddh_b], writes=[pt.buf])
                    v.op(lambda bb, pt=pt: bb.tensor_copy(pev[:], pt.ap[0:64, 0:512]),
                         reads=[pt.buf], writes=[pev_b])
                    sy.dma(kvrwu[gi, h * 64:(h + 1) * 64, ts * 512:(ts + 1) * 512],
                           pev[:], reads=[pev_b], writes=[kvrwu_b])
        gw = WeightFM(P, "l3gw", D, D).load(P, wts["L3_attn_gate_w"])
        dch5 = P.sb([128, 8, 512], BF16); dch5_b = Buf()
        gev = P.sb([128, 512], BF16); gev_b = Buf()
        for ts in range(Tt // 512):
            sy.dma(dch5[:], dd_hbm[5].rearrange("(c p) t -> p c t", p=128)
                   [:, :, ts * 512:(ts + 1) * 512], reads=[dd_b], writes=[dch5_b])
            for oc in range(8):
                pt = pf[oc % 2]
                pairs = [(gw.lhsT(dc, oc * 128, (oc + 1) * 128), dch5[:, dc, :])
                         for dc in range(8)]
                t.matmul_acc(pt.ap[:, 0:512], pairs, reads=[gw.buf, dch5_b],
                             writes=[pt.buf])
                s.op(lambda bb, pt=pt: bb.activation(gev[:], pt.ap[:, 0:512], AF.Silu),
                     reads=[pt.buf], writes=[gev_b])
                sy.dma(gate_hbm[oc * 128:(oc + 1) * 128, ts * 512:(ts + 1) * 512],
                       gev[:], reads=[gev_b], writes=[gate_b])
    import os
    dbg = None
    if os.environ.get("L3_DBG"):
        dbg = {"Y": (P.dscratch("l3_Ydbg", [D, Tt], BF16), Buf()),
               "C": (P.dscratch("l3_Cdbg", [D, Tt], F32), Buf()),
               "G": (P.dscratch("l3_Gdbg", [128, 2080], F32), Buf())}
    for hp in range(8):
        _l3_scan_pair(P, ctx, wts, hp, kvrwu, kvrwu_b, rwkv_hbm, rwkv_b,
                      a_hbm, a_hbm_b, dbg=dbg)
    with P.phase("l3epi"):
        gng_sb, gng_b = bias_cols(P, ctx, "l3gng", wts["L3_attn_gn_gamma"], D)
        gnbt_sb, gnbt_b = bias_cols(P, ctx, "l3gnb", wts["L3_attn_gn_beta"], D)
        ow = WeightFM(P, "l3ow", D, D).load(P, wts["L3_attn_out_w"])
        ob_sb, ob_b = bias_cols(P, ctx, "l3ob", wts["L3_attn_out_b"], D)
        nch = P.sb([128, 8, 512], BF16); nch_b = Buf()
        gch = P.sb([128, 8, 512], BF16); gch_b = Buf()
        evq = P.sb([128, 512], F32); evq_b = Buf()
        ot = [P.sb([128, D], F32) for _ in range(4)]
        ot_b = [Buf() for _ in range(4)]
        for ts in range(Tt // 512):
            sy.dma(nch[:], rwkv_hbm.rearrange("(c p) t -> p c t", p=128)
                   [:, :, ts * 512:(ts + 1) * 512], reads=[rwkv_b], writes=[nch_b])
            sy.dma(gch[:], gate_hbm.rearrange("(c p) t -> p c t", p=128)
                   [:, :, ts * 512:(ts + 1) * 512], reads=[gate_b], writes=[gch_b])
            for dc in range(8):
                v.op(lambda bb, dc=dc: bb.tensor_scalar(
                    nch[:, dc, :], nch[:, dc, :], gng_sb[:, dc:dc + 1],
                    None, ALU.mult),
                    reads=[gng_b], writes=[nch_b])
                v.op(lambda bb, dc=dc: bb.tensor_scalar(
                    nch[:, dc, :], nch[:, dc, :], gnbt_sb[:, dc:dc + 1],
                    None, ALU.add),
                    reads=[gnbt_b], writes=[nch_b])
                v.op(lambda bb, dc=dc: bb.tensor_mul(nch[:, dc, :], nch[:, dc, :],
                                                     gch[:, dc, :]),
                     reads=[gch_b], writes=[nch_b])
            for oc in range(8):
                pt = pf[oc % 2]
                pairs = [(ow.lhsT(dc, oc * 128, (oc + 1) * 128), nch[:, dc, :])
                         for dc in range(8)]
                t.matmul_acc(pt.ap[:, 0:512], pairs, reads=[ow.buf, nch_b],
                             writes=[pt.buf])
                v.op(lambda bb, pt=pt, oc=oc: bb.tensor_scalar(
                    evq[:], pt.ap[:, 0:512], ob_sb[:, oc:oc + 1], None, ALU.add),
                    reads=[pt.buf, ob_b], writes=[evq_b])
                for tsub in range(4):
                    pt2 = pf[4]
                    t.transpose(pt2.ap[:, 0:128], evq[:, tsub * 128:(tsub + 1) * 128],
                                idf_sb[:], reads=[evq_b, idf_buf], writes=[pt2.buf])
                    v.op(lambda bb, pt2=pt2, oc=oc, tsub=tsub: bb.tensor_copy(
                        ot[tsub][:, oc * 128:(oc + 1) * 128], pt2.ap[:, 0:128]),
                        reads=[pt2.buf], writes=[ot_b[tsub]])
                    if oc == 7:
                        ti = ts * 4 + tsub
                        sy.dma(attn_out[ti * 128:(ti + 1) * 128, :], ot[tsub][:],
                               reads=[ot_b[tsub]], writes=[attn_out_b])


def _l3_scan_pair(P, ctx, wts, hp, kvrwu, kvrwu_b, rwkv_hbm, rwkv_b, a_hbm, a_hbm_b,
                  dbg=None):
    v, s, t, sy, g = P.vector, P.scalar, P.tensor, P.sync, P.gpsimd
    NT = NTILE
    Tt = NT * 128
    NC = Tt // 32
    NS = Tt // 512 if Tt >= 512 else 1
    SC = min(512, Tt)
    idf_sb, idf_buf = cref(ctx, "ident_f32")
    idb_sb, idb_buf = cref(ctx, "ident_bf")
    onesr_sb, onesr_buf = cref(ctx, "ones_row")
    onesc_sb, onesc_buf = cref(ctx, "ones_col")
    epssb, epsbuf = cref(ctx, "eps_cols")
    m32_sb, m32_buf = cref(ctx, "mask_strict32")
    hsel_sb, hsel_buf = cref(ctx, "headsel")
    pf = ctx["pf"]; pbk = ctx["pb"]
    r0, r1 = hp * 128, (hp + 1) * 128
    with P.phase(f"l3sc{hp}"):
        kf = P.sb([128, Tt], BF16); kf_b = Buf()
        vf = P.sb([128, Tt], BF16); vf_b = Buf()
        rf = P.sb([128, Tt], BF16); rf_b = Buf()
        wf = P.sb([128, Tt], BF16); wf_b = Buf()
        uf = P.sb([128, Tt], BF16); uf_b = Buf()
        for ap_sb, ap_b, gi in ((kf, kf_b, 0), (vf, vf_b, 1), (rf, rf_b, 2),
                                (wf, wf_b, 3), (uf, uf_b, 4)):
            sy.dma(ap_sb[:], kvrwu[gi, r0:r1, :], reads=[kvrwu_b], writes=[ap_b])
        wa_sb = P.sb([128, 64], BF16); wa_b = Buf()
        wbv_sb = P.sb([128, 64], BF16); wbv_b = Buf()
        g.dma(wa_sb[:], wts["L3_attn_wkv_A"][2 * hp:2 * hp + 2]
              .rearrange("h d e -> (h d) e"), writes=[wa_b])
        g.dma(wbv_sb[:], wts["L3_attn_wkv_B"][2 * hp:2 * hp + 2]
              .rearrange("h d e -> (h d) e"), writes=[wbv_b])
        lam_sb = P.sb([128, 1], F32); lam_b = Buf()
        sy.dma(lam_sb[:], wts["L3_attn_wkv_lamb"][2 * hp:2 * hp + 2]
               .rearrange("h d -> (h d)").rearrange("(p o) -> p o", o=1),
               writes=[lam_b])
        wh = P.sb([128, SC], BF16); wh_b = Buf()
        NSUP = Tt // SC
        cpad = P.sb([128, NSUP, SC + 1], F32); cpad_b = Buf()
        v.op(lambda bb: bb.memset(cpad[:, :, 0:1], 0.0), writes=[cpad_b])
        zrow = P.sb([128, SC], F32); zrow_b = Buf()
        v.op(lambda bb: bb.memset(zrow[:], 0.0), writes=[zrow_b])
        lwt = P.sb([128, SC], F32); lwt_b = Buf()
        for ts in range(Tt // SC):
            for h2 in range(2):
                hb = h2 * 64
                pw = pf[2]
                t.matmul_acc(pw.ap[0:64, 0:SC] if hb == 0 else pw.ap[64:128, 0:SC],
                             [(wa_sb[hb:hb + 64, :], wf[hb:hb + 64, ts * SC:(ts + 1) * SC])],
                             reads=[wa_b, wf_b], writes=[pw.buf],
                             flags=[(True, True)])
            s.op(lambda bb, pw=pf[2]: bb.activation(wh[:], pw.ap[:, 0:SC], AF.Tanh),
                 reads=[pf[2].buf], writes=[wh_b])
            for h2 in range(2):
                hb = h2 * 64
                pw = pf[3]
                t.matmul_acc(pw.ap[0:64, 0:SC] if hb == 0 else pw.ap[64:128, 0:SC],
                             [(wbv_sb[hb:hb + 64, :], wh[hb:hb + 64, :])],
                             reads=[wbv_b, wh_b], writes=[pw.buf],
                             flags=[(True, True)])
            s.op(lambda bb, pw=pf[3]: bb.activation(lwt[:], pw.ap[:, 0:SC], AF.Exp,
                                                    bias=lam_sb[:]),
                 reads=[pf[3].buf, lam_b], writes=[lwt_b])
            v.op(lambda bb: bb.tensor_scalar_mul(lwt[:], lwt[:], -1.0),
                 reads=[lwt_b], writes=[lwt_b])
            v.op(lambda bb, ts=ts: bb.tensor_tensor_scan(
                cpad[:, ts, 1:SC + 1], lwt[:], zrow[:],
                0.0, ALU.add, ALU.add),
                reads=[lwt_b, zrow_b], writes=[cpad_b])
        Y = P.sb([128, Tt], BF16); Y_b = Buf()
        vr = P.sb([128, SC], BF16); vr_b = Buf()
        ar2 = P.sb([2, Tt], BF16); ar2_b = Buf()
        arow1 = P.sb([1, SC], BF16); arow1_b = Buf()
        ar_hbm = P.dscratch(f"l3_ar{hp}", [2, Tt], BF16); ar_hbm_b = Buf()
        for ts in range(Tt // SC):
            v.op(lambda bb, ts=ts: bb.tensor_tensor(
                vr[:], vf[:, ts * SC:(ts + 1) * SC], rf[:, ts * SC:(ts + 1) * SC],
                ALU.mult),
                reads=[vf_b, rf_b], writes=[vr_b])
            for h2 in range(2):
                pa = pf[2]
                t.matmul_acc(pa.ap[0:1, 0:SC],
                             [(onesc_sb[h2 * 64:h2 * 64 + 64, :],
                               vr[h2 * 64:h2 * 64 + 64, :])],
                             reads=[vr_b, onesc_buf], writes=[pa.buf])
                v.op(lambda bb, pa=pa: bb.tensor_copy(arow1[:], pa.ap[0:1, 0:SC]),
                     reads=[pa.buf], writes=[arow1_b])
                sy.dma(ar_hbm[h2, ts * SC:(ts + 1) * SC].rearrange("(o f) -> o f", o=1),
                       arow1[:], reads=[arow1_b], writes=[ar_hbm_b])
        sy.dma(ar2[:], ar_hbm, reads=[ar_hbm_b], writes=[ar2_b])
        for ts in range(Tt // SC):
            pb2 = pf[3]
            t.matmul_acc(pb2.ap[:, 0:SC], [(hsel_sb[:], ar2[:, ts * SC:(ts + 1) * SC])],
                         reads=[ar2_b, hsel_buf], writes=[pb2.buf])
            v.op(lambda bb, ts=ts, pb2=pb2: bb.tensor_tensor(
                Y[:, ts * SC:(ts + 1) * SC], uf[:, ts * SC:(ts + 1) * SC],
                pb2.ap[:, 0:SC], ALU.mult),
                reads=[uf_b, pb2.buf], writes=[Y_b])
            v.op(lambda bb, ts=ts: bb.tensor_tensor(
                Y[:, ts * SC:(ts + 1) * SC], Y[:, ts * SC:(ts + 1) * SC],
                kf[:, ts * SC:(ts + 1) * SC], ALU.mult),
                reads=[kf_b], writes=[Y_b])
        Sst = P.sb([128, 64], F32); Sst_b = Buf()
        v.op(lambda bb: bb.memset(Sst[:], 0.0), writes=[Sst_b])
        abuf = P.sb([64, 32], BF16); abuf_b = Buf()
        aflat = P.sb([2, 1024], BF16); aflat_b = Buf()
        etile = P.sb([128, 512], BF16); etile_b = Buf()
        gtile = P.sb([128, 512], F32); gtile_b = Buf()
        tmp32 = P.sb([128, 32], F32); tmp32_b = Buf()
        ftile = P.sb([128, 32], F32); ftile_b = Buf()
        rscr = P.sb([64, 32], BF16); rscr_b = Buf()
        st_sb = P.sb([64, 128], BF16); st_sb_b = Buf()
        kbar = P.sb([128, 32], BF16); kbar_b = Buf()
        kbt = P.sb([32, 128], BF16); kbt_b = Buf()
        vtk = P.sb([32, 128], BF16); vtk_b = Buf()
        dcol = P.sb([128, 1], F32); dcol_b = Buf()
        for c in range(NC):
            c0 = c * 32
            sc = c0 // SC
            lo = c0 % SC
            ctm1 = cpad[:, sc, lo:lo + 32]
            csrow = cpad[:, sc, lo + 1:lo + 33]
            ccol = cpad[:, sc, lo:lo + 1]
            cnewc = cpad[:, sc, lo + 32:lo + 33]
            pA = pf[2]
            for h2 in range(2):
                hb = h2 * 64
                t.matmul_acc(pA.ap[h2 * 32:h2 * 32 + 32, 0:32],
                             [(rf[hb:hb + 64, c0:c0 + 32], vf[hb:hb + 64, c0:c0 + 32])],
                             reads=[rf_b, vf_b], writes=[pA.buf],
                             flags=[(True, True)])
            for h2 in range(2):
                v.op(lambda bb, h2=h2, pA=pA: bb.tensor_tensor(
                    abuf[h2 * 32:h2 * 32 + 32, :], pA.ap[h2 * 32:h2 * 32 + 32, 0:32],
                    m32_sb[:], ALU.mult),
                    reads=[pA.buf, m32_buf], writes=[abuf_b])
            sy.dma(a_hbm, abuf[:], reads=[abuf_b], writes=[a_hbm_b])
            sy.dma(aflat[:], a_hbm.rearrange("(h t) s -> h (t s)", h=2),
                   reads=[a_hbm_b], writes=[aflat_b])
            pb3, pb4 = pf[3], pf[4]
            t.matmul_acc(pb3.ap[:, 0:512], [(hsel_sb[:], aflat[:, 0:512])],
                         reads=[aflat_b, hsel_buf], writes=[pb3.buf])
            t.matmul_acc(pb4.ap[:, 0:512], [(hsel_sb[:], aflat[:, 512:1024])],
                         reads=[aflat_b, hsel_buf], writes=[pb4.buf])
            for half, pbx, pbx_b in ((0, pb3.ap, pb3.buf), (1, pb4.ap, pb4.buf)):
                t0 = half * 16
                cin0 = cpad[:, sc, lo + t0:lo + t0 + 16]
                v.op(lambda bb, cin0=cin0, csrow=csrow: bb.tensor_tensor(
                    gtile[:].rearrange("p (a b) -> p a b", a=16),
                    cin0.rearrange("p (a o) -> p a o", o=1).broadcast_to([128, 16, 32]),
                    csrow.rearrange("p (o b) -> p o b", o=1)
                    .broadcast_to([128, 16, 32]), ALU.subtract),
                    reads=[cpad_b], writes=[gtile_b])
                v.op(lambda bb: bb.tensor_scalar_min(gtile[:], gtile[:], 0.0),
                     reads=[gtile_b], writes=[gtile_b])
                s.op(lambda bb: bb.activation(etile[:], gtile[:], AF.Exp),
                     reads=[gtile_b], writes=[etile_b])
                v.op(lambda bb, c0=c0: bb.tensor_tensor(
                    etile[:].rearrange("p (a b) -> p a b", a=16), 
                    etile[:].rearrange("p (a b) -> p a b", a=16),
                    kf[:, c0:c0 + 32].rearrange("p (o b) -> p o b", o=1)
                    .broadcast_to([128, 16, 32]), ALU.mult),
                    reads=[kf_b], writes=[etile_b])
                v.op(lambda bb, pbx=pbx: bb.tensor_tensor(gtile[:], etile[:],
                                                          pbx[:, 0:512], ALU.mult),
                     reads=[etile_b, pbx_b], writes=[gtile_b])
                if dbg is not None and hp == 0 and c == 0 and half == 0:
                    sy.dma(dbg["G"][0][:, 0:512], gtile[:], reads=[gtile_b],
                           writes=[dbg["G"][1]])
                    g.dma(dbg["G"][0][:, 512:1024], etile[:], reads=[etile_b],
                           writes=[dbg["G"][1]])
                    g.dma(dbg["G"][0][0:64, 1024:1056],
                           abuf[:, 0:32], reads=[abuf_b], writes=[dbg["G"][1]])
                    g.dma(dbg["G"][0][0:2, 1056:2080], aflat[:], reads=[aflat_b],
                           writes=[dbg["G"][1]])
                v.op(lambda bb: bb.tensor_reduce(
                    tmp32[:, 0:16],
                    gtile[:].rearrange("p (a b) -> p a b", a=16), AX.X, ALU.add),
                    reads=[gtile_b], writes=[tmp32_b])
                v.op(lambda bb, t0=t0, c0=c0: bb.tensor_tensor(
                    Y[:, c0 + t0:c0 + t0 + 16], Y[:, c0 + t0:c0 + t0 + 16],
                    tmp32[:, 0:16], ALU.add),
                    reads=[tmp32_b], writes=[Y_b])
            for h2 in range(2):
                pt6 = pbk[0]
                psS = pf[5]
                t.transpose(psS.ap[0:64, 0:64], Sst[h2 * 64:h2 * 64 + 64, :],
                            idf_sb[h2 * 64:h2 * 64 + 64, h2 * 64:h2 * 64 + 64],
                            reads=[Sst_b, idf_buf], writes=[psS.buf])
                s.op(lambda bb, psS=psS, h2=h2: bb.copy(st_sb[:, h2 * 64:h2 * 64 + 64],
                                                        psS.ap[0:64, 0:64]),
                     reads=[psS.buf], writes=[st_sb_b])
            v.op(lambda bb, c0=c0: bb.tensor_copy(rscr[:, 0:32], rf[64:128, c0:c0 + 32]),
                 reads=[rf_b], writes=[rscr_b])
            pI = pf[5]
            t.matmul_acc(pI.ap[0:64, 0:32],
                         [(st_sb[:, 0:64], rf[0:64, c0:c0 + 32])],
                         reads=[st_sb_b, rf_b], writes=[pI.buf],
                         flags=[(True, True)])
            t.matmul_acc(pI.ap[64:128, 0:32],
                         [(st_sb[:, 64:128], rscr[:, 0:32])],
                         reads=[st_sb_b, rscr_b], writes=[pI.buf],
                         flags=[(True, True)])
            v.op(lambda bb, ctm1=ctm1, ccol=ccol: bb.tensor_scalar(
                ftile[:], ctm1, ccol, None, ALU.subtract),
                reads=[cpad_b], writes=[ftile_b])
            s.op(lambda bb: bb.activation(ftile[:], ftile[:], AF.Exp),
                 reads=[ftile_b], writes=[ftile_b])
            v.op(lambda bb, pI=pI: bb.tensor_tensor(ftile[:], ftile[:],
                                                    pI.ap[:, 0:32], ALU.mult),
                 reads=[ftile_b, pI.buf], writes=[ftile_b])
            v.op(lambda bb, c0=c0: bb.tensor_tensor(
                Y[:, c0:c0 + 32], Y[:, c0:c0 + 32], ftile[:], ALU.add),
                reads=[ftile_b], writes=[Y_b])
            v.op(lambda bb, csrow=csrow, cnewc=cnewc: bb.tensor_scalar(
                tmp32[:], csrow, cnewc, None, ALU.subtract),
                reads=[cpad_b], writes=[tmp32_b])
            v.op(lambda bb: bb.tensor_scalar_mul(tmp32[:], tmp32[:], -1.0),
                 reads=[tmp32_b], writes=[tmp32_b])
            s.op(lambda bb: bb.activation(tmp32[:], tmp32[:], AF.Exp),
                 reads=[tmp32_b], writes=[tmp32_b])
            v.op(lambda bb, c0=c0: bb.tensor_tensor(kbar[:], tmp32[:],
                                                    kf[:, c0:c0 + 32], ALU.mult),
                 reads=[tmp32_b, kf_b], writes=[kbar_b])
            pt7 = pbk[1]
            t.transpose(pt7.ap[0:32, 0:128], kbar[:], idb_sb[:],
                        reads=[kbar_b, idb_buf], writes=[pt7.buf])
            v.op(lambda bb, pt7=pt7: bb.tensor_copy(kbt[:], pt7.ap[0:32, 0:128]),
                 reads=[pt7.buf], writes=[kbt_b])
            pt8 = pbk[1]
            t.transpose(pt8.ap[0:32, 0:128], vf[:, c0:c0 + 32], idb_sb[:],
                        reads=[vf_b, idb_buf], writes=[pt8.buf])
            v.op(lambda bb, pt8=pt8: bb.tensor_copy(vtk[:], pt8.ap[0:32, 0:128]),
                 reads=[pt8.buf], writes=[vtk_b])
            v.op(lambda bb, cnewc=cnewc, ccol=ccol: bb.tensor_tensor(
                dcol[:], cnewc, ccol, ALU.subtract),
                reads=[cpad_b], writes=[dcol_b])
            s.op(lambda bb: bb.activation(dcol[:], dcol[:], AF.Exp),
                 reads=[dcol_b], writes=[dcol_b])
            v.op(lambda bb: bb.tensor_scalar(Sst[:], Sst[:], dcol[:], None, ALU.mult),
                 reads=[dcol_b], writes=[Sst_b])
            pD = pf[2]
            t.matmul_acc(pD.ap[0:64, 0:64], [(kbt[:, 0:64], vtk[:, 0:64])],
                         reads=[kbt_b, vtk_b], writes=[pD.buf], flags=[(True, True)])
            t.matmul_acc(pD.ap[64:128, 0:64], [(kbt[:, 64:128], vtk[:, 64:128])],
                         reads=[kbt_b, vtk_b], writes=[pD.buf], flags=[(True, True)])
            v.op(lambda bb, pD=pD: bb.tensor_tensor(Sst[:], Sst[:], pD.ap[:, 0:64],
                                                    ALU.add),
                 reads=[pD.buf, Sst_b], writes=[Sst_b])
        if dbg is not None:
            for ts in range(Tt // SC):
                sy.dma(dbg["Y"][0][r0:r1, ts * SC:(ts + 1) * SC],
                       Y[:, ts * SC:(ts + 1) * SC], reads=[Y_b], writes=[dbg["Y"][1]])
            for sc2 in range(Tt // SC):
                sy.dma(dbg["C"][0][r0:r1, sc2 * SC:(sc2 + 1) * SC],
                       cpad[:, sc2, 1:SC + 1], reads=[cpad_b], writes=[dbg["C"][1]])
        sqv = P.sb([128, SC], BF16); sqv_b = Buf()
        mrow = P.sb([1, SC], F32); mrow_b = Buf()
        vrow = P.sb([1, SC], F32); vrow_b = Buf()
        brow = P.sb([1, SC], BF16); brow_b = Buf()
        nout = P.sb([128, SC], BF16); nout_b = Buf()
        for ts in range(Tt // SC):
            ysl = Y[:, ts * SC:(ts + 1) * SC]
            s.op(lambda bb, ysl=ysl: bb.activation(sqv[:], ysl, AF.Square),
                 reads=[Y_b], writes=[sqv_b])
            for h2 in range(2):
                hb = h2 * 64
                pm = pf[2]
                pq = pf[3]
                t.matmul_acc(pm.ap[0:1, 0:SC],
                             [(onesc_sb[hb:hb + 64, :], ysl[hb:hb + 64, :]
                               if False else Y[hb:hb + 64, ts * SC:(ts + 1) * SC])],
                             reads=[Y_b, onesc_buf], writes=[pm.buf])
                t.matmul_acc(pq.ap[0:1, 0:SC],
                             [(onesc_sb[hb:hb + 64, :], sqv[hb:hb + 64, :])],
                             reads=[sqv_b, onesc_buf], writes=[pq.buf])
                v.op(lambda bb, pm=pm: bb.tensor_scalar_mul(mrow[:], pm.ap[0:1, 0:SC],
                                                            1.0 / 64),
                     reads=[pm.buf], writes=[mrow_b])
                v.op(lambda bb, pq=pq: bb.tensor_scalar_mul(vrow[:], pq.ap[0:1, 0:SC],
                                                            1.0 / 64),
                     reads=[pq.buf], writes=[vrow_b])
                v.op(lambda bb: bb.tensor_mul(zrow[0:1, :], mrow[:], mrow[:]),
                     reads=[mrow_b], writes=[zrow_b])
                v.op(lambda bb: bb.tensor_tensor(vrow[:], vrow[:], zrow[0:1, :],
                                                 ALU.subtract),
                     reads=[zrow_b, vrow_b], writes=[vrow_b])
                s.op(lambda bb: bb.activation(vrow[:], vrow[:], AF.Ln,
                                              bias=epssb[0:1, 1:2]),
                     reads=[vrow_b, epsbuf], writes=[vrow_b])
                s.op(lambda bb: bb.activation(vrow[:], vrow[:], AF.Exp, scale=-0.5),
                     reads=[vrow_b], writes=[vrow_b])
                v.op(lambda bb: bb.tensor_copy(brow[:], mrow[:]),
                     reads=[mrow_b], writes=[brow_b])
                pmb = pf[4]
                t.matmul_acc(pmb.ap[0:64, 0:SC], [(onesr_sb[:, 0:64], brow[:])],
                             reads=[brow_b, onesr_buf], writes=[pmb.buf])
                v.op(lambda bb: bb.tensor_copy(brow[:], vrow[:]),
                     reads=[vrow_b], writes=[brow_b])
                prb = pf[5]
                t.matmul_acc(prb.ap[0:64, 0:SC], [(onesr_sb[:, 0:64], brow[:])],
                             reads=[brow_b, onesr_buf], writes=[prb.buf])
                v.op(lambda bb, hb=hb, ts=ts, pmb=pmb: bb.tensor_tensor(
                    nout[hb:hb + 64, :], Y[hb:hb + 64, ts * SC:(ts + 1) * SC],
                    pmb.ap[0:64, 0:SC], ALU.subtract),
                    reads=[Y_b, pmb.buf], writes=[nout_b])
                v.op(lambda bb, hb=hb, prb=prb: bb.tensor_tensor(
                    nout[hb:hb + 64, :], nout[hb:hb + 64, :], prb.ap[0:64, 0:SC],
                    ALU.mult),
                    reads=[prb.buf], writes=[nout_b])
            sy.dma(rwkv_hbm[r0:r1, ts * SC:(ts + 1) * SC], nout[:],
                   reads=[nout_b], writes=[rwkv_b])


def emit_l3_ffn(P, ctx, wts, xn_fm_sb, xn_fm_buf, ffn_out, ffn_out_b):
    v, s, t, sy, g = P.vector, P.scalar, P.tensor, P.sync, P.gpsimd
    NT = NTILE
    Tt = NT * 128
    idf_sb, idf_buf = cref(ctx, "ident_f32")
    onesr_sb, onesr_buf = cref(ctx, "ones_row")
    pf = ctx["pf"]
    with P.phase("l3ffn"):
        wq = WeightFM(P, "l3wq", D, D)
        wraw = P.sb([128, 512], F32); wraw_b = Buf()
        wi = P.sb([128, 512], I32); wi_b = Buf()
        wf2 = P.sb([128, 512], F32); wf2_b = Buf()
        mcol = P.sb([128, 1], F32); mcol_b = Buf()
        macc = P.sb([128, 1], F32); macc_b = Buf()
        mpad = P.sb([128, 128], F32); mpad_b = Buf()
        srow = P.sb([1, 128], F32); srow_b = Buf()
        sval = P.sb([1, 1], BF16); sval_b = Buf()
        scol = P.sb([128, 1], F32); scol_b = Buf()
        rcol = P.sb([128, 1], F32); rcol_b = Buf()
        nt2 = D * D // (128 * 512)
        for i in range(nt2):
            a_, b_ = divmod(i, 2)
            sy.dma(wraw[:], wts["L3_ffn_kernel"][a_ * 128:(a_ + 1) * 128,
                                                 b_ * 512:(b_ + 1) * 512],
                   reads=[], writes=[wraw_b])
            v.op(lambda bb: bb.tensor_reduce(mcol[:], wraw[:], AX.X, ALU.max,
                                             apply_absolute_value=True),
                 reads=[wraw_b], writes=[mcol_b])
            if i == 0:
                v.op(lambda bb: bb.tensor_copy(macc[:], mcol[:]),
                     reads=[mcol_b], writes=[macc_b])
            else:
                v.op(lambda bb: bb.tensor_max(macc[:], macc[:], mcol[:]),
                     reads=[mcol_b, macc_b], writes=[macc_b])
        v.op(lambda bb: bb.memset(mpad[:], 0.0), writes=[mpad_b])
        v.op(lambda bb: bb.tensor_copy(mpad[:, 0:1], macc[:]),
             reads=[macc_b], writes=[mpad_b])
        pt = pf[0]
        t.transpose(pt.ap[:, 0:128], mpad[:], idf_sb[:],
                    reads=[mpad_b, idf_buf], writes=[pt.buf])
        s.op(lambda bb, pt=pt: bb.copy(srow[:], pt.ap[0:1, 0:128]),
             reads=[pt.buf], writes=[srow_b])
        v.op(lambda bb: bb.tensor_reduce(srow[:, 0:1], srow[:], AX.X, ALU.max),
             reads=[srow_b], writes=[srow_b])
        v.op(lambda bb: bb.tensor_scalar_mul(srow[:, 0:1], srow[:, 0:1], 1.0 / 7.0),
             reads=[srow_b], writes=[srow_b])
        sc_hbm = P.dscratch("l3_scale", [1, 1], F32)
        sc_hbm_b = Buf()
        sy.dma(sc_hbm, srow[0:1, 0:1], reads=[srow_b], writes=[sc_hbm_b])
        sy.dma(scol[:], sc_hbm.broadcast_to([128, 1]), reads=[sc_hbm_b],
               writes=[scol_b], allow_slow_non_contiguous=True)
        v.op(lambda bb: bb.reciprocal(rcol[:], scol[:]),
             reads=[scol_b], writes=[rcol_b])
        wqv = wq.sb[:].rearrange("p c o -> p (c o)")
        for i in range(nt2):
            a_, b_ = divmod(i, 2)
            sy.dma(wraw[:], wts["L3_ffn_kernel"][a_ * 128:(a_ + 1) * 128,
                                                 b_ * 512:(b_ + 1) * 512],
                   reads=[], writes=[wraw_b])
            v.op(lambda bb: bb.tensor_scalar(wf2[:], wraw[:], rcol[:], None, ALU.mult),
                 reads=[wraw_b, rcol_b], writes=[wf2_b])
            v.op(lambda bb: bb.tensor_copy(wi[:], wf2[:]),
                 reads=[wf2_b], writes=[wi_b])
            v.op(lambda bb: bb.tensor_copy(wf2[:], wi[:]),
                 reads=[wi_b], writes=[wf2_b])
            v.op(lambda bb: bb.tensor_scalar_min(wf2[:], wf2[:], 7.0),
                 reads=[wf2_b], writes=[wf2_b])
            v.op(lambda bb: bb.tensor_scalar_max(wf2[:], wf2[:], -8.0),
                 reads=[wf2_b], writes=[wf2_b])
            v.op(lambda bb, a_=a_, b_=b_: bb.tensor_scalar(
                wq.sb[:, a_, b_ * 512:(b_ + 1) * 512], wf2[:], scol[:], None, ALU.mult),
                reads=[wf2_b, scol_b], writes=[wq.buf])
        bias_sb, bias_b = bias_cols(P, ctx, "l3fb", wts["L3_ffn_bias"], D)
        a_sb = P.sb([128, 8, 16], BF16); a_b = Buf()
        g.dma(a_sb[:], wts["L3_ffn_a"].rearrange("(c p) r -> p c r", p=128),
              writes=[a_b])
        b_sb = P.sb([16, D], BF16); bmat_b = Buf()
        g.dma(b_sb[:], wts["L3_ffn_b"], writes=[bmat_b])
        xa = P.sb([16, 512], BF16); xa_b = Buf()
        evq = P.sb([128, 512], F32); evq_b = Buf()
        ot = [P.sb([128, D], F32) for _ in range(4)]
        ot_b = [Buf() for _ in range(4)]
        for ts in range(Tt // 512):
            pxa = pf[2]
            pairs = [(a_sb[:, dc, :], xn_fm_sb[:, dc, ts * 512:(ts + 1) * 512])
                     for dc in range(8)]
            t.matmul_acc(pxa.ap[0:16, 0:512], pairs, reads=[a_b, xn_fm_buf],
                         writes=[pxa.buf])
            v.op(lambda bb, pxa=pxa: bb.tensor_copy(xa[:], pxa.ap[0:16, 0:512]),
                 reads=[pxa.buf], writes=[xa_b])
            for oc in range(8):
                pt2 = pf[oc % 2]
                pairs = [(wq.lhsT(dc, oc * 128, (oc + 1) * 128),
                          xn_fm_sb[:, dc, ts * 512:(ts + 1) * 512]) for dc in range(8)]
                pairs.append((b_sb[:, oc * 128:(oc + 1) * 128], xa[:]))
                flags = [(i2 == 0, False) for i2 in range(8)] + [(False, True)]
                t.matmul_acc(pt2.ap[:, 0:512], pairs,
                             reads=[wq.buf, xn_fm_buf, bmat_b, xa_b],
                             writes=[pt2.buf], flags=flags)
                v.op(lambda bb, pt2=pt2, oc=oc: bb.tensor_scalar(
                    evq[:], pt2.ap[:, 0:512], bias_sb[:, oc:oc + 1], None, ALU.add),
                    reads=[pt2.buf, bias_b], writes=[evq_b])
                for tsub in range(4):
                    pt3 = pf[4]
                    t.transpose(pt3.ap[:, 0:128], evq[:, tsub * 128:(tsub + 1) * 128],
                                idf_sb[:], reads=[evq_b, idf_buf], writes=[pt3.buf])
                    v.op(lambda bb, pt3=pt3, oc=oc, tsub=tsub: bb.tensor_copy(
                        ot[tsub][:, oc * 128:(oc + 1) * 128], pt3.ap[:, 0:128]),
                        reads=[pt3.buf], writes=[ot_b[tsub]])
                    if oc == 7:
                        ti = ts * 4 + tsub
                        sy.dma(ffn_out[ti * 128:(ti + 1) * 128, :], ot[tsub][:],
                               reads=[ot_b[tsub]], writes=[ffn_out_b])


CAP = 5120
SEG0 = 768
SEG1 = 1536
NSLOT0 = 8 * SEG0
NSLOT1 = 8 * SEG1

K_L1ATTN = ["L1_attn_ln1_g", "L1_attn_ln1_b", "L1_attn_w_qkv", "L1_attn_b_qkv",
            "L1_attn_w_out", "L1_attn_b_out", "L1_attn_ln2_g", "L1_attn_ln2_b",
            "L1_attn_w1", "L1_attn_b1", "L1_attn_w2", "L1_attn_b2"]
K_L3 = ["L3_attn_mu_x", "L3_attn_lora_lambda", "L3_attn_lora_A_w",
        "L3_attn_lora_A_b", "L3_attn_lora_B_w", "L3_attn_lora_B_b",
        "L3_attn_mhd_w", "L3_attn_gate_w", "L3_attn_wkv_lamb", "L3_attn_wkv_A",
        "L3_attn_wkv_B", "L3_attn_gn_gamma", "L3_attn_gn_beta", "L3_attn_out_w",
        "L3_attn_out_b", "L3_ffn_kernel", "L3_ffn_bias", "L3_ffn_a", "L3_ffn_b"]


def _hbm_copy(P, src, src_b, dst, dst_b, dtype, rows, cols):
    sy = P.sync
    with P.phase("hcopy"):
        tb = [P.sb([128, cols], dtype) for _ in range(2)]
        tbb = [Buf() for _ in range(2)]
        for i in range(rows // 128):
            b = i % 2
            sy.dma(tb[b][:], src[i * 128:(i + 1) * 128, :], reads=[src_b],
                   writes=[tbb[b]])
            sy.dma(dst[i * 128:(i + 1) * 128, :], tb[b][:], reads=[tbb[b]],
                   writes=[dst_b])


def _wts_din(P, shapes, keys):
    return {k: P.din(k, list(shapes[k]), F32) for k in keys}


def _load_rlog(P, name):
    rlog_in = P.din(name, [128, NTILE * 8], F32)
    rl = P.sb([128, NTILE, 8], F32, name=name + "sb"); rl_b = Buf()
    P.sync.dma(rl[:], rlog_in.rearrange("p (n e) -> p n e", e=8), writes=[rl_b])
    return rl, rl_b


def _build_A(P, shapes):
    x_src = P.din("x_in", [T, D], F32)
    wts = _wts_din(P, shapes, ["L0_ffn_Wr", "L0_attn_wq", "L0_attn_wk",
                               "L0_attn_wv", "L0_attn_wo"])
    xnt0o = P.dout("xnt0", [T, D], BF16); xnt0o_b = Buf()
    attn0 = P.dout("attn0", [T, D], F32); attn0_b = Buf()
    rlog_out = P.dout("rlog0", [128, NTILE * 8], F32); rlog_out_b = Buf()
    disp0 = P.dout("disp0", [NSLOT0, D], BF16); disp0_b = Buf()
    ctx = make_ctx(P)
    xn_fm = P.sb([128, 8, T], BF16, name="xn_fm"); xn_fm_buf = Buf("xn_fm")
    wr_sb = P.sb([128, 8, 8], BF16, name="wr0bf"); wr_buf = Buf()
    P.gpsimd.dma(wr_sb[:], wts["L0_ffn_Wr"].rearrange("(c p) e -> p c e", p=128),
                 writes=[wr_buf])
    rlog0 = P.sb([128, NTILE, 8], F32, name="rlog0sb"); rlog0_b = Buf()
    xnt0 = P.dscratch("xn_tok0", [T, D], BF16); xnt0_b = Buf()
    with P.phase("pre0"):
        emit_pre_pass(P, ctx, (x_src, Buf()), [], None, (xnt0, xnt0_b),
                      xn_fm, xn_fm_buf, xn_tok2=(xnt0o, xnt0o_b))
    emit_rlog_fm(P, ctx, wr_sb, wr_buf, xn_fm, xn_fm_buf, rlog0, rlog0_b)
    P.sync.dma(rlog_out, rlog0[:].rearrange("p n e -> p (n e)"),
               reads=[rlog0_b], writes=[rlog_out_b])
    with P.phase("l0disp"):
        emit_l0_moe(P, ctx, wts, (xnt0, xnt0_b), None, rlog0, rlog0_b,
                    collective=False, cap=CAP, seg=SEG0, stage="disp",
                    disp_io=(disp0, disp0_b))
    emit_l0_attn(P, ctx, wts, xn_fm, xn_fm_buf, attn0, attn0_b)
    P.final_barrier()


def _build_B2(P, shapes):
    xnt0 = P.din("xnt0", [T, D], BF16); xnt0_b = Buf()
    disp0 = P.din("disp0", [NSLOT0, D], BF16); disp0_b = Buf()
    wts = _wts_din(P, shapes, ["L0_ffn_W1", "L0_ffn_W2"])
    ffn0 = P.dout("ffn0", [T, D], F32); ffn0_b = Buf()
    ctx = make_ctx(P)
    rlog0, rlog0_b = _load_rlog(P, "rlog0")
    with P.phase("l0moe"):
        emit_l0_moe(P, ctx, wts, (xnt0, xnt0_b), (ffn0, ffn0_b),
                    rlog0, rlog0_b, collective=True, cap=CAP, seg=SEG0,
                    stage="expcomb", disp_io=(disp0, disp0_b))
    P.final_barrier()


def _build_C(P, shapes):
    x_src = P.din("x_in", [T, D], F32)
    attn0 = P.din("attn0", [T, D], F32)
    ffn0 = P.din("ffn0", [T, D], F32)
    wts = _wts_din(P, shapes, ["L1_ffn_Wg"] + K_L1ATTN)
    xr0 = P.dout("xr0", [T, D], F32); xr0_b = Buf()
    xnt1o = P.dout("xnt1", [T, D], BF16); xnt1o_b = Buf()
    attn1 = P.dout("attn1", [T, D], F32); attn1_b = Buf()
    rlog_out = P.dout("rlog1", [128, NTILE * 8], F32); rlog_out_b = Buf()
    disp1 = P.dout("disp1", [NSLOT1, D], BF16); disp1_b = Buf()
    ctx = make_ctx(P)
    xn_fm = P.sb([128, 8, T], BF16, name="xn_fm"); xn_fm_buf = Buf("xn_fm")
    wg_sb = P.sb([128, 8, 8], BF16, name="wg1bf"); wg_buf = Buf()
    P.gpsimd.dma(wg_sb[:], wts["L1_ffn_Wg"].rearrange("(c p) e -> p c e", p=128),
                 writes=[wg_buf])
    rlog1 = P.sb([128, NTILE, 8], F32, name="rlog1sb"); rlog1_b = Buf()
    xnt1 = P.dscratch("xn_tok1", [T, D], BF16); xnt1_b = Buf()
    with P.phase("pre1"):
        emit_pre_pass(P, ctx, (x_src, Buf()), [(attn0, Buf()), (ffn0, Buf())],
                      (xr0, xr0_b), (xnt1, xnt1_b), xn_fm, xn_fm_buf,
                      xn_tok2=(xnt1o, xnt1o_b))
    emit_rlog_fm(P, ctx, wg_sb, wg_buf, xn_fm, xn_fm_buf, rlog1, rlog1_b)
    P.sync.dma(rlog_out, rlog1[:].rearrange("p n e -> p (n e)"),
               reads=[rlog1_b], writes=[rlog_out_b])
    emit_l1_moe(P, ctx, wts, None, None, (xnt1, xnt1_b), None, seg=SEG1,
                rlog=(rlog1, rlog1_b), stage="disp1", disp_io=(disp1, disp1_b))
    emit_l1_attn(P, ctx, wts, (xnt1, xnt1_b), attn1, attn1_b)
    P.final_barrier()


def _build_D2(P, shapes):
    xnt1 = P.din("xnt1", [T, D], BF16); xnt1_b = Buf()
    disp1i = P.din("disp1", [NSLOT1, D], BF16); disp1i_b = Buf()
    wts = _wts_din(P, shapes, ["L1_ffn_W1", "L1_ffn_W2", "L1_ffn_b1",
                               "L1_ffn_b2"])
    eout1 = P.dout("eout1", [NSLOT1, D], F32); eout1_b = Buf()
    ctx = make_ctx(P)
    rlog1, rlog1_b = _load_rlog(P, "rlog1")
    disp1 = P.dscratch("l1_dispS", [NSLOT1, D], BF16); disp1_b = Buf()
    emit_l1_moe(P, ctx, wts, None, None, (xnt1, xnt1_b), None, seg=SEG1,
                rlog=(rlog1, rlog1_b), stage="disp2exp",
                disp_io=(disp1, disp1_b), disp_in=(disp1i, disp1i_b),
                eout_io=(eout1, eout1_b))
    P.final_barrier()


def _build_D3(P, shapes):
    xnt1 = P.din("xnt1", [T, D], BF16); xnt1_b = Buf()
    eout1 = P.din("eout1", [NSLOT1, D], F32); eout1_b = Buf()
    wts = {}
    part1 = P.dout("part1", [T, D], F32); part1_b = Buf()
    ctx = make_ctx(P)
    rlog1, rlog1_b = _load_rlog(P, "rlog1")
    emit_l1_moe(P, ctx, wts, None, None, (xnt1, xnt1_b), None, seg=SEG1,
                rlog=(rlog1, rlog1_b), stage="comb1",
                eout_io=(eout1, eout1_b), part_io=(part1, part1_b))
    P.final_barrier()


def _build_D4(P, shapes):
    xnt1 = P.din("xnt1", [T, D], BF16); xnt1_b = Buf()
    eout1 = P.din("eout1", [NSLOT1, D], F32); eout1_b = Buf()
    part1 = P.din("part1", [T, D], F32); part1_b = Buf()
    wts = {}
    ffn1 = P.dout("ffn1", [T, D], F32); ffn1_b = Buf()
    ctx = make_ctx(P)
    rlog1, rlog1_b = _load_rlog(P, "rlog1")
    emit_l1_moe(P, ctx, wts, None, None, (xnt1, xnt1_b), (ffn1, ffn1_b),
                seg=SEG1, rlog=(rlog1, rlog1_b), stage="comb2",
                eout_io=(eout1, eout1_b), part_io=(part1, part1_b))
    P.final_barrier()


def _build_EF(P, shapes):
    xnt1 = P.din("xnt1", [T, D], BF16); xnt1_b = Buf()
    eout1 = P.din("eout1", [NSLOT1, D], F32); eout1_b = Buf()
    part1 = P.din("part1", [T, D], F32); part1_b = Buf()
    xr0 = P.din("xr0", [T, D], F32)
    attn1 = P.din("attn1", [T, D], F32)
    wts = _wts_din(P, shapes, ["L2_ffn_controller", "L2_attn_Wq", "L2_attn_Wk",
                               "L2_attn_Wv", "L2_attn_Wo", "L2_attn_gn_weight",
                               "L2_attn_gn_bias", "L2_ffn_f1", "L2_ffn_f2",
                               "L2_ffn_bias"] + K_L3)
    y_out = P.dout("y", [T, D], F32)
    ctx = make_ctx(P)
    rlog1, rlog1_b = _load_rlog(P, "rlog1")
    ffn1 = P.dscratch("ffn1", [T, D], F32); ffn1_b = Buf()
    emit_l1_moe(P, ctx, wts, None, None, (xnt1, xnt1_b), (ffn1, ffn1_b),
                seg=SEG1, rlog=(rlog1, rlog1_b), stage="comb2",
                eout_io=(eout1, eout1_b), part_io=(part1, part1_b))
    xn_fm = P.sb([128, 8, T], BF16, name="xn_fm"); xn_fm_buf = Buf("xn_fm")
    ctl_sb = P.sb([128, 8, 64], BF16, name="ctlbf"); ctl_buf = Buf()
    P.gpsimd.dma(ctl_sb[:], wts["L2_ffn_controller"]
                 .rearrange("(c p) e s -> p c (e s)", p=128), writes=[ctl_buf])
    xr1 = P.dscratch("x_res1", [T, D], F32); xr1_b = Buf()
    xnt2 = P.dscratch("xn_tok2", [T, D], BF16); xnt2_b = Buf()
    l2log = P.dscratch("l2log_h", [64, T], F32); l2log_b = Buf()
    with P.phase("pre2"):
        emit_pre_pass(P, ctx, (xr0, Buf()), [(attn1, Buf()), (ffn1, ffn1_b)],
                      (xr1, xr1_b), (xnt2, xnt2_b), xn_fm, xn_fm_buf)
    emit_l2log_fm(P, ctx, ctl_sb, ctl_buf, xn_fm, xn_fm_buf, l2log, l2log_b)
    attn2 = P.dscratch("attn2", [T, D], F32); attn2_b = Buf()
    emit_l2_attn(P, ctx, wts, xn_fm, xn_fm_buf, attn2, attn2_b)
    ffn2 = P.dscratch("ffn2", [T, D], F32); ffn2_b = Buf()
    emit_l2_ffn(P, ctx, wts, (xnt2, xnt2_b), (ffn2, ffn2_b), l2log, l2log_b)
    xr2 = P.dscratch("x_res2", [T, D], F32); xr2_b = Buf()
    xnt3 = P.dscratch("xn_tok3", [T, D], BF16); xnt3_b = Buf()
    with P.phase("pre3"):
        emit_pre_pass(P, ctx, (xr1, xr1_b), [(attn2, attn2_b), (ffn2, ffn2_b)],
                      (xr2, xr2_b), (xnt3, xnt3_b), xn_fm, xn_fm_buf)
    attn3 = P.dscratch("attn3", [T, D], F32); attn3_b = Buf()
    ffn3 = P.dscratch("ffn3", [T, D], F32); ffn3_b = Buf()
    emit_l3_attn(P, ctx, wts, xn_fm, xn_fm_buf, (xnt3, xnt3_b), attn3, attn3_b)
    emit_l3_ffn(P, ctx, wts, xn_fm, xn_fm_buf, ffn3, ffn3_b)
    with P.phase("final"):
        emit_pre_pass(P, ctx, (xr2, xr2_b), [(attn3, attn3_b), (ffn3, ffn3_b)],
                      (y_out, Buf()), None, None, None, final=True)
    P.final_barrier()


_LAUNCHES = (
    (_build_A, ("x_in",),
     ["L0_ffn_Wr", "L0_attn_wq", "L0_attn_wk", "L0_attn_wv", "L0_attn_wo"]),
    (_build_B2, ("xnt0", "rlog0", "disp0", "coremask"),
     ["L0_ffn_W1", "L0_ffn_W2"]),
    (_build_C, ("x_in", "attn0", "ffn0"), ["L1_ffn_Wg"] + K_L1ATTN),
    (_build_D2, ("xnt1", "rlog1", "disp1"),
     ["L1_ffn_W1", "L1_ffn_W2", "L1_ffn_b1", "L1_ffn_b2"]),
    (_build_D3, ("xnt1", "rlog1", "eout1"), []),
    (_build_EF, ("xnt1", "rlog1", "eout1", "part1", "xr0", "attn1"),
     ["L2_ffn_controller", "L2_attn_Wq", "L2_attn_Wk", "L2_attn_Wv",
      "L2_attn_Wo", "L2_attn_gn_weight", "L2_attn_gn_bias", "L2_ffn_f1",
      "L2_ffn_f2", "L2_ffn_bias"] + K_L3),
)


def kernel(**inputs):
    from concourse.bass_utils import run_bass_kernel_spmd
    x = np.ascontiguousarray(np.asarray(inputs["L0_attn_x"], dtype=np.float32))
    wkeys = [k for k in inputs if k != "L0_attn_x"]
    shapes = {k: np.asarray(inputs[k]).shape for k in wkeys}
    wmap = {k: np.ascontiguousarray(np.asarray(inputs[k], dtype=np.float32))
            for k in wkeys}
    cm = []
    for c in range(8):
        m = np.zeros((8, 1), dtype=np.dtype(ml_dtypes.bfloat16))
        m[:c] = 1
        cm.append(m)
    state = [{"x_in": x[c], "coremask": cm[c]} for c in range(8)]
    for build_fn, extra, kws in _LAUNCHES:
        P = Prog()
        with P.es:
            build_fn(P, shapes)
            P.finalize(sim=False)
            maps = []
            for c in range(8):
                m = {k: wmap[k] for k in kws}
                for k in extra:
                    m[k] = state[c][k]
                m.update(P._consts)
                maps.append(m)
            res = run_bass_kernel_spmd(P.nc, maps, core_ids=list(range(8)))
        for c in range(8):
            for k, v in res.results[c].items():
                state[c][k] = np.asarray(v)
    out = np.stack([np.asarray(state[c]["y"], dtype=np.float32)
                    for c in range(8)], axis=0)
    return out


# version 08 (final)
# speedup vs baseline: 1.1580x; 1.0069x over previous
import contextlib
import numpy as np
import concourse.bass as bass
import concourse.mybir as mybir
from concourse.bass_core_rust import IndirectOffsetOnAxis

F32 = mybir.dt.float32
BF16 = mybir.dt.bfloat16
I32 = mybir.dt.int32
I16 = mybir.dt.int16
from concourse.library_config import mlp as MLP_LIB
U32 = mybir.dt.uint32
AF = mybir.ActivationFunctionType
ALU = mybir.AluOpType
AX = mybir.AxisListType

T = 4096
D = 1024
NTILE = T // 128


class Buf:
    __slots__ = ("name", "writer", "readers")
    def __init__(self, name=""):
        self.name = name
        self.writer = None
        self.readers = []


class Eng:
    def __init__(self, P, name, has_q=False):
        self.P = P
        self.name = name
        self.n = 0
        self.qn = 0
        self.seen = {}
        self.items = []
        self.semkey = name
        self.qsemkey = name + "_q"

    def _wait(self, semkey, val):
        if semkey == self.semkey and val > self.n:
            return
        if self.seen.get(semkey, 0) >= val:
            return
        self.seen[semkey] = val
        self.items.append(("wait", semkey, val))

    def _pre(self, reads, writes):
        for r in reads:
            if r.writer is not None:
                self._wait(*r.writer)
        for w in writes:
            if w.writer is not None:
                self._wait(*w.writer)
            for rd in w.readers:
                self._wait(*rd)

    def _post(self, reads, writes, tok):
        for w in writes:
            w.writer = tok
            w.readers = []
        for r in reads:
            r.readers.append(tok)

    def op(self, fn, reads=(), writes=()):
        return self.ops([fn], reads, writes)

    def ops(self, fns, reads=(), writes=()):
        self._pre(reads, writes)
        self.n += 1
        k, nn = self.semkey, self.n
        def run(b, S, inc, fns=tuple(fns), k=k):
            inst = None
            for f in fns:
                inst = f(b)
            if inc:
                inst.then_inc(S[k], inc)
        self.items.append(("op", k, nn, run))
        tok = (k, nn)
        self._post(reads, writes, tok)
        return tok

    def dma(self, out, in_, reads=(), writes=(), **kw):
        self._pre(reads, writes)
        self.qn += 16
        k, nn = self.qsemkey, self.qn
        self.items.append(("dma", k, lambda b, S: b.dma_start(out=out, in_=in_, **kw).then_inc(S[k], 16)))
        tok = (k, nn)
        self._post(reads, writes, tok)
        return tok

    def dma_transpose(self, out, in_, reads=(), writes=()):
        self._pre(reads, writes)
        self.qn += 16
        k, nn = self.qsemkey, self.qn
        self.items.append(("dma", k, lambda b, S: b.dma_start_transpose(out, in_).then_inc(S[k], 16)))
        tok = (k, nn)
        self._post(reads, writes, tok)
        return tok

    def indirect(self, out, out_off, in_, in_off, reads=(), writes=(), bound=None):
        self._pre(reads, writes)
        self.qn += 16
        k, nn = self.qsemkey, self.qn
        import os as _os
        tag = None
        if _os.environ.get("MK_DBG"):
            import traceback as _tb
            tag = "".join(_tb.format_stack()[-4:-1])
        def _run(b, S, k=k, out=out, out_off=out_off, in_=in_, in_off=in_off,
                 bound=bound, tag=tag):
            try:
                return b.indirect_dma_start(
                    out, out_off, in_, in_off, bounds_check=bound,
                    oob_is_err=False).then_inc(S[k], 16)
            except Exception:
                if tag:
                    print("INDIRECT FAIL AT:\n" + tag, flush=True)
                    print("OUT:", repr(out), flush=True)
                    print("IN_:", repr(in_), flush=True)
                    print("OUT_OFF:", repr(out_off), flush=True)
                    print("IN_OFF:", repr(in_off), flush=True)
                    print("BOUND:", bound, flush=True)
                raise
        self.items.append(("dma", k, _run))
        tok = (k, nn)
        self._post(reads, writes, tok)
        return tok

    def load_lib(self, lib=None, reads=(), writes=()):
        if lib is None:
            lib = MLP_LIB
        self._pre(reads, writes)
        self.n += 1
        k, nn = self.semkey, self.n
        self.items.append(("op", k, nn, lambda b, S, inc: b.load_library(
            lib).then_inc(S[k], max(inc, 1))))
        tok = (k, nn)
        self._post(reads, writes, tok)
        return tok

    def gather_ant(self, out, in_, idxs, num_idxs, num_valid, elem_size,
                   elem_step=None, reads=(), writes=(), transpose=False,
                   queue_num=0):
        self._pre(reads, writes)
        self.qn += 16
        k, nn = self.qsemkey, self.qn
        self.items.append(("dma", k, lambda b, S: b.dma_gather(
            out, in_, idxs, num_idxs, num_valid, elem_size,
            elem_step=elem_step, transpose=transpose,
            queue_num=queue_num).then_inc(S[k], 16)))
        tok = (k, nn)
        self._post(reads, writes, tok)
        return tok

    def scatter_add_ant(self, out, in_, idxs, num_idxs, num_valid, elem_size,
                        elem_step=None, reads=(), writes=(), queue_num=0):
        self._pre(reads, writes)
        self.qn += 16
        k, nn = self.qsemkey, self.qn
        self.items.append(("dma", k, lambda b, S: b.dma_scatter_add(
            out, in_, idxs, num_idxs, num_valid, elem_size,
            elem_step=elem_step, queue_num=queue_num).then_inc(S[k], 16)))
        tok = (k, nn)
        self._post(reads, writes, tok)
        return tok

    def collective(self, kind, op, groups, ins, outs, reads=(), writes=()):
        self._pre(reads, writes)
        self.n += 1
        k, nn = self.semkey, self.n
        self.items.append(("op", k, nn, lambda b, S, inc: b.collective_compute(
            kind, op, replica_groups=groups, ins=ins, outs=outs).then_inc(S[k], max(inc, 1))))
        tok = (k, nn)
        self._post(reads, writes, tok)
        return tok

    def matmul_acc(self, psum_aps, pairs, reads=(), writes=(), flags=None):
        self._pre(reads, writes)
        self.n += 1
        k, nn = self.semkey, self.n
        n = len(pairs)
        if not isinstance(psum_aps, (list, tuple)):
            psum_aps = [psum_aps] * n
        if flags is None:
            flags = [(i == 0, i == n - 1) for i in range(n)]
        def run(b, S, inc, psum_aps=tuple(psum_aps), pairs=tuple(pairs), flags=tuple(flags)):
            inst = None
            for pa, (lhsT, rhs), (st, sp) in zip(psum_aps, pairs, flags):
                inst = b.matmul(pa, lhsT, rhs, start=st, stop=sp)
            if inc:
                inst.then_inc(S[k], inc)
        self.items.append(("op", k, nn, run))
        tok = (k, nn)
        self._post(reads, writes, tok)
        return tok

    def transpose(self, out_psum, in_sbuf, ident, reads=(), writes=()):
        self._pre(reads, writes)
        self.n += 1
        k, nn = self.semkey, self.n
        self.items.append(("op", k, nn, lambda b, S, inc: b.transpose(out_psum, in_sbuf, ident).then_inc(S[k], inc) if inc else b.transpose(out_psum, in_sbuf, ident)))
        tok = (k, nn)
        self._post(reads, writes, tok)
        return tok


class Prog:
    def __init__(self, name="mk", dyn_dma=16384):
        self.nc = bass.Bass("TRN2", target_bir_lowering=False,
                            dynamic_dma_scratch_size=dyn_dma)
        self.es = contextlib.ExitStack()
        self._consts = {}
        self._sb_names = 0
        self.sync = Eng(self, "sync")
        self.vector = Eng(self, "vector")
        self.scalar = Eng(self, "scalar")
        self.tensor = Eng(self, "tensor")
        self.gpsimd = Eng(self, "gpsimd")
        self.sync.semkey = self.sync.qsemkey

    def din(self, name, shape, dtype=F32):
        return self.nc.dram_tensor(name, list(shape), dtype, kind="ExternalInput").ap()

    def dout(self, name, shape, dtype=F32):
        return self.nc.dram_tensor(name, list(shape), dtype, kind="ExternalOutput").ap()

    def dscratch(self, name, shape, dtype=F32):
        return self.nc.dram_tensor(name, list(shape), dtype, kind="Internal").ap()

    def const(self, name, arr):
        self._consts[name] = arr
        if arr.dtype == np.float32:
            dt = F32
        elif arr.dtype == np.int32:
            dt = I32
        else:
            import ml_dtypes
            assert arr.dtype == ml_dtypes.bfloat16, arr.dtype
            dt = BF16
        return self.nc.dram_tensor(name, list(arr.shape), dt, kind="ExternalInput").ap()

    def sb(self, shape, dtype=F32, name=None):
        if name is None:
            self._sb_names += 1
            name = f"sb{self._sb_names}"
        st = self._phase_stacks[-1] if getattr(self, "_phase_stacks", None) else self.es
        return st.enter_context(self.nc.sbuf_tensor(name, list(shape), dtype))

    def barrier(self):
        engines = (self.sync, self.vector, self.scalar, self.tensor, self.gpsimd)
        snap = [(e, e.semkey, e.n, e.qsemkey, e.qn) for e in engines]
        for e in engines:
            for (o, sk, n, qk, qn) in snap:
                if o is not e and n:
                    e._wait(sk, n)
                if qn:
                    e._wait(qk, qn)

    import contextlib as _ctxlib

    @_ctxlib.contextmanager
    def phase(self, name=""):
        import os
        if not hasattr(self, "_phase_stacks"):
            self._phase_stacks = []
        st = contextlib.ExitStack()
        self._phase_stacks.append(st)
        try:
            yield st
        finally:
            self.barrier()
            self._phase_stacks.pop()
            if os.environ.get("MK_NO_FREE"):
                self.es.enter_context(st.pop_all())
            else:
                st.close()

    def psum(self, shape, dtype=F32, name=None):
        if name is None:
            self._sb_names += 1
            name = f"ps{self._sb_names}"
        return self.es.enter_context(self.nc.psum_tensor(name, list(shape), dtype))

    def final_barrier(self):
        for e in (self.sync, self.vector, self.scalar, self.tensor, self.gpsimd):
            if e.qn:
                self.gpsimd.items.append(("wait", e.qsemkey, e.qn))
            if e.n and e is not self.gpsimd:
                self.gpsimd.items.append(("wait", e.semkey, e.n))

    def finalize(self, sim=False):
        import bisect, os
        if os.environ.get("MK_HW_DENSE"):
            sim = True
        dense_inc = bool(os.environ.get("MK_T_DENSEINC"))
        keepsame = True
        engines = (self.sync, self.vector, self.scalar, self.tensor, self.gpsimd)
        semnames = set()
        for e in engines:
            semnames.add(e.semkey)
            semnames.add(e.qsemkey)
        import os
        same_edges = True
        enames = {e.semkey for e in engines} - {e.qsemkey for e in engines}
        waited = {k: set() for k in enames}
        for e in engines:
            for it in e.items:
                if it[0] == "wait" and it[1] in enames and (same_edges or it[1] != e.semkey):
                    waited[it[1]].add(it[2])
        remap = {}
        for k, vs in waited.items():
            sv = sorted(vs)
            remap[k] = ({v: i + 1 for i, v in enumerate(sv)}, sv)
        bounds = {}
        for e in engines:
            q = e.qsemkey
            cum = 0
            bl = []
            for it in e.items:
                if it[0] == "dma" and it[1] == q:
                    cum += 16
                elif it[0] == "wait":
                    bl.append(cum)
            bl.append(cum)
            bounds[q] = sorted(set(bl))
        S = {n: self.es.enter_context(self.nc.semaphore(n)) for n in sorted(semnames)}
        block = self.es.enter_context(self.nc.Block())
        for eng, opener in ((self.sync, block.sync), (self.vector, block.vector),
                            (self.scalar, block.scalar), (self.tensor, block.tensor),
                            (self.gpsimd, block.gpsimd)):
            items = eng.items
            if not items:
                continue
            def body(b, items=items, eng=eng):
                cum = {}
                for it in items:
                    if it[0] == "wait":
                        _, k, v = it
                        if k in enames:
                            if sim:
                                b.wait_ge(S[k], v)
                            else:
                                if k == eng.semkey and not same_edges:
                                    continue
                                if dense_inc:
                                    b.wait_ge(S[k], v)
                                else:
                                    m, sv = remap[k]
                                    b.wait_ge(S[k], m[v])
                        else:
                            if sim:
                                b.wait_ge(S[k], v)
                            else:
                                bl = bounds.get(k)
                                if bl:
                                    i = bisect.bisect_left(bl, v)
                                    v = bl[i] if i < len(bl) else bl[-1]
                                b.wait_ge(S[k], v)
                    elif it[0] == "dma":
                        it[-1](b, S)
                        if sim or os.environ.get("MK_DMA_FENCE"):
                            k = it[1]
                            cum[k] = cum.get(k, 0) + 16
                            b.wait_ge(S[k], cum[k])
                    else:
                        _, k, nn, run = it
                        if sim or dense_inc:
                            run(b, S, 1)
                        else:
                            run(b, S, 1 if nn in waited[k] else 0)
            opener(body)
        return S




import numpy as np
import ml_dtypes


BF = ml_dtypes.bfloat16


def host_constants():
    C = {}
    C["ident_bf"] = np.eye(128, dtype=BF)
    C["ident_f32"] = np.eye(128, dtype=np.float32)
    inv = 1.0 / 10000.0 ** (np.arange(0, 64, 2, dtype=np.float32) / 64)
    fr = np.outer(inv, np.arange(T, dtype=np.float32))
    C["rope_cos"] = np.cos(fr).astype(BF)
    C["rope_sin"] = np.sin(fr).astype(BF)
    tk = np.arange(128)[:, None]; tq = np.arange(128)[None, :]
    C["mask_diag"] = (tk <= tq).astype(BF)
    q = np.arange(128)[:, None]; p = np.arange(128)[None, :]
    C["tri_strict"] = (q < p).astype(BF)
    C["ones_col"] = np.ones((128, 1), dtype=BF)
    C["ones_row"] = np.ones((1, 128), dtype=BF)
    C["iota_tok"] = (np.arange(128)[:, None] + 128.0 * np.arange(NTILE)[None, :]).astype(np.float32)
    C["iota8"] = np.broadcast_to(np.arange(8, dtype=np.float32)[None, :], (128, 8)).copy()
    C["eps_cols"] = np.array([[1e-5, 1e-3, -8.0, 0.0]], np.float32).repeat(128, 0)
    kc = np.arange(128)[:, None]; qq = np.arange(64)[None, :]
    C["mask_bucket"] = ((kc < 64) | (kc - 64 <= qq)).astype(BF)
    C["mask_bucket0"] = ((kc < 64) & (kc <= qq)).astype(BF)
    import math
    gam = (1.0 - np.exp(np.linspace(math.log(1 / 32), math.log(1 / 512), 8))).astype(np.float64)
    pp = np.arange(128)[:, None]
    C["gamma_pow"] = (gam[None, :] ** pp).astype(np.float32)
    C["gamma_npow"] = (gam[None, :] ** (-pp)).astype(np.float32)
    C["tie64"] = np.broadcast_to(np.linspace(0, 1e-6, 16, dtype=np.float32)[None, :],
                                 (64, 16)).copy()
    bd = np.zeros((128, 8), dtype=BF)
    for tt_ in range(128):
        bd[tt_, tt_ // 16] = 1
    C["blockdiag16"] = bd
    t32 = np.arange(32)[:, None]; s32 = np.arange(32)[None, :]
    C["mask_strict32"] = (s32 < t32).astype(BF)
    hs = np.zeros((2, 128), dtype=BF)
    hs[0, 0:64] = 1; hs[1, 64:128] = 1
    C["headsel"] = hs
    return C


class PsumTile:
    def __init__(self, ap, name):
        self.ap = ap
        self.buf = Buf(name)


def make_ctx(P):
    ctx = {}
    consts = host_constants()
    ctx["c"] = {}
    for k, v in consts.items():
        ap = P.const("c_" + k, v)
        sb = P.sb(list(v.shape), {np.dtype(np.float32): F32}.get(v.dtype, BF16), name="sb_" + k)
        P.sync.dma(sb[:], ap)
        ctx["c"][k] = (sb, Buf("c_" + k))
        ctx["c"][k][1].writer = (P.sync.qsemkey, P.sync.qn)
    ctx["pf"] = [PsumTile(P.psum([128, 512], F32, name=f"psf{i}"), f"pf{i}") for i in range(6)]
    ctx["pb"] = [PsumTile(P.psum([128, 1024], BF16, name=f"psb{i}"), f"pb{i}") for i in range(2)]
    return ctx


def cref(ctx, name):
    return ctx["c"][name]


class WeightFM:
    def __init__(self, P, name, d_in, d_out, alloc=True):
        self.d_in, self.d_out = d_in, d_out
        self.nc_in = d_in // 128
        if alloc:
            self.sb = P.sb([128, self.nc_in, d_out], BF16, name=name)
        self.buf = Buf(name)

    def load(self, P, src_ap):
        P.gpsimd.dma(self.sb[:], src_ap.rearrange("(c p) o -> p c o", p=128),
                     writes=[self.buf])
        return self

    def lhsT(self, dc, o0, o1):
        return self.sb[:, dc, o0:o1]


def load_weight_T_cols(P, name, src_ap, d_out, d_in, c0, c1):
    assert c0 % 128 == 0 and (c1 - c0) == 128
    tr = WeightFM(P, name, d_in, 128)
    with P.phase(name + "_ld"):
        nat = P.sb([128, d_in], BF16, name=name + "_nat"); nat_b = Buf()
        P.gpsimd.dma(nat[:], src_ap[c0:c1, :], writes=[nat_b])
        for dc in range(d_in // 128):
            P.sync.dma_transpose(tr.sb[:, dc, 0:128],
                                 nat[:, dc * 128:(dc + 1) * 128],
                                 reads=[nat_b], writes=[tr.buf])
    return tr


def load_weight_T(P, name, src_ap, d_out, d_in):
    tr = WeightFM(P, name, d_in, d_out)
    with P.phase(name + "_ld"):
        nat = WeightFM(P, name + "_nat", d_out, d_in).load(P, src_ap)
        for oc in range(d_out // 128):
            for dc in range(d_in // 128):
                P.sync.dma_transpose(tr.sb[:, dc, oc * 128:(oc + 1) * 128],
                                     nat.sb[:, oc, dc * 128:(dc + 1) * 128],
                                     reads=[nat.buf], writes=[tr.buf])
    return tr


def emit_proj_fm(P, ctx, w, xn_fm_sb, xn_fm_buf, out_cb, o_chunks, t_tiles=None,
                 n_t=512, extra_reads=()):
    tt_list = t_tiles if t_tiles is not None else range(T // n_t)
    slot = 0
    for oc in o_chunks:
        for tt in tt_list:
            pt = ctx["pf"][slot % 2]
            slot += 1
            t0 = tt * n_t
            pairs = [(w.lhsT(dc, oc * 128, (oc + 1) * 128),
                      xn_fm_sb[:, dc, t0:t0 + n_t]) for dc in range(w.nc_in)]
            P.tensor.matmul_acc(pt.ap[:, 0:n_t], pairs,
                                reads=[w.buf, xn_fm_buf] + list(extra_reads),
                                writes=[pt.buf])
            out_cb(P, ctx, oc, tt, pt)


def emit_pre_pass(P, ctx, x_src, adds, out_x, xn_tok, xn_fm_sb, xn_fm_buf,
                  router_hook=None, eps=1e-5, final=False, dbg=None,
                  xn_tok2=None):
    v, s, t, sy = P.vector, P.scalar, P.tensor, P.sync
    x_src, x_src_b = x_src
    adds_b = [a[1] for a in adds]; adds = [a[0] for a in adds]
    out_x, out_x_b = out_x if out_x is not None else (None, None)
    xn_tok, xn_tok_b = xn_tok if xn_tok is not None else (None, None)
    xn_tok2, xn_tok2_b = xn_tok2 if xn_tok2 is not None else (None, None)
    idf_sb, idf_buf = cref(ctx, "ident_f32")
    xt = [P.sb([128, D], F32) for _ in range(2)]
    at = [P.sb([128, D], F32) for _ in range(2)]
    bt = [P.sb([128, D], F32) for _ in range(2)]
    xnb = [P.sb([128, D], BF16) for _ in range(2)]
    xnf = [P.sb([128, D], F32) for _ in range(2)]
    sq = [P.sb([128, D], F32) for _ in range(2)]
    fchunk = [P.sb([128, 128], F32) for _ in range(2)]
    stat = [[P.sb([128, 1], F32) for _ in range(4)] for _ in range(2)]
    xtb = [Buf() for _ in range(2)]; atb = [Buf() for _ in range(2)]
    btb = [Buf() for _ in range(2)]; xnbb = [Buf() for _ in range(2)]
    xnfb = [Buf() for _ in range(2)]; fcb = [Buf() for _ in range(2)]
    sqb = [Buf() for _ in range(2)]
    statb = [Buf() for _ in range(2)]
    for i in range(NTILE):
        b = i % 2
        r = slice(i * 128, (i + 1) * 128)
        sy.dma(xt[b][:], x_src[r, :], reads=[x_src_b], writes=[xtb[b]])
        for j, ap in enumerate(adds):
            dst = (at, bt)[j]; dstb = (atb, btb)[j]
            sy.dma(dst[b][:], ap[r, :], reads=[adds_b[j]], writes=[dstb[b]])
        rd = [xtb[b]] + [(atb, btb)[j][b] for j in range(len(adds))]
        if adds:
            ops = [lambda bb, b=b: bb.tensor_add(xt[b][:], xt[b][:], at[b][:])]
            if len(adds) > 1:
                ops.append(lambda bb, b=b: bb.tensor_add(xt[b][:], xt[b][:], bt[b][:]))
            v.ops(ops, reads=rd, writes=[xtb[b]])
        if out_x is not None:
            sy.dma(out_x[r, :], xt[b][:], reads=[xtb[b]], writes=[out_x_b])
        if final:
            continue
        v.op(lambda bb, b=b: bb.tensor_reduce(stat[b][0][:], xt[b][:], AX.X, ALU.add),
             reads=[xtb[b]], writes=[statb[b]])
        v.op(lambda bb, b=b: bb.tensor_scalar_mul(stat[b][0][:], stat[b][0][:], 1.0 / D),
             reads=[statb[b]], writes=[statb[b]])
        v.op(lambda bb, b=b: bb.tensor_scalar(xnf[b][:], xt[b][:], stat[b][0][:], None,
                                              ALU.subtract),
             reads=[xtb[b], statb[b]], writes=[xnfb[b]])
        s.op(lambda bb, b=b: bb.activation(sq[b][:], xnf[b][:], AF.Square),
             reads=[xnfb[b]], writes=[sqb[b]])
        v.op(lambda bb, b=b: bb.tensor_reduce(stat[b][1][:], sq[b][:], AX.X, ALU.add),
             reads=[sqb[b]], writes=[statb[b]])
        epssb, epsbuf = cref(ctx, "eps_cols")
        epscol = epssb[:, (0 if eps == 1e-5 else 1):(1 if eps == 1e-5 else 2)]
        s.op(lambda bb, b=b: bb.activation(stat[b][2][:], stat[b][1][:], AF.Ln,
                                           scale=1.0 / D, bias=epscol),
             reads=[statb[b], epsbuf], writes=[statb[b]])
        s.op(lambda bb, b=b: bb.activation(stat[b][3][:], stat[b][2][:], AF.Exp,
                                           scale=-0.5),
             reads=[statb[b]], writes=[statb[b]])
        v.op(lambda bb, b=b: bb.tensor_scalar(xnf[b][:], xnf[b][:], stat[b][3][:], None,
                                              ALU.mult),
             reads=[statb[b]], writes=[xnfb[b]])
        v.op(lambda bb, b=b: bb.tensor_copy(xnb[b][:], xnf[b][:]),
             reads=[xnfb[b]], writes=[xnbb[b]])
        if xn_tok is not None:
            sy.dma(xn_tok[r, :], xnb[b][:], reads=[xnbb[b]], writes=[xn_tok_b])
        if xn_tok2 is not None:
            sy.dma(xn_tok2[r, :], xnb[b][:], reads=[xnbb[b]], writes=[xn_tok2_b])
        if dbg is not None and i == 0:
            for jj in range(4):
                sy.dma(dbg[:, jj:jj+1], stat[b][jj][:], reads=[statb[b]], allow_slow_non_contiguous=True)
        for dc in range(8):
            pt = ctx["pf"][2 + (dc % 2)]
            P.tensor.transpose(pt.ap[:, 0:128], xnf[b][:, dc * 128:(dc + 1) * 128],
                               idf_sb[:], reads=[xnfb[b], idf_buf], writes=[pt.buf])
            v.op(lambda bb, b=b, dc=dc, pt=pt, i=i: bb.tensor_copy(
                    xn_fm_sb[:, dc, i * 128:(i + 1) * 128], pt.ap[:, 0:128]),
                 reads=[pt.buf], writes=[xn_fm_buf])
            if router_hook is not None:
                s.op(lambda bb, b=b, pt=pt: bb.copy(fchunk[b][:], pt.ap[:, 0:128]),
                     reads=[pt.buf], writes=[fcb[b]])
                router_hook(P, ctx, i, dc, fchunk[b], fcb[b])


def emit_l0_router_hook(P, ctx, wr_sb, wr_buf, rlog_sb, rlog_buf):
    def hook(P, ctx, i, dc, fchunk, fcb):
        pt = ctx["pf"][5]
        P.tensor.matmul_acc(pt.ap[0:128, 0:8],
                            [(fchunk[:], wr_sb[:, dc, :])],
                            reads=[fcb, wr_buf], writes=[pt.buf])
        if dc == 0:
            P.vector.op(lambda bb, pt=pt, i=i: bb.tensor_copy(
                rlog_sb[:, i, :], pt.ap[0:128, 0:8]),
                reads=[pt.buf], writes=[rlog_buf])
        else:
            P.vector.op(lambda bb, pt=pt, i=i: bb.tensor_add(
                rlog_sb[:, i, :], rlog_sb[:, i, :], pt.ap[0:128, 0:8]),
                reads=[pt.buf, rlog_buf], writes=[rlog_buf])
    return hook


def emit_l0_moe(P, ctx, wts, xn_tok, ffn_out, rlog_sb, rlog_buf,
                collective=True, cap=5120, seg=768, warena=None,
                stage="all", disp_io=None):
    v, s, t, sy, g = P.vector, P.scalar, P.tensor, P.sync, P.gpsimd
    NT = NTILE
    Tt = NT * 128
    E = 8
    xn_tok, xn_tok_b = xn_tok
    ffn_out, ffn_out_b = ffn_out if ffn_out is not None else (None, None)
    idb_sb, idb_buf = cref(ctx, "ident_bf")
    tri_sb, tri_buf = cref(ctx, "tri_strict")
    onesr_sb, onesr_buf = cref(ctx, "ones_row")
    onesc_sb, onesc_buf = cref(ctx, "ones_col")
    iota8_sb, iota8_buf = cref(ctx, "iota8")
    iotat_sb, iotat_buf = cref(ctx, "iota_tok")
    pf = ctx["pf"]; pb = ctx["pb"]
    nslot = E * seg

    eout = P.dscratch("l0_eout", [nslot, D], F32)
    eout_b = Buf()
    cc_in = P.dscratch("l0_ccin", [1, 8], F32)
    cc_out = P.dscratch("l0_ccout", [8, 8], F32)

    probs = P.sb([128, 8], F32); probs_b = Buf()
    mcol = P.sb([128, 1], F32); mcol_b = Buf()
    scol = P.sb([128, 1], F32); scol_b = Buf()
    eqt = P.sb([128, 8], F32); eqt_b = Buf()
    pmaxm = P.sb([128, NT], F32); pmaxm_b = Buf()
    idxm = P.sb([128, NT], F32); idxm_b = Buf()
    inds = P.sb([128, E * NT], BF16); inds_b = Buf()
    indsf = P.sb([128, E * NT], F32); indsf_b = Buf()
    for i in range(NT):
        v.op(lambda bb, i=i: bb.reduce_max(mcol[:], rlog_sb[:, i, :], AX.X),
             reads=[rlog_buf], writes=[mcol_b])
        s.op(lambda bb, i=i: bb.activation(probs[:], rlog_sb[:, i, :], AF.Exp,
                                           scale=-1.0, bias=mcol[:]),
             reads=[rlog_buf, mcol_b], writes=[probs_b])
        v.op(lambda bb, i=i: bb.tensor_scalar(probs[:], rlog_sb[:, i, :], mcol[:], None,
                                              ALU.subtract),
             reads=[rlog_buf, mcol_b], writes=[probs_b])
        s.op(lambda bb: bb.activation(probs[:], probs[:], AF.Exp),
             reads=[probs_b], writes=[probs_b])
        v.op(lambda bb: bb.tensor_reduce(scol[:], probs[:], AX.X, ALU.add),
             reads=[probs_b], writes=[scol_b])
        v.op(lambda bb: bb.reciprocal(scol[:], scol[:]),
             reads=[scol_b], writes=[scol_b])
        v.op(lambda bb: bb.tensor_scalar(probs[:], probs[:], scol[:], None, ALU.mult),
             reads=[probs_b, scol_b], writes=[probs_b])
        v.op(lambda bb, i=i: bb.reduce_max(pmaxm[:, i:i + 1], probs[:], AX.X),
             reads=[probs_b], writes=[pmaxm_b])
        v.op(lambda bb, i=i: bb.tensor_scalar(eqt[:], probs[:], pmaxm[:, i:i + 1], None,
                                              ALU.is_ge),
             reads=[probs_b, pmaxm_b], writes=[eqt_b])
        v.op(lambda bb, i=i: bb.tensor_copy(
            inds[:].rearrange("p (e j) -> p e j", e=E)[:, :, i], eqt[:]),
            reads=[eqt_b], writes=[inds_b])
        v.op(lambda bb, i=i: bb.tensor_copy(
            indsf[:].rearrange("p (e j) -> p e j", e=E)[:, :, i], eqt[:]),
            reads=[eqt_b], writes=[indsf_b])
        v.op(lambda bb: bb.tensor_mul(eqt[:], eqt[:], iota8_sb[:]),
             reads=[eqt_b, iota8_buf], writes=[eqt_b])
        v.op(lambda bb, i=i: bb.tensor_reduce(idxm[:, i:i + 1], eqt[:], AX.X, ALU.add),
             reads=[eqt_b], writes=[idxm_b])
    colsum_p = pf[0]
    t.matmul_acc(colsum_p.ap[0:1, 0:E * NT], [(onesc_sb[:], inds[:])],
                 reads=[inds_b, onesc_buf], writes=[colsum_p.buf])
    colrow = P.sb([1, E * NT], F32); colrow_b = Buf()
    s.op(lambda bb: bb.copy(colrow[:], colsum_p.ap[0:1, 0:E * NT]),
         reads=[colsum_p.buf], writes=[colrow_b])
    excl = P.sb([1, E, NT + 1], F32); excl_b = Buf()
    v.op(lambda bb: bb.memset(excl[:, :, 0:1], 0.0), writes=[excl_b])
    zrow = P.sb([1, NT], F32); zrow_b = Buf()
    v.op(lambda bb: bb.memset(zrow[:], 0.0), writes=[zrow_b])
    for e in range(E):
        v.op(lambda bb, e=e: bb.tensor_tensor_scan(
            excl[:, e, 1:NT + 1], colrow[:, e * NT:(e + 1) * NT], zrow[:],
            0.0, ALU.add, ALU.add),
            reads=[colrow_b, zrow_b], writes=[excl_b])
    exclb = P.sb([1, E * NT], BF16); exclb_b = Buf()
    v.op(lambda bb: bb.tensor_copy(exclb[:].rearrange("p (e j) -> p e j", e=E),
                                   excl[:, :, 0:NT]),
         reads=[excl_b], writes=[exclb_b])
    exhf = P.sb([1, E * NT], F32); exhf_b = Buf()
    v.op(lambda bb: bb.tensor_copy(exhf[:], exclb[:]),
         reads=[exclb_b], writes=[exhf_b])
    exlo = P.sb([1, E * NT], F32); exlo_b = Buf()
    v.op(lambda bb: bb.tensor_sub(exlo[:].rearrange("p (e j) -> p e j", e=E),
                                  excl[:, :, 0:NT],
                                  exhf[:].rearrange("p (e j) -> p e j", e=E)),
         reads=[excl_b, exhf_b], writes=[exlo_b])
    exclo = P.sb([1, E * NT], BF16); exclo_b = Buf()
    v.op(lambda bb: bb.tensor_copy(exclo[:], exlo[:]),
         reads=[exlo_b], writes=[exclo_b])
    counts = P.sb([1, 8], F32); counts_b = Buf()
    v.op(lambda bb: bb.tensor_copy(
        counts[:].rearrange("p (e o) -> p e o", o=1), excl[:, :, NT:NT + 1]),
        reads=[excl_b], writes=[counts_b])
    cpfxr = None
    if collective:
        cmask_ap = P.din("coremask", [8, 1], BF16)
        cmask_sb = P.sb([8, 1], BF16); cmask_b = Buf()
        sy.dma(cmask_sb[:], cmask_ap, writes=[cmask_b])
        ccin_b = Buf(); ccout_b = Buf()
        g.dma(cc_in, counts[:], reads=[counts_b], writes=[ccin_b])
        g.collective("AllGather", ALU.bypass, [[0, 1, 2, 3, 4, 5, 6, 7]],
                     [cc_in.opt()], [cc_out.opt()], reads=[ccin_b], writes=[ccout_b])
        ag = P.sb([8, 8], F32); ag_b = Buf()
        g.dma(ag[:], cc_out, reads=[ccout_b], writes=[ag_b])
        agb = P.sb([8, 8], BF16); agb_b = Buf()
        v.op(lambda bb: bb.tensor_copy(agb[:], ag[:]), reads=[ag_b], writes=[agb_b])
        cpfx_p = pf[2]
        t.matmul_acc(cpfx_p.ap[0:1, 0:8], [(cmask_sb[:], agb[:])],
                     reads=[cmask_b, agb_b], writes=[cpfx_p.buf])
        cpfx = P.sb([1, 8], F32); cpfx_b = Buf()
        s.op(lambda bb: bb.copy(cpfx[:], cpfx_p.ap[0:1, 0:8]),
             reads=[cpfx_p.buf], writes=[cpfx_b])
        cpfxr = P.sb([1, E * NT], BF16); cpfxr_b = Buf()
        v.op(lambda bb: bb.tensor_copy(
            cpfxr[:].rearrange("p (e j) -> p e j", e=E),
            cpfx[:].rearrange("p (e o) -> p e o", o=1).broadcast_to([1, E, NT])),
            reads=[cpfx_b], writes=[cpfxr_b])
    Cp = pf[1]
    lpairs = [(tri_sb[:], inds[:]), (onesr_sb[:, 0:128], exclb[:]),
              (onesr_sb[:, 0:128], exclo[:])]
    lreads = [inds_b, tri_buf, exclb_b, exclo_b, onesr_buf]
    lflags = [(True, False), (False, False), (False, True)]
    t.matmul_acc(Cp.ap[:, 0:E * NT], lpairs, reads=lreads, writes=[Cp.buf],
                 flags=lflags)
    tmp = P.sb([128, E * NT], F32); tmp_b = Buf()
    lpos = P.sb([128, NT], F32); lpos_b = Buf()
    v.op(lambda bb: bb.tensor_tensor(tmp[:], Cp.ap[:, 0:E * NT], indsf[:], ALU.mult),
         reads=[Cp.buf, indsf_b], writes=[tmp_b])
    v.op(lambda bb: bb.tensor_reduce(
        lpos[:], tmp[:].rearrange("p (e j) -> p j e", e=E), AX.X, ALU.add),
        reads=[tmp_b], writes=[lpos_b])
    pos = P.sb([128, NT], F32); pos_b = Buf()
    if collective:
        Cp2 = pf[3]
        t.matmul_acc(Cp2.ap[:, 0:E * NT],
                     [(tri_sb[:], inds[:]), (onesr_sb[:, 0:128], exclb[:]),
                      (onesr_sb[:, 0:128], exclo[:]),
                      (onesr_sb[:, 0:128], cpfxr[:])],
                     reads=[inds_b, tri_buf, exclb_b, exclo_b, onesr_buf,
                            cpfxr_b],
                     writes=[Cp2.buf],
                     flags=[(True, False), (False, False), (False, False),
                            (False, True)])
        v.op(lambda bb: bb.tensor_tensor(tmp[:], Cp2.ap[:, 0:E * NT], indsf[:],
                                         ALU.mult),
             reads=[Cp2.buf, indsf_b], writes=[tmp_b])
    else:
        v.op(lambda bb: bb.tensor_tensor(tmp[:], Cp.ap[:, 0:E * NT], indsf[:],
                                         ALU.mult),
             reads=[Cp.buf, indsf_b], writes=[tmp_b])
    v.op(lambda bb: bb.tensor_reduce(
        pos[:], tmp[:].rearrange("p (e j) -> p j e", e=E), AX.X, ALU.add),
        reads=[tmp_b], writes=[pos_b])
    kept = P.sb([128, NT], F32); kept_b = Buf()
    nkept = P.sb([128, NT], F32); nkept_b = Buf()
    factor = P.sb([128, NT], F32); factor_b = Buf()
    slot = P.sb([128, NT], F32); slot_b = Buf()
    slot_i = P.sb([128, NT], I32); slot_i_b = Buf()
    gidx = P.sb([128, NT], F32); gidx_b = Buf()
    gidx_i = P.sb([128, NT], I32); gidx_i_b = Buf()
    tokid_i = P.sb([128, NT], I32); tokid_i_b = Buf()
    v.op(lambda bb: bb.tensor_single_scalar(kept[:], pos[:], float(cap), ALU.is_lt),
         reads=[pos_b], writes=[kept_b])
    v.op(lambda bb: bb.tensor_scalar(nkept[:], kept[:], -1.0, -1.0, ALU.mult,
                                     ALU.subtract),
         reads=[kept_b], writes=[nkept_b])
    v.op(lambda bb: bb.tensor_mul(factor[:], kept[:], pmaxm[:]),
         reads=[kept_b, pmaxm_b], writes=[factor_b])
    v.op(lambda bb: bb.tensor_add(factor[:], factor[:], nkept[:]),
         reads=[factor_b, nkept_b], writes=[factor_b])
    v.op(lambda bb: bb.tensor_single_scalar(slot[:], idxm[:], float(seg), ALU.mult),
         reads=[idxm_b], writes=[slot_b])
    v.op(lambda bb: bb.tensor_add(slot[:], slot[:], lpos[:]),
         reads=[slot_b, lpos_b], writes=[slot_b])
    ovf = P.sb([128, NT], F32); ovf_b = Buf()
    v.op(lambda bb: bb.tensor_single_scalar(ovf[:], lpos[:], float(seg), ALU.is_ge),
         reads=[lpos_b], writes=[ovf_b])
    v.op(lambda bb: bb.scalar_tensor_tensor(slot[:], ovf[:], 1e9, slot[:],
                                            ALU.mult, ALU.add),
         reads=[ovf_b, slot_b], writes=[slot_b])
    v.op(lambda bb: bb.tensor_copy(slot_i[:], slot[:]),
         reads=[slot_b], writes=[slot_i_b])
    v.op(lambda bb: bb.scalar_tensor_tensor(gidx[:], nkept[:], 1e9, slot[:],
                                            ALU.mult, ALU.add),
         reads=[nkept_b, slot_b], writes=[gidx_b])
    v.op(lambda bb: bb.tensor_copy(gidx_i[:], gidx[:]),
         reads=[gidx_b], writes=[gidx_i_b])
    v.op(lambda bb: bb.tensor_copy(tokid_i[:], iotat_sb[:, 0:NT]),
         reads=[iotat_buf], writes=[tokid_i_b])
    disp_ap, disp_b = disp_io
    if stage in ("all", "disp"):
        xtile = [P.sb([128, D], BF16) for _ in range(2)]
        xtb = [Buf() for _ in range(2)]
        for j in range(NT):
            b2 = j % 2
            sy.dma(xtile[b2][:], xn_tok[j * 128:(j + 1) * 128, :],
                   reads=[xn_tok_b], writes=[xtb[b2]])
            g.indirect(disp_ap, IndirectOffsetOnAxis(slot_i[:, j:j + 1], 0),
                       xtile[b2][:], None, reads=[slot_i_b, xtb[b2]],
                       writes=[disp_b], bound=nslot - 1)
    if stage == "disp":
        return
    with P.phase("l0exp"):
        if warena is None:
            warena = P.sb([128, 32768], BF16, name="l0warena")
        w1v = warena[:].rearrange("p (c f) -> p c f", c=8)
        w2v = warena[:].rearrange("p (c f) -> p c f", c=32)
        warena_b = Buf()
        sub = min(384, seg)
        nsub = seg // sub
        nst = sub // 128
        harena = P.sb([128, 32, sub], BF16, name="l0harena")
        harena_b = Buf()
        dispfm = P.sb([128, 8, sub], BF16, name="l0disp")
        dispfm_b = Buf()
        smcol = P.sb([128, 1], I32); smcol_b = Buf()
        dtok = [P.sb([128, D], BF16) for _ in range(2)]
        dtok_b = [Buf() for _ in range(2)]
        yev = [P.sb([128, 512], F32) for _ in range(2)]
        yev_b = [Buf() for _ in range(2)]
        for e in range(E):
          for sh in range(nsub):
            s0 = e * seg + sh * sub
            g.dma(w1v, wts["L0_ffn_W1"][e].rearrange("(c p) f -> p c f", p=128),
                  writes=[warena_b])
            for st in range(nst):
                db = st % 2
                sy.dma(dtok[db][:], disp_ap[s0 + st * 128:s0 + (st + 1) * 128, :],
                       reads=[disp_b], writes=[dtok_b[db]])
                for dc in range(8):
                    pt = pb[dc % 2]
                    t.transpose(pt.ap[:, 0:128], dtok[db][:, dc * 128:(dc + 1) * 128],
                                idb_sb[:], reads=[dtok_b[db], idb_buf], writes=[pt.buf])
                    v.op(lambda bb, pt=pt, dc=dc, st=st: bb.tensor_copy(
                        dispfm[:, dc, st * 128:(st + 1) * 128], pt.ap[:, 0:128]),
                        reads=[pt.buf], writes=[dispfm_b])
            for fc in range(32):
                ptc = pf[fc % 2]
                pairs = [(w1v[:, dc, fc * 128:(fc + 1) * 128],
                          dispfm[:, dc, :]) for dc in range(8)]
                t.matmul_acc(ptc.ap[:, 0:sub], pairs,
                             reads=[warena_b, dispfm_b], writes=[ptc.buf])
                s.op(lambda bb, ptc=ptc, fc=fc: bb.activation(
                    harena[:, fc, :], ptc.ap[:, 0:sub], AF.Relu),
                    reads=[ptc.buf], writes=[harena_b])
            g.dma(w2v, wts["L0_ffn_W2"][e].rearrange("(c p) f -> p c f", p=128),
                  writes=[warena_b])
            for st in range(nst):
                for dh in range(2):
                    ptc = pf[2 + dh]
                    pairs = [(harena[:, fc, st * 128:(st + 1) * 128],
                              w2v[:, fc, dh * 512:(dh + 1) * 512]) for fc in range(32)]
                    t.matmul_acc(ptc.ap[:, 0:512], pairs,
                                 reads=[warena_b, harena_b], writes=[ptc.buf])
                    yb = dh
                    v.op(lambda bb, ptc=ptc, yb=yb: bb.tensor_copy(yev[yb][:],
                                                                   ptc.ap[:, 0:512]),
                         reads=[ptc.buf], writes=[yev_b[yb]])
                    sy.dma(eout[s0 + st * 128:s0 + (st + 1) * 128,
                                dh * 512:(dh + 1) * 512], yev[yb][:],
                           reads=[yev_b[yb]], writes=[eout_b])
    xnt = [P.sb([128, D], BF16) for _ in range(2)]
    xnt_b = [Buf() for _ in range(2)]
    geo = [P.sb([128, D], F32) for _ in range(2)]
    geo_b = [Buf() for _ in range(2)]
    out_t = [P.sb([128, D], F32) for _ in range(2)]
    out_t_b = [Buf() for _ in range(2)]
    for i in range(NT):
        b2 = i % 2
        sy.dma(xnt[b2][:], xn_tok[i * 128:(i + 1) * 128, :],
               reads=[xn_tok_b], writes=[xnt_b[b2]])
        v.op(lambda bb, b2=b2: bb.memset(geo[b2][:], 0.0), writes=[geo_b[b2]])
        g.indirect(geo[b2][:], None, eout, IndirectOffsetOnAxis(gidx_i[:, i:i + 1], 0),
                   reads=[gidx_i_b, eout_b], writes=[geo_b[b2]], bound=nslot - 1)
        v.op(lambda bb, b2=b2, i=i: bb.tensor_scalar(
            out_t[b2][:], geo[b2][:], factor[:, i:i + 1], None, ALU.mult),
            reads=[geo_b[b2], factor_b], writes=[out_t_b[b2]])
        v.op(lambda bb, b2=b2, i=i: bb.tensor_scalar(
            xnt[b2][:], xnt[b2][:], nkept[:, i:i + 1], None, ALU.mult),
            reads=[xnt_b[b2], nkept_b], writes=[xnt_b[b2]])
        v.op(lambda bb, b2=b2: bb.tensor_add(out_t[b2][:], out_t[b2][:], xnt[b2][:]),
             reads=[xnt_b[b2], out_t_b[b2]], writes=[out_t_b[b2]])
        sy.dma(ffn_out[i * 128:(i + 1) * 128, :], out_t[b2][:],
               reads=[out_t_b[b2]], writes=[ffn_out_b])


def emit_l0_attn(P, ctx, wts, xn_fm_sb, xn_fm_buf, attn_out, attn_out_b):
    v, s, t, sy, g = P.vector, P.scalar, P.tensor, P.sync, P.gpsimd
    NT = NTILE
    Tt = NT * 128
    nq = Tt // 512
    idf_sb, idf_buf = cref(ctx, "ident_f32")
    cos_sb, cos_buf = cref(ctx, "rope_cos")
    sin_sb, sin_buf = cref(ctx, "rope_sin")
    mask_sb, mask_buf = cref(ctx, "mask_diag")
    ones_sb, ones_buf = cref(ctx, "ones_row")
    epssb, epsbuf = cref(ctx, "eps_cols")
    pf = ctx["pf"]
    def _group(grp):
        with P.phase(f"l0attn{grp}"):
            qk = {}
            for nm in ("q", "k"):
                for hp in range(2):
                    qk[(nm, hp)] = (P.sb([128, Tt], BF16), Buf())
            vtok = P.sb([128, NT, 4 * 65], BF16)
            vtok_buf = Buf()
            att_ar = P.sb([128, 2, 512], BF16)
            att_ar_buf = Buf()
            for wkey, nm in (("L0_attn_wq", "q"), ("L0_attn_wk", "k")):
              with P.phase(f"l0{nm}p{grp}"):
                wT = load_weight_T(P, f"l0{nm}{grp}", wts[wkey], D, D)
                for hp in range(2):
                    oc0 = (grp * 4 + hp * 2) * 64
                    sb, bf = qk[(nm, hp)]
                    for tt in range(Tt // 512):
                        pt = pf[tt % 2]
                        pairs = [(wT.lhsT(dc, oc0, oc0 + 128),
                                  xn_fm_sb[:, dc, tt * 512:(tt + 1) * 512]) for dc in range(8)]
                        t.matmul_acc(pt.ap[:, 0:512], pairs,
                                     reads=[wT.buf, xn_fm_buf], writes=[pt.buf])
                        v.op(lambda bb, pt=pt, sb=sb, tt=tt: bb.tensor_copy(
                            sb[:, tt * 512:(tt + 1) * 512], pt.ap[:, 0:512]),
                            reads=[pt.buf], writes=[bf])
            vph = P.phase(f"l0vp{grp}"); vph.__enter__()
            wvT = load_weight_T(P, f"l0v{grp}", wts["L0_attn_wv"], D, D)
            for ti in range(NT):
                pt = pf[ti % 2]
                oc0 = grp * 256
                pairs = [(xn_fm_sb[:, dc, ti * 128:(ti + 1) * 128],
                          wvT.sb[:, dc, oc0:oc0 + 256]) for dc in range(8)]
                t.matmul_acc(pt.ap[:, 0:256], pairs,
                             reads=[wvT.buf, xn_fm_buf], writes=[pt.buf])
                v.op(lambda bb, pt=pt, ti=ti: bb.tensor_copy(
                    vtok[:, ti, 0:260].rearrange("p (h c) -> p h c", h=4)[:, :, 0:64],
                    pt.ap[:, 0:256].rearrange("p (h c) -> p h c", h=4)),
                    reads=[pt.buf], writes=[vtok_buf])
            v.op(lambda bb: bb.memset(
                vtok[:].rearrange("p n (h c) -> p n h c", h=4)[:, :, :, 64:65], 1.0),
                writes=[vtok_buf])
            vph.__exit__(None, None, None)
            rope_ph = P.phase(f"l0rope{grp}"); rope_ph.__enter__()
            rscr = P.sb([64, Tt], BF16); rscr_b = Buf()
            rope_t = [P.sb([32, Tt], BF16) for _ in range(3)]
            rope_tb = [Buf() for _ in range(3)]
            cosv = cos_sb[:, 0:Tt]; sinv = sin_sb[:, 0:Tt]
            x2s = P.sb([32, Tt], BF16); x2s_b = Buf()
            def rope_rows(eng, x1, bf):
                eng.op(lambda bb: bb.tensor_mul(rope_t[0][:], x1, cosv),
                       reads=[bf, cos_buf], writes=[rope_tb[0]])
                eng.op(lambda bb: bb.tensor_mul(rope_t[1][:], x1, sinv),
                       reads=[bf, sin_buf], writes=[rope_tb[1]])
                eng.op(lambda bb: bb.tensor_mul(rope_t[2][:], x2s[:], sinv),
                       reads=[x2s_b, sin_buf], writes=[rope_tb[2]])
                eng.op(lambda bb: bb.tensor_sub(x1, rope_t[0][:], rope_t[2][:]),
                       reads=[rope_tb[0], rope_tb[2]], writes=[bf])
                eng.op(lambda bb: bb.tensor_mul(rope_t[0][:], x2s[:], cosv),
                       reads=[x2s_b, cos_buf], writes=[rope_tb[0]])
                eng.op(lambda bb: bb.tensor_add(x2s[:], rope_t[1][:], rope_t[0][:]),
                       reads=[rope_tb[0], rope_tb[1]], writes=[x2s_b])
            for nm in ("q", "k"):
                for hp in range(2):
                    sb, bf = qk[(nm, hp)]
                    v.op(lambda bb, sb=sb: bb.tensor_copy(x2s[:], sb[32:64, :]),
                         reads=[bf], writes=[x2s_b])
                    rope_rows(v, sb[0:32, :], bf)
                    v.op(lambda bb, sb=sb: bb.tensor_copy(sb[32:64, :], x2s[:]),
                         reads=[x2s_b], writes=[bf])
                    g.op(lambda bb, sb=sb: bb.tensor_copy(rscr[:], sb[64:128, :]),
                         reads=[bf], writes=[rscr_b])
                    g.op(lambda bb: bb.tensor_copy(x2s[:], rscr[32:64, :]),
                         reads=[rscr_b], writes=[x2s_b])
                    rope_rows(g, rscr[0:32, :], rscr_b)
                    g.op(lambda bb: bb.tensor_copy(rscr[32:64, :], x2s[:]),
                         reads=[x2s_b], writes=[rscr_b])
                    g.op(lambda bb, sb=sb: bb.tensor_copy(sb[64:128, :], rscr[:]),
                         reads=[rscr_b], writes=[bf])
            rope_ph.__exit__(None, None, None)
            ptile = [P.sb([128, 512], BF16) for _ in range(2)]
            ptile_b = [Buf() for _ in range(2)]
            rrf = P.sb([1, 512], F32); rrf_b = Buf()
            rrow = P.sb([1, 512], BF16); rrow_b = Buf()
            recb = P.sb([64, 512], F32); recb_b = Buf()
            evq = [P.sb([128, 512], F32) for _ in range(2)]
            evq_b = [Buf() for _ in range(2)]
            prev = [P.sb([128, D], F32) for _ in range(2)]
            prev_b = [Buf() for _ in range(2)]
            atok = [P.sb([128, D], F32) for _ in range(4)]
            atok_b = [Buf() for _ in range(4)]
            woT = load_weight_T(P, f"l0o{grp}", wts["L0_attn_wo"], D, D)
            for tq in range(nq):
                for hh in range(4):
                    qsb, qbf = qk[("q", hh // 2)]
                    ksb, kbf = qk[("k", hh // 2)]
                    r0 = (hh % 2) * 64
                    patt = pf[2]
                    nk = tq * 4 + 4
                    for tkc in range(nk):
                        psc = pf[3 + (tkc % 2)]
                        t.matmul_acc(psc.ap[0:128, 0:512],
                                     [(ksb[r0:r0 + 64, tkc * 128:(tkc + 1) * 128],
                                       qsb[r0:r0 + 64, tq * 512:(tq + 1) * 512])],
                                     reads=[kbf, qbf], writes=[psc.buf])
                        pbi = tkc % 2
                        s.op(lambda bb, psc=psc, pbi=pbi: bb.activation(
                            ptile[pbi][:], psc.ap[0:128, 0:512], AF.Exp,
                            scale=0.125, bias=epssb[:, 2:3]),
                            reads=[psc.buf, epsbuf], writes=[ptile_b[pbi]])
                        dsub = tkc - tq * 4
                        if dsub >= 1:
                            v.op(lambda bb, pbi=pbi, dsub=dsub: bb.memset(
                                ptile[pbi][:, 0:dsub * 128], 0.0), writes=[ptile_b[pbi]])
                        if dsub >= 0:
                            v.op(lambda bb, pbi=pbi, dsub=dsub: bb.tensor_mul(
                                ptile[pbi][:, dsub * 128:(dsub + 1) * 128],
                                ptile[pbi][:, dsub * 128:(dsub + 1) * 128], mask_sb[:]),
                                reads=[mask_buf], writes=[ptile_b[pbi]])
                        t.matmul_acc(patt.ap[0:65, 0:512],
                                     [(vtok[:, tkc, hh * 65:hh * 65 + 65], ptile[pbi][:])],
                                     reads=[vtok_buf, ptile_b[pbi]], writes=[patt.buf],
                                     flags=[(tkc == 0, tkc == nk - 1)])
                    v.op(lambda bb, patt=patt: bb.reciprocal(rrf[:], patt.ap[64:65, 0:512]),
                         reads=[patt.buf], writes=[rrf_b])
                    v.op(lambda bb: bb.tensor_copy(rrow[:], rrf[:]),
                         reads=[rrf_b], writes=[rrow_b])
                    pn = pf[5]
                    t.matmul_acc(pn.ap[0:64, 0:512], [(ones_sb[:, 0:64], rrow[:])],
                                 reads=[rrow_b, ones_buf], writes=[pn.buf])
                    s.op(lambda bb, pn=pn: bb.copy(recb[:], pn.ap[0:64, 0:512]),
                         reads=[pn.buf], writes=[recb_b])
                    v.op(lambda bb, patt=patt, hh=hh: bb.tensor_tensor(
                        att_ar[(hh % 2) * 64:(hh % 2) * 64 + 64, hh // 2, :],
                        patt.ap[0:64, 0:512], recb[:], ALU.mult),
                        reads=[patt.buf, recb_b], writes=[att_ar_buf])
                for oc in range(8):
                    pt = pf[3]
                    pairs = [(woT.lhsT(grp * 2 + dc, oc * 128, (oc + 1) * 128),
                              att_ar[:, dc, :]) for dc in range(2)]
                    t.matmul_acc(pt.ap[:, 0:512], pairs,
                                 reads=[woT.buf, att_ar_buf], writes=[pt.buf])
                    eb = oc % 2
                    s.op(lambda bb, pt=pt, eb=eb: bb.copy(evq[eb][:], pt.ap[:, 0:512]),
                         reads=[pt.buf], writes=[evq_b[eb]])
                    for tsub in range(4):
                        pt2 = pf[4]
                        t.transpose(pt2.ap[:, 0:128], evq[eb][:, tsub * 128:(tsub + 1) * 128],
                                    idf_sb[:], reads=[evq_b[eb], idf_buf], writes=[pt2.buf])
                        v.op(lambda bb, pt2=pt2, oc=oc, tsub=tsub: bb.tensor_copy(
                            atok[tsub][:, oc * 128:(oc + 1) * 128], pt2.ap[:, 0:128]),
                            reads=[pt2.buf], writes=[atok_b[tsub]])
                        if oc == 7:
                            ti = tq * 4 + tsub
                            if grp == 0:
                                sy.dma(attn_out[ti * 128:(ti + 1) * 128, :], atok[tsub][:],
                                       reads=[atok_b[tsub]], writes=[attn_out_b])
                            else:
                                sy.dma(prev[tsub % 2][:], attn_out[ti * 128:(ti + 1) * 128, :],
                                       reads=[attn_out_b], writes=[prev_b[tsub % 2]])
                                v.op(lambda bb, tsub=tsub: bb.tensor_add(
                                    atok[tsub][:], atok[tsub][:], prev[tsub % 2][:]),
                                    reads=[prev_b[tsub % 2]], writes=[atok_b[tsub]])
                                sy.dma(attn_out[ti * 128:(ti + 1) * 128, :], atok[tsub][:],
                                       reads=[atok_b[tsub]], writes=[attn_out_b])
    for grp in range(4):
        _group(grp)


def emit_ln_pass(P, ctx, src, src_dtype, h_tok, h_fm_sb, h_fm_buf, gain=None,
                 beta=None, eps=1e-5):
    v, s, t, sy = P.vector, P.scalar, P.tensor, P.sync
    NT = NTILE
    idb_sb, idb_buf = cref(ctx, "ident_bf")
    src, src_b = src
    h_tok, h_tok_b = h_tok if h_tok is not None else (None, None)
    epssb, epsbuf = cref(ctx, "eps_cols")
    xt = [P.sb([128, D], src_dtype) for _ in range(2)]
    xtb = [Buf() for _ in range(2)]
    xnf = [P.sb([128, D], F32) for _ in range(2)]
    xnfb = [Buf() for _ in range(2)]
    xnb = [P.sb([128, D], BF16) for _ in range(2)]
    xnbb = [Buf() for _ in range(2)]
    sq = [P.sb([128, D], F32) for _ in range(2)]
    sqb = [Buf() for _ in range(2)]
    stat = [[P.sb([128, 1], F32) for _ in range(4)] for _ in range(2)]
    statb = [Buf() for _ in range(2)]
    for i in range(NT):
        b = i % 2
        r = slice(i * 128, (i + 1) * 128)
        sy.dma(xt[b][:], src[r, :], reads=[src_b], writes=[xtb[b]])
        v.op(lambda bb, b=b: bb.tensor_reduce(stat[b][0][:], xt[b][:], AX.X, ALU.add),
             reads=[xtb[b]], writes=[statb[b]])
        v.op(lambda bb, b=b: bb.tensor_scalar_mul(stat[b][0][:], stat[b][0][:], 1.0 / D),
             reads=[statb[b]], writes=[statb[b]])
        v.op(lambda bb, b=b: bb.tensor_scalar(xnf[b][:], xt[b][:], stat[b][0][:], None,
                                              ALU.subtract),
             reads=[xtb[b], statb[b]], writes=[xnfb[b]])
        s.op(lambda bb, b=b: bb.activation(sq[b][:], xnf[b][:], AF.Square),
             reads=[xnfb[b]], writes=[sqb[b]])
        v.op(lambda bb, b=b: bb.tensor_reduce(stat[b][1][:], sq[b][:], AX.X, ALU.add),
             reads=[sqb[b]], writes=[statb[b]])
        s.op(lambda bb, b=b: bb.activation(stat[b][2][:], stat[b][1][:], AF.Ln,
                                           scale=1.0 / D, bias=epssb[:, 0:1]),
             reads=[statb[b], epsbuf], writes=[statb[b]])
        s.op(lambda bb, b=b: bb.activation(stat[b][3][:], stat[b][2][:], AF.Exp,
                                           scale=-0.5),
             reads=[statb[b]], writes=[statb[b]])
        v.op(lambda bb, b=b: bb.tensor_scalar(xnf[b][:], xnf[b][:], stat[b][3][:], None,
                                              ALU.mult),
             reads=[statb[b]], writes=[xnfb[b]])
        if gain is not None:
            v.op(lambda bb, b=b: bb.tensor_mul(xnf[b][:], xnf[b][:], gain[0][:]),
                 reads=[gain[1], xnfb[b]], writes=[xnfb[b]])
            v.op(lambda bb, b=b: bb.tensor_add(xnf[b][:], xnf[b][:], beta[0][:]),
                 reads=[beta[1], xnfb[b]], writes=[xnfb[b]])
        v.op(lambda bb, b=b: bb.tensor_copy(xnb[b][:], xnf[b][:]),
             reads=[xnfb[b]], writes=[xnbb[b]])
        if h_tok is not None:
            sy.dma(h_tok[r, :], xnb[b][:], reads=[xnbb[b]], writes=[h_tok_b])
        if h_fm_sb is not None:
            for dc in range(8):
                pt = ctx["pb"][dc % 2]
                t.transpose(pt.ap[:, 0:128], xnb[b][:, dc * 128:(dc + 1) * 128],
                            idb_sb[:], reads=[xnbb[b], idb_buf], writes=[pt.buf])
                v.op(lambda bb, b=b, dc=dc, pt=pt, i=i: bb.tensor_copy(
                    h_fm_sb[:, dc, i * 128:(i + 1) * 128], pt.ap[:, 0:128]),
                    reads=[pt.buf], writes=[h_fm_buf])


def bias_cols(P, ctx, name, src_ap, n):
    sb = P.sb([128, n // 128], F32, name=name)
    b = Buf(name)
    P.sync.dma(sb[:], src_ap.rearrange("(c p) -> p c", p=128), writes=[b],
               allow_slow_non_contiguous=True)
    return sb, b


def repl_row(P, ctx, name, src_ap, n):
    row = P.sb([1, n], BF16, name=name + "_r")
    rowb = Buf()
    P.gpsimd.dma(row[:], src_ap.rearrange("(o n) -> o n", o=1), writes=[rowb])
    out = P.sb([128, n], F32, name=name)
    ob = Buf(name)
    ones_sb, ones_buf = cref(ctx, "ones_row")
    for c in range(n // 512):
        pt = ctx["pf"][5]
        P.tensor.matmul_acc(pt.ap[:, 0:512], [(ones_sb[:, 0:128], row[:, c * 512:(c + 1) * 512])],
                            reads=[rowb, ones_buf], writes=[pt.buf])
        P.vector.op(lambda bb, pt=pt, c=c: bb.tensor_copy(out[:, c * 512:(c + 1) * 512],
                                                          pt.ap[:, 0:512]),
                    reads=[pt.buf], writes=[ob])
    return out, ob


def emit_l1_attn(P, ctx, wts, xn_tok, attn_out, attn_out_b, stop_after=None):
    v, s, t, sy, g = P.vector, P.scalar, P.tensor, P.sync, P.gpsimd
    NT = NTILE
    Tt = NT * 128
    idf_sb, idf_buf = cref(ctx, "ident_f32")
    idb_sb, idb_buf = cref(ctx, "ident_bf")
    onesr_sb, onesr_buf = cref(ctx, "ones_row")
    epssb, epsbuf = cref(ctx, "eps_cols")
    mbk_sb, mbk_buf = cref(ctx, "mask_bucket")
    mbk0_sb, mbk0_buf = cref(ctx, "mask_bucket0")
    pf = ctx["pf"]; pbk = ctx["pb"]
    h_tok = P.dscratch("l1_h", [Tt, D], BF16); h_tok_b = Buf()
    hfm_hbm = P.dscratch("l1_hfm", [D, Tt], BF16); hfm_b = Buf()
    x1 = P.dscratch("l1_x1", [Tt, D], F32); x1_b = Buf()
    attfm_hbm = P.dscratch("l1_attfm", [D, Tt], BF16); attfm_b = Buf()
    gfm_hbm = P.dscratch("l1_gfm", [4 * D, Tt], BF16); gfm_b = Buf()
    with P.phase("l1ln1"):
        g1 = repl_row(P, ctx, "l1g1", wts["L1_attn_ln1_g"], D)
        b1 = repl_row(P, ctx, "l1b1", wts["L1_attn_ln1_b"], D)
        hfm_sb = P.sb([128, 8, Tt], BF16); hfm_sb_b = Buf()
        emit_ln_pass(P, ctx, xn_tok, BF16, (h_tok, h_tok_b), hfm_sb, hfm_sb_b,
                     gain=g1, beta=b1)
        sy.dma(hfm_hbm.rearrange("(c p) t -> p c t", p=128), hfm_sb[:],
               reads=[hfm_sb_b], writes=[hfm_b])
    for grp in range(4):
        _l1_attn_group(P, ctx, wts, grp, hfm_hbm, hfm_b, attfm_hbm, attfm_b)
    if stop_after == "groups":
        return {"attfm": attfm_hbm, "attfm_b": attfm_b, "h_tok": h_tok, "h_tok_b": h_tok_b}
    with P.phase("l1wout"):
        wout = WeightFM(P, "l1wout", D, D).load(P, wts["L1_attn_w_out"])
        bout_sb, bout_b = bias_cols(P, ctx, "l1bout", wts["L1_attn_b_out"], D)
        afm = P.sb([128, 8, 512], BF16, name="l1afm"); afm_b = Buf()
        evq = P.sb([128, 512], F32, name="l1evqw"); evq_b = Buf()
        xn_t = [P.sb([128, D], BF16) for _ in range(2)]
        xn_tb = [Buf() for _ in range(2)]
        ot = [P.sb([128, D], F32) for _ in range(4)]
        ot_b = [Buf() for _ in range(4)]
        for tt in range(Tt // 512):
            sy.dma(afm[:], attfm_hbm.rearrange("(c p) t -> p c t", p=128)[:, :, tt * 512:(tt + 1) * 512],
                   reads=[attfm_b], writes=[afm_b])
            for oc in range(8):
                pt = pf[oc % 2]
                pairs = [(wout.lhsT(dc, oc * 128, (oc + 1) * 128), afm[:, dc, :])
                         for dc in range(8)]
                t.matmul_acc(pt.ap[:, 0:512], pairs, reads=[wout.buf, afm_b],
                             writes=[pt.buf])
                v.op(lambda bb, pt=pt, oc=oc: bb.tensor_scalar(
                    evq[:], pt.ap[:, 0:512], bout_sb[:, oc:oc + 1], None, ALU.add),
                    reads=[pt.buf, bout_b], writes=[evq_b])
                for tsub in range(4):
                    pt2 = pf[4]
                    t.transpose(pt2.ap[:, 0:128], evq[:, tsub * 128:(tsub + 1) * 128],
                                idf_sb[:], reads=[evq_b, idf_buf], writes=[pt2.buf])
                    v.op(lambda bb, pt2=pt2, oc=oc, tsub=tsub: bb.tensor_copy(
                        ot[tsub][:, oc * 128:(oc + 1) * 128], pt2.ap[:, 0:128]),
                        reads=[pt2.buf], writes=[ot_b[tsub]])
                    if oc == 7:
                        ti = tt * 4 + tsub
                        b2 = tsub % 2
                        sy.dma(xn_t[b2][:], xn_tok[0][ti * 128:(ti + 1) * 128, :],
                               reads=[xn_tok[1]], writes=[xn_tb[b2]])
                        v.op(lambda bb, tsub=tsub, b2=b2: bb.tensor_add(
                            ot[tsub][:], ot[tsub][:], xn_t[b2][:]),
                            reads=[xn_tb[b2]], writes=[ot_b[tsub]])
                        sy.dma(x1[ti * 128:(ti + 1) * 128, :], ot[tsub][:],
                               reads=[ot_b[tsub]], writes=[x1_b])
    h2fm_hbm = P.dscratch("l1_h2fm", [D, Tt], BF16); h2fm_hb = Buf()
    with P.phase("l1ffn1"):
        with P.phase("l1ln2"):
            g2 = repl_row(P, ctx, "l1g2", wts["L1_attn_ln2_g"], D)
            b2r = repl_row(P, ctx, "l1b2r", wts["L1_attn_ln2_b"], D)
            h2fm_sb = P.sb([128, 8, Tt], BF16); h2fm_b = Buf()
            emit_ln_pass(P, ctx, (x1, x1_b), F32, None, h2fm_sb, h2fm_b,
                         gain=g2, beta=b2r)
            sy.dma(h2fm_hbm.rearrange("(c p) t -> p c t", p=128), h2fm_sb[:],
                   reads=[h2fm_b], writes=[h2fm_hb])
        w1 = WeightFM(P, "l1w1", D, 4 * D).load(P, wts["L1_attn_w1"])
        b1f_sb, b1f_b = bias_cols(P, ctx, "l1b1f", wts["L1_attn_b1"], 4 * D)
        h2ch = P.sb([128, 8, 512], BF16); h2ch_b = Buf()
        gout = P.sb([128, 512], BF16); gout_b = Buf()
        for tt in range(Tt // 512):
            sy.dma(h2ch[:], h2fm_hbm.rearrange("(c p) t -> p c t", p=128)
                   [:, :, tt * 512:(tt + 1) * 512], reads=[h2fm_hb], writes=[h2ch_b])
            for oc in range(32):
                pt = pf[oc % 2]
                pairs = [(w1.lhsT(dc, oc * 128, (oc + 1) * 128),
                          h2ch[:, dc, :]) for dc in range(8)]
                t.matmul_acc(pt.ap[:, 0:512], pairs, reads=[w1.buf, h2ch_b],
                             writes=[pt.buf])
                s.op(lambda bb, pt=pt, oc=oc: bb.activation(
                    gout[:], pt.ap[:, 0:512], AF.Gelu, bias=b1f_sb[:, oc:oc + 1]),
                    reads=[pt.buf, b1f_b], writes=[gout_b])
                sy.dma(gfm_hbm[oc * 128:(oc + 1) * 128, tt * 512:(tt + 1) * 512],
                       gout[:], reads=[gout_b], writes=[gfm_b])
    with P.phase("l1ffn2"):
        w2 = WeightFM(P, "l1w2", 4 * D, D).load(P, wts["L1_attn_w2"])
        b2f_sb, b2f_b = bias_cols(P, ctx, "l1b2f", wts["L1_attn_b2"], D)
        gch = P.sb([128, 32, 256], BF16); gch_b = Buf()
        evq2 = P.sb([128, 256], F32); evq2_b = Buf()
        x1t = [P.sb([128, D], F32) for _ in range(2)]
        x1tb = [Buf() for _ in range(2)]
        ot2 = [P.sb([128, D], F32) for _ in range(2)]
        ot2_b = [Buf() for _ in range(2)]
        for tt in range(Tt // 256):
            sy.dma(gch[:], gfm_hbm.rearrange("(c p) t -> p c t", p=128)[:, :, tt * 256:(tt + 1) * 256],
                   reads=[gfm_b], writes=[gch_b])
            for oc in range(8):
                pt = pf[oc % 2]
                pairs = [(w2.lhsT(fc, oc * 128, (oc + 1) * 128), gch[:, fc, :])
                         for fc in range(32)]
                t.matmul_acc(pt.ap[:, 0:256], pairs, reads=[w2.buf, gch_b],
                             writes=[pt.buf])
                v.op(lambda bb, pt=pt, oc=oc: bb.tensor_scalar(
                    evq2[:], pt.ap[:, 0:256], b2f_sb[:, oc:oc + 1], None, ALU.add),
                    reads=[pt.buf, b2f_b], writes=[evq2_b])
                for tsub in range(2):
                    pt2 = pf[4]
                    t.transpose(pt2.ap[:, 0:128], evq2[:, tsub * 128:(tsub + 1) * 128],
                                idf_sb[:], reads=[evq2_b, idf_buf], writes=[pt2.buf])
                    v.op(lambda bb, pt2=pt2, oc=oc, tsub=tsub: bb.tensor_copy(
                        ot2[tsub][:, oc * 128:(oc + 1) * 128], pt2.ap[:, 0:128]),
                        reads=[pt2.buf], writes=[ot2_b[tsub]])
                    if oc == 7:
                        ti = tt * 2 + tsub
                        b2 = tsub % 2
                        sy.dma(x1t[b2][:], x1[ti * 128:(ti + 1) * 128, :],
                               reads=[x1_b], writes=[x1tb[b2]])
                        v.op(lambda bb, tsub=tsub, b2=b2: bb.tensor_add(
                            ot2[tsub][:], ot2[tsub][:], x1t[b2][:]),
                            reads=[x1tb[b2]], writes=[ot2_b[tsub]])
                        sy.dma(attn_out[ti * 128:(ti + 1) * 128, :], ot2[tsub][:],
                               reads=[ot2_b[tsub]], writes=[attn_out_b])


def _l1_attn_group(P, ctx, wts, grp, hfm_hbm, hfm_b, attfm_hbm, attfm_b):
    v, s, t, sy, g = P.vector, P.scalar, P.tensor, P.sync, P.gpsimd
    NT = NTILE
    Tt = NT * 128
    NB = Tt // 64
    onesr_sb, onesr_buf = cref(ctx, "ones_row")
    mbk_sb, mbk_buf = cref(ctx, "mask_bucket")
    mbk0_sb, mbk0_buf = cref(ctx, "mask_bucket0")
    pf = ctx["pf"]
    with P.phase(f"l1att{grp}"):
        hch = P.sb([128, 8, 512], BF16); hch_b = Buf()
        wsl = P.sb([128, 8, 1024], BF16); wsl_b = Buf()
        bq_sb = P.sb([128, 24], F32); bq_b = Buf()
        bvrow = P.sb([1, 512], BF16); bvrow_b = Buf()
        qf = P.sb([128, 2, Tt], BF16); qf_b = Buf()
        kf = P.sb([128, 2, Tt], BF16); kf_b = Buf()
        vtok = P.sb([128, NT, 4 * 65], BF16); vtok_b = Buf()
        vtok_s = P.sb([128, NT, 4 * 65], BF16); vtok_s_b = Buf()
        ptile = P.sb([128, 512], BF16); ptile_b = Buf()
        rrf = P.sb([1, 512], F32); rrf_b = Buf()
        rrow = P.sb([1, 512], BF16); rrow_b = Buf()
        recb = P.sb([128, 512], F32); recb_b = Buf()
        aev = P.sb([128, 512], BF16); aev_b = Buf()
        qc0 = grp * 256
        vc0 = grp * 256
        g.dma(wsl[:, :, 0:256], wts["L1_attn_w_qkv"][:, qc0:qc0 + 256]
              .rearrange("(c p) o -> p c o", p=128), writes=[wsl_b])
        g.dma(wsl[:, :, 256:512], wts["L1_attn_w_qkv"][:, 1024 + qc0:1024 + qc0 + 256]
              .rearrange("(c p) o -> p c o", p=128), writes=[wsl_b])
        g.dma(wsl[:, :, 512:768], wts["L1_attn_w_qkv"][:, 2048 + vc0:2048 + vc0 + 256]
              .rearrange("(c p) o -> p c o", p=128), writes=[wsl_b])
        sy.dma(bq_sb[:], wts["L1_attn_b_qkv"].rearrange("(c p) -> p c", p=128),
               writes=[bq_b], allow_slow_non_contiguous=True)
        g.dma(bvrow[:, 0:256], wts["L1_attn_b_qkv"][2048 + vc0:2048 + vc0 + 256]
              .rearrange("(o n) -> o n", o=1), writes=[bvrow_b])
        for base, dst, dstb, bcbase in ((0, qf, qf_b, grp * 2), (256, kf, kf_b, 8 + grp * 2)):
            for op2 in range(2):
                for tt in range(Tt // 512):
                    pt = pf[tt % 2]
                    sy.dma(hch[:], hfm_hbm.rearrange("(c p) t -> p c t", p=128)
                           [:, :, tt * 512:(tt + 1) * 512], reads=[hfm_b], writes=[hch_b])
                    pairs = [(wsl[:, dc, base + op2 * 128:base + (op2 + 1) * 128],
                              hch[:, dc, :]) for dc in range(8)]
                    t.matmul_acc(pt.ap[:, 0:512], pairs,
                                 reads=[wsl_b, hch_b], writes=[pt.buf])
                    v.op(lambda bb, pt=pt, dst=dst, op2=op2, tt=tt, bc=bcbase + op2:
                         bb.tensor_scalar(dst[:, op2, tt * 512:(tt + 1) * 512],
                                          pt.ap[:, 0:512], bq_sb[:, bc:bc + 1], None,
                                          ALU.add),
                         reads=[pt.buf, bq_b], writes=[dstb])
        for tt in range(Tt // 512):
            sy.dma(hch[:], hfm_hbm.rearrange("(c p) t -> p c t", p=128)
                   [:, :, tt * 512:(tt + 1) * 512], reads=[hfm_b], writes=[hch_b])
            for ts4 in range(4):
                ti = tt * 4 + ts4
                pt = pf[ts4 % 2]
                pairs = [(hch[:, dc, ts4 * 128:(ts4 + 1) * 128],
                          wsl[:, dc, 512:768]) for dc in range(8)]
                flags = [(i == 0, False) for i in range(8)] + [(False, True)]
                t.matmul_acc([pt.ap[:, 0:256]] * 8 + [pt.ap[:, 0:256]],
                             pairs + [(onesr_sb[0:1, :], bvrow[:, 0:256])],
                             reads=[hch_b, wsl_b, bvrow_b, onesr_buf],
                             writes=[pt.buf], flags=flags)
                v.op(lambda bb, pt=pt, ti=ti: bb.tensor_copy(
                    vtok[:, ti, 0:260].rearrange("p (h c) -> p h c", h=4)[:, :, 0:64],
                    pt.ap[:, 0:256].rearrange("p (h c) -> p h c", h=4)),
                    reads=[pt.buf], writes=[vtok_b])
        v.op(lambda bb: bb.memset(
            vtok[:].rearrange("p n (h c) -> p n h c", h=4)[:, :, :, 64:65], 1.0),
            writes=[vtok_b])
        for tj in range(1, NT):
            sy.dma(vtok_s[0:64, tj, :], vtok[64:128, tj - 1, :],
                   reads=[vtok_b], writes=[vtok_s_b])
            sy.dma(vtok_s[64:128, tj, :], vtok[0:64, tj, :],
                   reads=[vtok_b], writes=[vtok_s_b])
        for hh in range(4):
            for bg in range(NB // 8):
                psc = pf[2]
                patt = pf[3]
                for j in range(8):
                    b = bg * 8 + j
                    if b == 0:
                        kl = kf[(hh % 2) * 64:(hh % 2) * 64 + 64, hh // 2, 0:128]
                    else:
                        kl = kf[(hh % 2) * 64:(hh % 2) * 64 + 64, hh // 2,
                                (b - 1) * 64:(b + 1) * 64]
                    ql = qf[(hh % 2) * 64:(hh % 2) * 64 + 64, hh // 2,
                            b * 64:(b + 1) * 64]
                    t.matmul_acc(psc.ap[0:128, j * 64:(j + 1) * 64], [(kl, ql)],
                                 reads=[kf_b, qf_b], writes=[psc.buf],
                                 flags=[(True, True)])
                s.op(lambda bb, psc=psc: bb.activation(ptile[:], psc.ap[0:128, 0:512],
                                                       AF.Exp, scale=0.125),
                     reads=[psc.buf], writes=[ptile_b])
                if bg == 0:
                    v.op(lambda bb: bb.tensor_mul(ptile[:, 0:64], ptile[:, 0:64],
                                                  mbk0_sb[:]),
                         reads=[mbk0_buf], writes=[ptile_b])
                    v.op(lambda bb: bb.tensor_tensor(
                        ptile[:, 64:512].rearrange("p (g q) -> p g q", q=64),
                        ptile[:, 64:512].rearrange("p (g q) -> p g q", q=64),
                        mbk_sb[:].rearrange("p (g q) -> p g q", g=1)
                        .broadcast_to([128, 7, 64]), ALU.mult),
                        reads=[mbk_buf], writes=[ptile_b])
                else:
                    v.op(lambda bb: bb.tensor_tensor(
                        ptile[:].rearrange("p (g q) -> p g q", q=64),
                        ptile[:].rearrange("p (g q) -> p g q", q=64),
                        mbk_sb[:].rearrange("p (g q) -> p g q", g=1)
                        .broadcast_to([128, 8, 64]), ALU.mult),
                        reads=[mbk_buf], writes=[ptile_b])
                for j in range(8):
                    b = bg * 8 + j
                    pcol = ptile[:, j * 64:(j + 1) * 64]
                    if b == 0:
                        t.matmul_acc(patt.ap[0:65, j * 64:(j + 1) * 64],
                                     [(vtok[:, 0, hh * 65:(hh + 1) * 65], pcol)],
                                     reads=[vtok_b, ptile_b], writes=[patt.buf],
                                     flags=[(True, True)])
                    elif b % 2 == 1:
                        ti0 = (b - 1) // 2
                        t.matmul_acc(patt.ap[0:65, j * 64:(j + 1) * 64],
                                     [(vtok[:, ti0, hh * 65:(hh + 1) * 65], pcol)],
                                     reads=[vtok_b, ptile_b], writes=[patt.buf],
                                     flags=[(True, True)])
                    else:
                        t.matmul_acc(patt.ap[0:65, j * 64:(j + 1) * 64],
                                     [(vtok_s[:, b // 2, hh * 65:(hh + 1) * 65], pcol)],
                                     reads=[vtok_s_b, ptile_b], writes=[patt.buf],
                                     flags=[(True, True)])
                v.op(lambda bb, patt=patt: bb.tensor_copy(rrf[:], patt.ap[64:65, 0:512]),
                     reads=[patt.buf], writes=[rrf_b])
                if bg == 0:
                    v.op(lambda bb: bb.tensor_scalar_add(rrf[:, 0:64], rrf[:, 0:64], 64.0),
                         reads=[rrf_b], writes=[rrf_b])
                v.op(lambda bb: bb.reciprocal(rrf[:], rrf[:]),
                     reads=[rrf_b], writes=[rrf_b])
                v.op(lambda bb: bb.tensor_copy(rrow[:], rrf[:]),
                     reads=[rrf_b], writes=[rrow_b])
                pn = pf[5]
                t.matmul_acc(pn.ap[0:64, 0:512], [(onesr_sb[:, 0:64], rrow[:])],
                             reads=[rrow_b, onesr_buf], writes=[pn.buf])
                s.op(lambda bb, pn=pn: bb.copy(recb[0:64, :], pn.ap[0:64, 0:512]),
                     reads=[pn.buf], writes=[recb_b])
                v.op(lambda bb, patt=patt: bb.tensor_tensor(
                    aev[0:64, :], patt.ap[0:64, 0:512], recb[0:64, :], ALU.mult),
                    reads=[patt.buf, recb_b], writes=[aev_b])
                sy.dma(attfm_hbm[grp * 256 + hh * 64:grp * 256 + (hh + 1) * 64,
                                 bg * 512:(bg + 1) * 512], aev[0:64, :],
                       reads=[aev_b], writes=[attfm_b])


def emit_l1_moe(P, ctx, wts, xn_fm_sb, xn_fm_buf, xn_tok, ffn_out, seg=1280,
                rlog=None, stage="all", disp_io=None, disp_in=None,
                eout_io=None, part_io=None):
    v, s, t, sy, g = P.vector, P.scalar, P.tensor, P.sync, P.gpsimd
    NT = NTILE
    Tt = NT * 128
    E = 8
    xn_tok, xn_tok_b = xn_tok
    ffn_out, ffn_out_b = ffn_out if ffn_out is not None else (None, None)
    idb_sb, idb_buf = cref(ctx, "ident_bf")
    tri_sb, tri_buf = cref(ctx, "tri_strict")
    onesr_sb, onesr_buf = cref(ctx, "ones_row")
    onesc_sb, onesc_buf = cref(ctx, "ones_col")
    iota8_sb, iota8_buf = cref(ctx, "iota8")
    iotat_sb, iotat_buf = cref(ctx, "iota_tok")
    pf = ctx["pf"]; pbk = ctx["pb"]
    nslot = E * seg
    nst = seg // 128
    if eout_io is not None:
        eout, eout_b = eout_io
    else:
        eout = P.dscratch("l1_eout", [nslot, D], F32); eout_b = Buf()
    with P.phase("l1moe"):
        if rlog is None:
            wg_sb = P.sb([128, 8, 8], BF16); wg_b = Buf()
            g.dma(wg_sb[:], wts["L1_ffn_Wg"].rearrange("(c p) e -> p c e", p=128),
                  writes=[wg_b])
        else:
            rlog_sb, rlog_buf = rlog
        probs = P.sb([128, 8], F32); probs_b = Buf()
        mcol = P.sb([128, 1], F32); mcol_b = Buf()
        m2col = P.sb([128, 1], F32); m2col_b = Buf()
        scol = P.sb([128, 1], F32); scol_b = Buf()
        eq1 = P.sb([128, 8], F32); eq1_b = Buf()
        eq2 = P.sb([128, 8], F32); eq2_b = Buf()
        a2 = P.sb([128, 8], F32); a2_b = Buf()
        w1m = P.sb([128, NT], F32); w1m_b = Buf()
        w2m = P.sb([128, NT], F32); w2m_b = Buf()
        idx1 = P.sb([128, NT], F32); idx1_b = Buf()
        idx2 = P.sb([128, NT], F32); idx2_b = Buf()
        inds = P.sb([128, E * NT], BF16); inds_b = Buf()
        ind1f = P.sb([128, E * NT], F32); ind1f_b = Buf()
        ind2f = P.sb([128, E * NT], F32); ind2f_b = Buf()
        for i in range(NT):
            if rlog is None:
                pt = pf[5]
                pairs = [(xn_fm_sb[:, dc, i * 128:(i + 1) * 128], wg_sb[:, dc, :])
                         for dc in range(8)]
                t.matmul_acc(pt.ap[0:128, 0:8], pairs, reads=[xn_fm_buf, wg_b],
                             writes=[pt.buf])
                s.op(lambda bb, pt=pt: bb.copy(probs[:], pt.ap[0:128, 0:8]),
                     reads=[pt.buf], writes=[probs_b])
            else:
                v.op(lambda bb, i=i: bb.tensor_copy(probs[:], rlog_sb[:, i, :]),
                     reads=[rlog_buf], writes=[probs_b])
            v.op(lambda bb: bb.reduce_max(mcol[:], probs[:], AX.X),
                 reads=[probs_b], writes=[mcol_b])
            v.op(lambda bb: bb.tensor_scalar(probs[:], probs[:], mcol[:], None,
                                             ALU.subtract),
                 reads=[probs_b, mcol_b], writes=[probs_b])
            s.op(lambda bb: bb.activation(probs[:], probs[:], AF.Exp),
                 reads=[probs_b], writes=[probs_b])
            v.op(lambda bb: bb.tensor_reduce(scol[:], probs[:], AX.X, ALU.add),
                 reads=[probs_b], writes=[scol_b])
            v.op(lambda bb: bb.reciprocal(scol[:], scol[:]),
                 reads=[scol_b], writes=[scol_b])
            v.op(lambda bb: bb.tensor_scalar(probs[:], probs[:], scol[:], None, ALU.mult),
                 reads=[probs_b, scol_b], writes=[probs_b])
            v.op(lambda bb: bb.reduce_max(mcol[:], probs[:], AX.X),
                 reads=[probs_b], writes=[mcol_b])
            v.op(lambda bb: bb.tensor_scalar(eq1[:], probs[:], mcol[:], None, ALU.is_ge),
                 reads=[probs_b, mcol_b], writes=[eq1_b])
            v.op(lambda bb: bb.scalar_tensor_tensor(a2[:], eq1[:], -2.0, probs[:],
                                                    ALU.mult, ALU.add),
                 reads=[eq1_b, probs_b], writes=[a2_b])
            v.op(lambda bb: bb.reduce_max(m2col[:], a2[:], AX.X),
                 reads=[a2_b], writes=[m2col_b])
            v.op(lambda bb: bb.tensor_scalar(eq2[:], a2[:], m2col[:], None, ALU.is_ge),
                 reads=[a2_b, m2col_b], writes=[eq2_b])
            v.op(lambda bb: bb.tensor_add(scol[:], mcol[:], m2col[:]),
                 reads=[mcol_b, m2col_b], writes=[scol_b])
            v.op(lambda bb: bb.reciprocal(scol[:], scol[:]),
                 reads=[scol_b], writes=[scol_b])
            v.op(lambda bb, i=i: bb.tensor_tensor(w1m[:, i:i + 1], mcol[:], scol[:],
                                                  ALU.mult),
                 reads=[mcol_b, scol_b], writes=[w1m_b])
            v.op(lambda bb, i=i: bb.tensor_tensor(w2m[:, i:i + 1], m2col[:], scol[:],
                                                  ALU.mult),
                 reads=[m2col_b, scol_b], writes=[w2m_b])
            v.op(lambda bb: bb.tensor_mul(a2[:], eq1[:], iota8_sb[:]),
                 reads=[eq1_b, iota8_buf], writes=[a2_b])
            v.op(lambda bb, i=i: bb.tensor_reduce(idx1[:, i:i + 1], a2[:], AX.X, ALU.add),
                 reads=[a2_b], writes=[idx1_b])
            v.op(lambda bb: bb.tensor_mul(a2[:], eq2[:], iota8_sb[:]),
                 reads=[eq2_b, iota8_buf], writes=[a2_b])
            v.op(lambda bb, i=i: bb.tensor_reduce(idx2[:, i:i + 1], a2[:], AX.X, ALU.add),
                 reads=[a2_b], writes=[idx2_b])
            iv = inds[:].rearrange("p (e j) -> p e j", e=E)
            v.op(lambda bb, i=i: bb.tensor_add(a2[:], eq1[:], eq2[:]),
                 reads=[eq1_b, eq2_b], writes=[a2_b])
            v.op(lambda bb, i=i, iv=iv: bb.tensor_copy(iv[:, :, i], a2[:]),
                 reads=[a2_b], writes=[inds_b])
            v.op(lambda bb, i=i: bb.tensor_copy(
                ind1f[:].rearrange("p (e j) -> p e j", e=E)[:, :, i], eq1[:]),
                reads=[eq1_b], writes=[ind1f_b])
            v.op(lambda bb, i=i: bb.tensor_copy(
                ind2f[:].rearrange("p (e j) -> p e j", e=E)[:, :, i], eq2[:]),
                reads=[eq2_b], writes=[ind2f_b])
        colsum_p = pf[0]
        t.matmul_acc(colsum_p.ap[0:1, 0:E * NT], [(onesc_sb[:], inds[:])],
                     reads=[inds_b, onesc_buf], writes=[colsum_p.buf])
        colrow = P.sb([1, E * NT], F32); colrow_b = Buf()
        s.op(lambda bb: bb.copy(colrow[:], colsum_p.ap[0:1, 0:E * NT]),
             reads=[colsum_p.buf], writes=[colrow_b])
        excl = P.sb([1, E, NT + 1], F32); excl_b = Buf()
        v.op(lambda bb: bb.memset(excl[:, :, 0:1], 0.0), writes=[excl_b])
        zrow = P.sb([1, NT], F32); zrow_b = Buf()
        v.op(lambda bb: bb.memset(zrow[:], 0.0), writes=[zrow_b])
        for e in range(E):
            v.op(lambda bb, e=e: bb.tensor_tensor_scan(
                excl[:, e, 1:NT + 1], colrow[:, e * NT:(e + 1) * NT], zrow[:],
                0.0, ALU.add, ALU.add),
                reads=[colrow_b, zrow_b], writes=[excl_b])
        exclb = P.sb([1, E * NT], BF16); exclb_b = Buf()
        v.op(lambda bb: bb.tensor_copy(exclb[:].rearrange("p (e j) -> p e j", e=E),
                                       excl[:, :, 0:NT]),
             reads=[excl_b], writes=[exclb_b])
        exhf = P.sb([1, E * NT], F32); exhf_b = Buf()
        v.op(lambda bb: bb.tensor_copy(exhf[:], exclb[:]),
             reads=[exclb_b], writes=[exhf_b])
        exlo = P.sb([1, E * NT], F32); exlo_b = Buf()
        v.op(lambda bb: bb.tensor_sub(exlo[:].rearrange("p (e j) -> p e j", e=E),
                                      excl[:, :, 0:NT],
                                      exhf[:].rearrange("p (e j) -> p e j", e=E)),
             reads=[excl_b, exhf_b], writes=[exlo_b])
        exclo = P.sb([1, E * NT], BF16); exclo_b = Buf()
        v.op(lambda bb: bb.tensor_copy(exclo[:], exlo[:]),
             reads=[exlo_b], writes=[exclo_b])
        Cp = pf[1]
        t.matmul_acc(Cp.ap[:, 0:E * NT],
                     [(tri_sb[:], inds[:]), (onesr_sb[:, 0:128], exclb[:]),
                      (onesr_sb[:, 0:128], exclo[:])],
                     reads=[inds_b, tri_buf, exclb_b, exclo_b, onesr_buf],
                     writes=[Cp.buf],
                     flags=[(True, False), (False, False), (False, True)])
        tmp = P.sb([128, E * NT], F32); tmp_b = Buf()
        slot1 = P.sb([128, NT], F32); slot1_b = Buf()
        slot2 = P.sb([128, NT], F32); slot2_b = Buf()
        ovf = P.sb([128, NT], F32); ovf_b = Buf()
        s1i = P.sb([128, NT], I32); s1i_b = Buf()
        s2i = P.sb([128, NT], I32); s2i_b = Buf()
        tokid_i = P.sb([128, NT], I32); tokid_i_b = Buf()
        for (indf, indf_b, idxm, idxm_b, slot, slot_b, si, si_b) in (
                (ind1f, ind1f_b, idx1, idx1_b, slot1, slot1_b, s1i, s1i_b),
                (ind2f, ind2f_b, idx2, idx2_b, slot2, slot2_b, s2i, s2i_b)):
            v.op(lambda bb, indf=indf: bb.tensor_tensor(tmp[:], Cp.ap[:, 0:E * NT],
                                                        indf[:], ALU.mult),
                 reads=[Cp.buf, indf_b], writes=[tmp_b])
            v.op(lambda bb, slot=slot: bb.tensor_reduce(
                slot[:], tmp[:].rearrange("p (e j) -> p j e", e=E), AX.X, ALU.add),
                reads=[tmp_b], writes=[slot_b])
            v.op(lambda bb, slot=slot: bb.tensor_single_scalar(ovf[:], slot[:],
                                                               float(seg), ALU.is_ge),
                 reads=[slot_b], writes=[ovf_b])
            v.op(lambda bb, slot=slot, idxm=idxm: bb.scalar_tensor_tensor(
                slot[:], idxm[:], float(seg), slot[:], ALU.mult, ALU.add),
                reads=[idxm_b, slot_b], writes=[slot_b])
            v.op(lambda bb, slot=slot: bb.scalar_tensor_tensor(
                slot[:], ovf[:], 1e9, slot[:], ALU.mult, ALU.add),
                reads=[ovf_b, slot_b], writes=[slot_b])
            v.op(lambda bb, si=si, slot=slot: bb.tensor_copy(si[:], slot[:]),
                 reads=[slot_b], writes=[si_b])
        v.op(lambda bb: bb.tensor_copy(tokid_i[:], iotat_sb[:, 0:NT]),
             reads=[iotat_buf], writes=[tokid_i_b])
        if stage in ("all", "disp1", "disp2exp"):
            if stage == "disp2exp" and disp_in is not None:
                disp_ap, disp_b = disp_in
            else:
                disp_ap, disp_b = disp_io
            si_use = s1i if stage in ("all", "disp1") else s2i
            si_use_b = s1i_b if stage in ("all", "disp1") else s2i_b
            xtile = [P.sb([128, D], BF16) for _ in range(2)]
            xtb = [Buf() for _ in range(2)]
            for j in range(NT):
                bq = j % 2
                sy.dma(xtile[bq][:], xn_tok[j * 128:(j + 1) * 128, :],
                       reads=[xn_tok_b], writes=[xtb[bq]])
                g.indirect(disp_ap, IndirectOffsetOnAxis(si_use[:, j:j + 1], 0),
                           xtile[bq][:], None, reads=[si_use_b, xtb[bq]],
                           writes=[disp_b], bound=nslot - 1)
                if stage == "all":
                    g.indirect(disp_ap, IndirectOffsetOnAxis(s2i[:, j:j + 1], 0),
                               xtile[bq][:], None, reads=[s2i_b, xtb[bq]],
                               writes=[disp_b], bound=nslot - 1)
        if stage == "disp1":
            return
        if stage in ("comb1", "comb2"):
            _l1_combine_stage(P, ctx, stage, NT, nslot, eout, eout_b,
                              s1i, s1i_b, s2i, s2i_b, w1m, w1m_b, w2m, w2m_b,
                              ffn_out, ffn_out_b, part_io)
            return
        with P.phase("l1exp"):
            warena = P.sb([128, 32768], BF16, name="l1warena")
            w1v = warena[:].rearrange("p (c f) -> p c f", c=8)
            w2v = warena[:].rearrange("p (c f) -> p c f", c=32)
            warena_b = Buf()
            sub = min(512, seg)
            nsub = seg // sub
            nst2 = sub // 128
            harena = P.sb([128, 32, sub], BF16, name="l1harena"); harena_b = Buf()
            dispfm = P.sb([128, 8, sub], BF16, name="l1disp"); dispfm_b = Buf()
            b1sb = P.sb([128, 8, 32], F32); b1sb_b = Buf()
            sy.dma(b1sb[:], wts["L1_ffn_b1"].rearrange("e (c p) -> p e c", p=128),
                   writes=[b1sb_b], allow_slow_non_contiguous=True)
            b2row = P.sb([1, 1024], BF16); b2row_b = Buf()
            smcol = P.sb([128, 1], I32); smcol_b = Buf()
            dtok = [P.sb([128, D], BF16) for _ in range(1)]
            dtok_b = [Buf() for _ in range(1)]
            yev = [P.sb([128, 512], F32) for _ in range(1)]
            yev_b = [Buf() for _ in range(1)]
            for e in range(E):
              for sh in range(nsub):
                s0 = e * seg + sh * sub
                if sh == 0:
                    g.dma(b2row[:], wts["L1_ffn_b2"][e:e + 1, :],
                          reads=[b2row_b], writes=[b2row_b])
                g.dma(w1v, wts["L1_ffn_W1"][e].rearrange("(c p) f -> p c f", p=128),
                      writes=[warena_b])
                for st in range(nst2):
                    db = 0
                    sy.dma(dtok[db][:], disp_ap[s0 + st * 128:s0 + (st + 1) * 128, :],
                           reads=[disp_b], writes=[dtok_b[db]])
                    for dc in range(8):
                        pt = pbk[dc % 2]
                        t.transpose(pt.ap[:, 0:128], dtok[db][:, dc * 128:(dc + 1) * 128],
                                    idb_sb[:], reads=[dtok_b[db], idb_buf], writes=[pt.buf])
                        v.op(lambda bb, pt=pt, dc=dc, st=st: bb.tensor_copy(
                            dispfm[:, dc, st * 128:(st + 1) * 128], pt.ap[:, 0:128]),
                            reads=[pt.buf], writes=[dispfm_b])
                for fc in range(32):
                    ptc = pf[fc % 2]
                    pairs = [(w1v[:, dc, fc * 128:(fc + 1) * 128],
                              dispfm[:, dc, :]) for dc in range(8)]
                    t.matmul_acc(ptc.ap[:, 0:sub], pairs,
                                 reads=[warena_b, dispfm_b], writes=[ptc.buf])
                    s.op(lambda bb, ptc=ptc, fc=fc, e=e: bb.activation(
                        harena[:, fc, :], ptc.ap[:, 0:sub], AF.Gelu,
                        bias=b1sb[:, e, fc:fc + 1]),
                        reads=[ptc.buf, b1sb_b], writes=[harena_b])
                g.dma(w2v, wts["L1_ffn_W2"][e].rearrange("(c p) d -> p c d", p=128),
                      writes=[warena_b])
                for st in range(nst2):
                    for dh in range(2):
                        ptc = pf[2 + dh]
                        pairs = [(harena[:, fc, st * 128:(st + 1) * 128],
                                  w2v[:, fc, dh * 512:(dh + 1) * 512]) for fc in range(32)]
                        pairs.append((onesr_sb[0:1, :], b2row[:, dh * 512:(dh + 1) * 512]))
                        flags = [(i2 == 0, False) for i2 in range(32)] + [(False, True)]
                        t.matmul_acc(ptc.ap[:, 0:512], pairs,
                                     reads=[warena_b, harena_b, b2row_b, onesr_buf],
                                     writes=[ptc.buf], flags=flags)
                        yb = 0
                        v.op(lambda bb, ptc=ptc, yb=yb: bb.tensor_copy(yev[yb][:],
                                                                       ptc.ap[:, 0:512]),
                             reads=[ptc.buf], writes=[yev_b[yb]])
                        sy.dma(eout[s0 + st * 128:s0 + (st + 1) * 128,
                                    dh * 512:(dh + 1) * 512], yev[yb][:],
                               reads=[yev_b[yb]], writes=[eout_b])
        if stage == "disp2exp":
            return
        geo = [P.sb([128, D], F32) for _ in range(2)]
        geo_b = [Buf() for _ in range(2)]
        out_t = [P.sb([128, D], F32) for _ in range(2)]
        out_t_b = [Buf() for _ in range(2)]
        for i in range(NT):
            b2 = i % 2
            v.op(lambda bb, b2=b2: bb.memset(geo[b2][:], 0.0), writes=[geo_b[b2]])
            g.indirect(geo[b2][:], None, eout, IndirectOffsetOnAxis(s1i[:, i:i + 1], 0),
                       reads=[s1i_b, eout_b], writes=[geo_b[b2]], bound=nslot - 1)
            v.op(lambda bb, b2=b2, i=i: bb.tensor_scalar(
                out_t[b2][:], geo[b2][:], w1m[:, i:i + 1], None, ALU.mult),
                reads=[geo_b[b2], w1m_b], writes=[out_t_b[b2]])
            v.op(lambda bb, b2=b2: bb.memset(geo[b2][:], 0.0), writes=[geo_b[b2]])
            g.indirect(geo[b2][:], None, eout, IndirectOffsetOnAxis(s2i[:, i:i + 1], 0),
                       reads=[s2i_b, eout_b], writes=[geo_b[b2]], bound=nslot - 1)
            v.op(lambda bb, b2=b2, i=i: bb.scalar_tensor_tensor(
                out_t[b2][:], geo[b2][:], w2m[:, i:i + 1], out_t[b2][:],
                ALU.mult, ALU.add),
                reads=[geo_b[b2], w2m_b, out_t_b[b2]], writes=[out_t_b[b2]])
            sy.dma(ffn_out[i * 128:(i + 1) * 128, :], out_t[b2][:],
                   reads=[out_t_b[b2]], writes=[ffn_out_b])


def _l1_combine_stage(P, ctx, stage, NT, nslot, eout, eout_b, s1i, s1i_b,
                      s2i, s2i_b, w1m, w1m_b, w2m, w2m_b, ffn_out, ffn_out_b,
                      part_io):
    v, sy, g = P.vector, P.sync, P.gpsimd
    part_ap, part_b = part_io
    si, si_b = (s1i, s1i_b) if stage == "comb1" else (s2i, s2i_b)
    wm, wm_b = (w1m, w1m_b) if stage == "comb1" else (w2m, w2m_b)
    geo = [P.sb([128, D], F32) for _ in range(2)]
    geo_b = [Buf() for _ in range(2)]
    out_t = [P.sb([128, D], F32) for _ in range(2)]
    out_t_b = [Buf() for _ in range(2)]
    pt_t = [P.sb([128, D], F32) for _ in range(2)]
    pt_b = [Buf() for _ in range(2)]
    for i in range(NT):
        b2 = i % 2
        v.op(lambda bb, b2=b2: bb.memset(geo[b2][:], 0.0), writes=[geo_b[b2]])
        g.indirect(geo[b2][:], None, eout, IndirectOffsetOnAxis(si[:, i:i + 1], 0),
                   reads=[si_b, eout_b], writes=[geo_b[b2]], bound=nslot - 1)
        v.op(lambda bb, b2=b2, i=i: bb.tensor_scalar(
            out_t[b2][:], geo[b2][:], wm[:, i:i + 1], None, ALU.mult),
            reads=[geo_b[b2], wm_b], writes=[out_t_b[b2]])
        if stage == "comb2":
            sy.dma(pt_t[b2][:], part_ap[i * 128:(i + 1) * 128, :],
                   reads=[part_b], writes=[pt_b[b2]])
            v.op(lambda bb, b2=b2: bb.tensor_add(out_t[b2][:], out_t[b2][:],
                                                 pt_t[b2][:]),
                 reads=[pt_b[b2], out_t_b[b2]], writes=[out_t_b[b2]])
            sy.dma(ffn_out[i * 128:(i + 1) * 128, :], out_t[b2][:],
                   reads=[out_t_b[b2]], writes=[ffn_out_b])
        else:
            sy.dma(part_ap[i * 128:(i + 1) * 128, :], out_t[b2][:],
                   reads=[out_t_b[b2]], writes=[part_b])


def emit_rlog_fm(P, ctx, w_sb, w_buf, xn_fm_sb, xn_fm_buf, rlog_sb, rlog_buf):
    NT = NTILE
    pf = ctx["pf"]
    with P.phase("rlogfm"):
        for i in range(NT):
            pt = pf[i % 2]
            pairs = [(xn_fm_sb[:, dc, i * 128:(i + 1) * 128], w_sb[:, dc, :])
                     for dc in range(8)]
            P.tensor.matmul_acc(pt.ap[0:128, 0:8], pairs,
                                reads=[xn_fm_buf, w_buf], writes=[pt.buf])
            P.vector.op(lambda bb, pt=pt, i=i: bb.tensor_copy(
                rlog_sb[:, i, :], pt.ap[0:128, 0:8]),
                reads=[pt.buf], writes=[rlog_buf])


def emit_l2log_fm(P, ctx, ctl_sb, ctl_buf, xn_fm_sb, xn_fm_buf,
                  l2log_hbm, l2log_buf):
    NT = NTILE
    pf = ctx["pf"]
    with P.phase("l2logfm"):
        stg = P.sb([64, 128], F32); stg_b = Buf()
        for i in range(NT):
            pt = pf[i % 2]
            pairs = [(ctl_sb[:, dc, :], xn_fm_sb[:, dc, i * 128:(i + 1) * 128])
                     for dc in range(8)]
            P.tensor.matmul_acc(pt.ap[0:64, 0:128], pairs,
                                reads=[xn_fm_buf, ctl_buf], writes=[pt.buf])
            P.vector.op(lambda bb, pt=pt: bb.tensor_copy(stg[:], pt.ap[0:64, 0:128]),
                        reads=[pt.buf], writes=[stg_b])
            P.sync.dma(l2log_hbm[:, i * 128:(i + 1) * 128], stg[:],
                       reads=[stg_b], writes=[l2log_buf])


def emit_l2_hook(P, ctx, ctl_sb, ctl_buf, l2log_hbm, l2log_buf, stg, stg_b):
    def hook(P, ctx, i, dc, fchunk, fcb):
        pt = ctx["pf"][4]
        P.tensor.matmul_acc(pt.ap[0:64, 0:128], [(ctl_sb[:, dc, :], fchunk[:])],
                            reads=[fcb, ctl_buf], writes=[pt.buf])
        if dc == 0:
            P.vector.op(lambda bb, pt=pt: bb.tensor_copy(stg[:], pt.ap[0:64, 0:128]),
                        reads=[pt.buf], writes=[stg_b])
        else:
            P.vector.op(lambda bb, pt=pt: bb.tensor_add(
                stg[:], stg[:], pt.ap[0:64, 0:128]),
                reads=[pt.buf, stg_b], writes=[stg_b])
        if dc == 7:
            P.sync.dma(l2log_hbm[:, i * 128:(i + 1) * 128], stg[:],
                       reads=[stg_b], writes=[l2log_buf])
    return hook


def emit_l2_attn(P, ctx, wts, xn_fm_sb, xn_fm_buf, attn_out, attn_out_b):
    import math
    v, s, t, sy, g = P.vector, P.scalar, P.tensor, P.sync, P.gpsimd
    NT = NTILE
    Tt = NT * 128
    gam = (1.0 - np.exp(np.linspace(math.log(1 / 32), math.log(1 / 512), 8)))
    idf_sb, idf_buf = cref(ctx, "ident_f32")
    idb_sb, idb_buf = cref(ctx, "ident_bf")
    gp_sb, gp_buf = cref(ctx, "gamma_pow")
    gn_sb, gn_buf = cref(ctx, "gamma_npow")
    mask_sb, mask_buf = cref(ctx, "mask_diag")
    onesr_sb, onesr_buf = cref(ctx, "ones_row")
    onesc_sb, onesc_buf = cref(ctx, "ones_col")
    epssb, epsbuf = cref(ctx, "eps_cols")
    pf = ctx["pf"]; pbk = ctx["pb"]
    with P.phase("l2attn"):
        gnw_sb, gnw_b = bias_cols(P, ctx, "l2gnw", wts["L2_attn_gn_weight"], D)
        gnb_sb, gnb_b = bias_cols(P, ctx, "l2gnb", wts["L2_attn_gn_bias"], D)
        ysb = []
        for h in range(8):
            ysb.append((P.sb([128, NT, 128], BF16, name=f"l2y{h}"), Buf()))
        for h in range(8):
            _l2_head(P, ctx, wts, h, float(gam[h]), xn_fm_sb, xn_fm_buf, ysb[h],
                     gnw_sb, gnw_b, gnb_sb, gnb_b)
        with P.phase("l2wo"):
            woT = load_weight_T(P, "l2wo", wts["L2_attn_Wo"], D, D)
            evq = P.sb([128, 512], F32); evq_b = Buf()
            ot = [P.sb([128, D], F32) for _ in range(4)]
            ot_b = [Buf() for _ in range(4)]
            for tt in range(Tt // 512):
                for oc in range(8):
                    pt = pf[oc % 2]
                    pairs = [(woT.lhsT(hc, oc * 128, (oc + 1) * 128),
                              ysb[hc][0][:].rearrange("p n c -> p (n c)")
                              [:, tt * 512:(tt + 1) * 512]) for hc in range(8)]
                    t.matmul_acc(pt.ap[:, 0:512], pairs,
                                 reads=[woT.buf] + [ysb[hc][1] for hc in range(8)],
                                 writes=[pt.buf])
                    s.op(lambda bb, pt=pt: bb.copy(evq[:], pt.ap[:, 0:512]),
                         reads=[pt.buf], writes=[evq_b])
                    for tsub in range(4):
                        pt2 = pf[4]
                        t.transpose(pt2.ap[:, 0:128], evq[:, tsub * 128:(tsub + 1) * 128],
                                    idf_sb[:], reads=[evq_b, idf_buf], writes=[pt2.buf])
                        v.op(lambda bb, pt2=pt2, oc=oc, tsub=tsub: bb.tensor_copy(
                            ot[tsub][:, oc * 128:(oc + 1) * 128], pt2.ap[:, 0:128]),
                            reads=[pt2.buf], writes=[ot_b[tsub]])
                        if oc == 7:
                            ti = tt * 4 + tsub
                            sy.dma(attn_out[ti * 128:(ti + 1) * 128, :], ot[tsub][:],
                                   reads=[ot_b[tsub]], writes=[attn_out_b])


def _l2_head(P, ctx, wts, h, gamh, xn_fm_sb, xn_fm_buf, ysbh, gnw_sb, gnw_b,
             gnb_sb, gnb_b):
    v, s, t, sy, g = P.vector, P.scalar, P.tensor, P.sync, P.gpsimd
    NT = NTILE
    Tt = NT * 128
    idb_sb, idb_buf = cref(ctx, "ident_bf")
    gp_sb, gp_buf = cref(ctx, "gamma_pow")
    gn_sb, gn_buf = cref(ctx, "gamma_npow")
    mask_sb, mask_buf = cref(ctx, "mask_diag")
    onesr_sb, onesr_buf = cref(ctx, "ones_row")
    onesc_sb, onesc_buf = cref(ctx, "ones_col")
    epssb, epsbuf = cref(ctx, "eps_cols")
    pf = ctx["pf"]; pbk = ctx["pb"]
    ysb_t, ysb_b = ysbh
    with P.phase(f"l2h{h}"):
        qtok = P.sb([128, NT, 128], BF16); qtok_b = Buf()
        ktok = P.sb([128, NT, 128], BF16); ktok_b = Buf()
        vtok = P.sb([128, NT, 128], BF16); vtok_b = Buf()
        qfm = P.sb([128, Tt], BF16); qfm_b = Buf()
        kfm = P.sb([128, Tt], BF16); kfm_b = Buf()
        k2s = P.sb([128, 128], BF16); k2s_b = Buf()
        ptl = P.sb([128, 128], BF16); ptl_b = Buf()
        Sf = P.sb([128, 128], F32); Sf_b = Buf()
        Sbf = P.sb([128, 128], BF16); Sbf_b = Buf()
        sq = P.sb([128, 512], BF16); sq_b = Buf()
        stat = [P.sb([1, 1], F32) for _ in range(4)]
        stat_b = Buf()
        mcol = P.sb([128, 1], F32); mcol_b = Buf()
        rcol = P.sb([128, 1], F32); rcol_b = Buf()
        for wkey, dst, dstb, scale_ap, scale_buf in (
                ("L2_attn_Wq", qtok, qtok_b, gp_sb, gp_buf),
                ("L2_attn_Wk", ktok, ktok_b, gn_sb, gn_buf),
                ("L2_attn_Wv", vtok, vtok_b, None, None)):
            wT = load_weight_T_cols(P, f"l2{wkey[-2:]}{h}", wts[wkey], D, D,
                                    h * 128, (h + 1) * 128)
            for ti in range(NT):
                pt = pf[ti % 2]
                pairs = [(xn_fm_sb[:, dc, ti * 128:(ti + 1) * 128],
                          wT.sb[:, dc, 0:128]) for dc in range(8)]
                t.matmul_acc(pt.ap[:, 0:128], pairs, reads=[wT.buf, xn_fm_buf],
                             writes=[pt.buf])
                if scale_ap is not None:
                    v.op(lambda bb, pt=pt, dst=dst, ti=ti, scale_ap=scale_ap:
                         bb.tensor_scalar(dst[:, ti, :], pt.ap[:, 0:128],
                                          scale_ap[:, h:h + 1], None, ALU.mult),
                         reads=[pt.buf, scale_buf], writes=[dstb])
                else:
                    v.op(lambda bb, pt=pt, dst=dst, ti=ti: bb.tensor_copy(
                        dst[:, ti, :], pt.ap[:, 0:128]),
                        reads=[pt.buf], writes=[dstb])
        for ti in range(NT):
            for src_t, src_b2, dst, dstb in ((qtok, qtok_b, qfm, qfm_b),
                                             (ktok, ktok_b, kfm, kfm_b)):
                pt = pbk[ti % 2]
                t.transpose(pt.ap[:, 0:128], src_t[:, ti, :], idb_sb[:],
                            reads=[src_b2, idb_buf], writes=[pt.buf])
                v.op(lambda bb, pt=pt, dst=dst, ti=ti: bb.tensor_copy(
                    dst[:, ti * 128:(ti + 1) * 128], pt.ap[:, 0:128]),
                    reads=[pt.buf], writes=[dstb])
        v.op(lambda bb: bb.memset(Sf[:], 0.0), writes=[Sf_b])
        v.op(lambda bb: bb.memset(Sbf[:], 0.0), writes=[Sbf_b])
        g128 = float(gamh ** 128)
        g1 = float(gamh)
        gLm1 = float(gamh ** 127)
        for c in range(NT):
            pA = pf[2]
            t.matmul_acc(pA.ap[0:128, 0:128],
                         [(kfm[:, c * 128:(c + 1) * 128], qfm[:, c * 128:(c + 1) * 128])],
                         reads=[kfm_b, qfm_b], writes=[pA.buf])
            v.op(lambda bb, pA=pA: bb.tensor_tensor(ptl[:], pA.ap[0:128, 0:128],
                                                    mask_sb[:], ALU.mult),
                 reads=[pA.buf, mask_buf], writes=[ptl_b])
            py = pf[3]
            t.matmul_acc(py.ap[0:128, 0:128],
                         [(vtok[:, c, :], ptl[:]), (Sbf[:], qfm[:, c * 128:(c + 1) * 128])],
                         reads=[vtok_b, ptl_b, Sbf_b, qfm_b], writes=[py.buf],
                         flags=[(True, False), (False, True)])
            v.op(lambda bb, py=py, c=c: bb.tensor_copy(ysb_t[:, c, :], py.ap[0:128, 0:128]),
                 reads=[py.buf], writes=[ysb_b])
            v.op(lambda bb, c=c: bb.tensor_scalar_mul(k2s[:], ktok[:, c, :], gLm1),
                 reads=[ktok_b], writes=[k2s_b])
            pS = pf[5]
            t.matmul_acc(pS.ap[0:128, 0:128], [(k2s[:], vtok[:, c, :])],
                         reads=[k2s_b, vtok_b], writes=[pS.buf])
            v.op(lambda bb: bb.tensor_scalar_mul(Sf[:], Sf[:], g128),
                 reads=[Sf_b], writes=[Sf_b])
            v.op(lambda bb, pS=pS: bb.tensor_add(Sf[:], Sf[:], pS.ap[0:128, 0:128]),
                 reads=[pS.buf, Sf_b], writes=[Sf_b])
            v.op(lambda bb: bb.tensor_scalar_mul(Sbf[:], Sf[:], g1),
                 reads=[Sf_b], writes=[Sbf_b])
        yview = ysb_t[:].rearrange("p n c -> p (n c)")
        pst = pf[2]
        for cc in range(Tt // 512):
            t.matmul_acc(pst.ap[0:1, 0:512], [(onesc_sb[:], yview[:, cc * 512:(cc + 1) * 512])],
                         reads=[ysb_b, onesc_buf], writes=[pst.buf],
                         flags=[(cc == 0, cc == Tt // 512 - 1)])
        srow = P.sb([1, 512], F32); srow_b = Buf()
        s.op(lambda bb: bb.copy(srow[:], pst.ap[0:1, 0:512]),
             reads=[pst.buf], writes=[srow_b])
        v.op(lambda bb: bb.tensor_reduce(stat[0][:], srow[:], AX.X, ALU.add),
             reads=[srow_b], writes=[stat_b])
        psq = pf[3]
        for cc in range(Tt // 512):
            s.op(lambda bb, cc=cc: bb.activation(sq[:], yview[:, cc * 512:(cc + 1) * 512],
                                                 AF.Square),
                 reads=[ysb_b], writes=[sq_b])
            t.matmul_acc(psq.ap[0:1, 0:512], [(onesc_sb[:], sq[:])],
                         reads=[sq_b, onesc_buf], writes=[psq.buf],
                         flags=[(cc == 0, cc == Tt // 512 - 1)])
        s.op(lambda bb: bb.copy(srow[:], psq.ap[0:1, 0:512]),
             reads=[psq.buf], writes=[srow_b])
        v.op(lambda bb: bb.tensor_reduce(stat[1][:], srow[:], AX.X, ALU.add),
             reads=[srow_b], writes=[stat_b])
        n = float(128 * Tt)
        v.op(lambda bb: bb.tensor_scalar_mul(stat[0][:], stat[0][:], 1.0 / n),
             reads=[stat_b], writes=[stat_b])
        v.op(lambda bb: bb.tensor_scalar(stat[2][:], stat[1][:], 1.0 / n, None, ALU.mult),
             reads=[stat_b], writes=[stat_b])
        v.op(lambda bb: bb.tensor_tensor(stat[3][:], stat[0][:], stat[0][:], ALU.mult),
             reads=[stat_b], writes=[stat_b])
        v.op(lambda bb: bb.tensor_sub(stat[2][:], stat[2][:], stat[3][:]),
             reads=[stat_b], writes=[stat_b])
        s.op(lambda bb: bb.activation(stat[2][:], stat[2][:], AF.Ln,
                                      bias=epssb[0:1, 0:1]),
             reads=[stat_b, epsbuf], writes=[stat_b])
        s.op(lambda bb: bb.activation(stat[2][:], stat[2][:], AF.Exp, scale=-0.5),
             reads=[stat_b], writes=[stat_b])
        for st_i, col, colb in ((0, mcol, mcol_b), (2, rcol, rcol_b)):
            pc = pf[4]
            srow2 = P.sb([1, 1], BF16)
            srow2_b = Buf()
            v.op(lambda bb, st_i=st_i, srow2=srow2: bb.tensor_copy(srow2[:], stat[st_i][:]),
                 reads=[stat_b], writes=[srow2_b])
            t.matmul_acc(pc.ap[0:128, 0:1], [(onesr_sb[:, 0:128], srow2[:])],
                         reads=[srow2_b, onesr_buf], writes=[pc.buf])
            s.op(lambda bb, pc=pc, col=col: bb.copy(col[:], pc.ap[0:128, 0:1]),
                 reads=[pc.buf], writes=[colb])
        for cc in range(Tt // 512):
            yv = yview[:, cc * 512:(cc + 1) * 512]
            v.op(lambda bb, yv=yv: bb.tensor_scalar(yv, yv, mcol[:], None, ALU.subtract),
                 reads=[mcol_b], writes=[ysb_b])
            v.op(lambda bb, yv=yv: bb.tensor_scalar(yv, yv, rcol[:], None, ALU.mult),
                 reads=[rcol_b], writes=[ysb_b])
            v.op(lambda bb, yv=yv, h=h: bb.tensor_scalar(yv, yv, gnw_sb[:, h:h + 1],
                                                         None, ALU.mult),
                 reads=[gnw_b], writes=[ysb_b])
            v.op(lambda bb, yv=yv, h=h: bb.tensor_scalar(yv, yv, gnb_sb[:, h:h + 1],
                                                         None, ALU.add),
                 reads=[gnb_b], writes=[ysb_b])


def emit_l2_ffn(P, ctx, wts, xn_tok, ffn_out, l2log_hbm, l2log_hbm_b):
    v, s, t, sy, g = P.vector, P.scalar, P.tensor, P.sync, P.gpsimd
    NT = NTILE
    Tt = NT * 128
    NG = Tt // 16
    GB = 32
    NBAT = NG // GB
    xn_tok, xn_tok_b = xn_tok
    ffn_out, ffn_out_b = ffn_out
    idb_sb, idb_buf = cref(ctx, "ident_bf")
    tie_sb, tie_buf = cref(ctx, "tie64")
    bd_sb, bd_buf = cref(ctx, "blockdiag16")
    pf = ctx["pf"]; pbk = ctx["pb"]
    inter_hbm = P.dscratch("l2_inter", [NG * 64, D], BF16); inter_b = Buf()
    with P.phase("l2ffn"):
        perm = P.sb([64, Tt], BF16); perm_b = Buf()
        with P.phase("l2perm"):
            l2log_sb = P.sb([64, Tt], F32, name="l2log_loc"); l2log_buf = Buf()
            sy.dma(l2log_sb[:], l2log_hbm, reads=[l2log_hbm_b], writes=[l2log_buf])
            mx = P.sb([64, NG], F32); mx_b = Buf()
            lv = l2log_sb[:].rearrange("p (g k) -> p g k", k=16)
            v.op(lambda bb: bb.tensor_tensor(
                lv, lv, tie_sb[:].rearrange("p (o k) -> p o k", o=1)
                .broadcast_to([64, NG, 16]), ALU.add),
                reads=[tie_buf, l2log_buf], writes=[l2log_buf])
            v.op(lambda bb: bb.tensor_reduce(mx[:], lv, AX.X, ALU.max),
                 reads=[l2log_buf], writes=[mx_b])
            v.op(lambda bb: bb.tensor_tensor(
                perm[:].rearrange("p (g k) -> p g k", k=16), lv,
                mx[:].rearrange("p (g o) -> p g o", o=1).broadcast_to([64, NG, 16]),
                ALU.is_ge),
                reads=[l2log_buf, mx_b], writes=[perm_b])
        bat_ph = P.phase("l2bat"); bat_ph.__enter__()
        f1sb = P.sb([128, 8, 4096], BF16, name="l2f1"); f1_b = Buf()
        g.dma(f1sb[:], wts["L2_ffn_f1"].rearrange("(c p) e s f -> p c (e s f)", p=128),
              writes=[f1_b])
        bsb = P.sb([64, 64], F32); bsb_b = Buf()
        sy.dma(bsb[:], wts["L2_ffn_bias"].rearrange("e s f -> f (e s)"),
               writes=[bsb_b], allow_slow_non_contiguous=True)
        f2sl = P.sb([64, 1024], BF16); f2sl_b = Buf()
        selb = P.sb([128, 8, 2048], BF16, name="l2selb"); selb_b = Buf()
        inner = P.sb([64, 2048], BF16, name="l2inner"); inner_b = Buf()
        xt = P.sb([128, D], BF16); xt_b = Buf()
        phat = [P.sb([128, 512], BF16) for _ in range(2)]
        phat_b = [Buf() for _ in range(2)]
        permT = P.sb([128, 64], BF16); permT_b = Buf()
        iev = P.sb([32, 1024], F32, name="l2iev"); iev_b = Buf()
        for bat in range(NBAT):
            for tc in range(4):
                ti = bat * 4 + tc
                db = tc % 2
                pt = pbk[0]
                t.transpose(pt.ap[:, 0:64], perm[:, ti * 128:(ti + 1) * 128],
                            idb_sb[0:64, 0:64],
                            reads=[perm_b, idb_buf], writes=[pt.buf])
                v.op(lambda bb, pt=pt: bb.tensor_copy(permT[:], pt.ap[0:128, 0:64]),
                     reads=[pt.buf], writes=[permT_b])
                v.op(lambda bb, db=db: bb.tensor_tensor(
                    phat[db][:].rearrange("p (g e) -> p g e", g=8),
                    permT[:].rearrange("p (o e) -> p o e", o=1).broadcast_to([128, 8, 64]),
                    bd_sb[:].rearrange("p (g o) -> p g o", o=1).broadcast_to([128, 8, 64]),
                    ALU.mult),
                    reads=[permT_b, bd_buf], writes=[phat_b[db]])
                sy.dma(xt[:], xn_tok[ti * 128:(ti + 1) * 128, :],
                       reads=[xn_tok_b], writes=[xt_b])
                for dc in range(8):
                    pt2 = pf[dc % 2]
                    t.matmul_acc(pt2.ap[:, 0:512],
                                 [(xt[:, dc * 128:(dc + 1) * 128], phat[db][:])],
                                 reads=[xt_b, phat_b[db]], writes=[pt2.buf])
                    v.op(lambda bb, pt2=pt2, dc=dc, tc=tc: bb.tensor_copy(
                        selb[:, dc, :].rearrange("p (e c) -> p e c", e=64)
                        [:, :, tc * 8:tc * 8 + 8],
                        pt2.ap[:, 0:512].rearrange("p (g e) -> p e g", g=8)),
                        reads=[pt2.buf], writes=[selb_b])
            for es in range(64):
                pt3 = pf[2 + es % 2]
                pairs = [(f1sb[:, dc, es * 64:(es + 1) * 64],
                          selb[:, dc, es * 32:(es + 1) * 32]) for dc in range(8)]
                t.matmul_acc(pt3.ap[0:64, 0:32], pairs, reads=[f1_b, selb_b],
                             writes=[pt3.buf])
                s.op(lambda bb, pt3=pt3, es=es: bb.activation(
                    inner[:, es * 32:(es + 1) * 32], pt3.ap[0:64, 0:32], AF.Relu,
                    bias=bsb[:, es:es + 1]),
                    reads=[pt3.buf, bsb_b], writes=[inner_b])
            for es in range(64):
                g.dma(f2sl[:], wts["L2_ffn_f2"].rearrange("e s f d -> (e s) f d")[es],
                      writes=[f2sl_b])
                pt4a, pt4b = pf[4], pf[5]
                t.matmul_acc(pt4a.ap[0:32, 0:512],
                             [(inner[:, es * 32:(es + 1) * 32], f2sl[:, 0:512])],
                             reads=[inner_b, f2sl_b], writes=[pt4a.buf])
                t.matmul_acc(pt4b.ap[0:32, 0:512],
                             [(inner[:, es * 32:(es + 1) * 32], f2sl[:, 512:1024])],
                             reads=[inner_b, f2sl_b], writes=[pt4b.buf])
                s.op(lambda bb, pt4a=pt4a: bb.copy(iev[:, 0:512], pt4a.ap[0:32, 0:512]),
                     reads=[pt4a.buf], writes=[iev_b])
                s.op(lambda bb, pt4b=pt4b: bb.copy(iev[:, 512:1024], pt4b.ap[0:32, 0:512]),
                     reads=[pt4b.buf], writes=[iev_b])
                g.dma(inter_hbm.rearrange("(r s) d -> r s d", s=64)
                      [bat * 32:(bat + 1) * 32, es, :], iev[:],
                      reads=[iev_b], writes=[inter_b])
        bat_ph.__exit__(None, None, None)
        shat = P.sb([128, 128], BF16); shat_b = Buf()
        irows = P.sb([128, 4, D], BF16, name="l2irows"); irows_b = Buf()
        oev = P.sb([128, D], F32); oev_b = Buf()
        for ti in range(NT):
            pt = pbk[0]
            t.transpose(pt.ap[:, 0:64], perm[:, ti * 128:(ti + 1) * 128],
                        idb_sb[0:64, 0:64],
                        reads=[perm_b, idb_buf], writes=[pt.buf])
            v.op(lambda bb, pt=pt: bb.tensor_copy(permT[:], pt.ap[0:128, 0:64]),
                 reads=[pt.buf], writes=[permT_b])
            v.op(lambda bb: bb.tensor_tensor(
                phat[0][:].rearrange("p (g e) -> p g e", g=8),
                permT[:].rearrange("p (o e) -> p o e", o=1).broadcast_to([128, 8, 64]),
                bd_sb[:].rearrange("p (g o) -> p g o", o=1).broadcast_to([128, 8, 64]),
                ALU.mult),
                reads=[permT_b, bd_buf], writes=[phat_b[0]])
            sy.dma(irows[:], inter_hbm[ti * 512:(ti + 1) * 512, :]
                   .rearrange("(c p) d -> p c d", p=128), reads=[inter_b],
                   writes=[irows_b])
            for dh in range(2):
                pt5 = pf[dh]
                pairs = []
                for kc in range(4):
                    pt6 = pbk[1]
                    t.transpose(pt6.ap[:, 0:128], phat[0][:, kc * 128:(kc + 1) * 128],
                                idb_sb[:], reads=[phat_b[0], idb_buf], writes=[pt6.buf])
                    v.op(lambda bb, pt6=pt6: bb.tensor_copy(shat[:], pt6.ap[0:128, 0:128]),
                         reads=[pt6.buf], writes=[shat_b])
                    t.matmul_acc(pt5.ap[:, 0:512],
                                 [(shat[:], irows[:, kc, dh * 512:(dh + 1) * 512])],
                                 reads=[shat_b, irows_b], writes=[pt5.buf],
                                 flags=[(kc == 0, kc == 3)])
                v.op(lambda bb, pt5=pt5, dh=dh: bb.tensor_copy(
                    oev[:, dh * 512:(dh + 1) * 512], pt5.ap[:, 0:512]),
                    reads=[pt5.buf], writes=[oev_b])
            sy.dma(ffn_out[ti * 128:(ti + 1) * 128, :], oev[:],
                   reads=[oev_b], writes=[ffn_out_b])


def emit_l3_attn(P, ctx, wts, xn_fm_sb, xn_fm_buf, xn_tok, attn_out, attn_out_b):
    v, s, t, sy, g = P.vector, P.scalar, P.tensor, P.sync, P.gpsimd
    NT = NTILE
    Tt = NT * 128
    NC = Tt // 32
    idf_sb, idf_buf = cref(ctx, "ident_f32")
    idb_sb, idb_buf = cref(ctx, "ident_bf")
    onesr_sb, onesr_buf = cref(ctx, "ones_row")
    onesc_sb, onesc_buf = cref(ctx, "ones_col")
    epssb, epsbuf = cref(ctx, "eps_cols")
    m32_sb, m32_buf = cref(ctx, "mask_strict32")
    hsel_sb, hsel_buf = cref(ctx, "headsel")
    pf = ctx["pf"]; pbk = ctx["pb"]
    xn_tok_ap, xn_tok_b = xn_tok
    dd_hbm = P.dscratch("l3_dd", [6, D, Tt], BF16); dd_b = Buf()
    kvrwu = P.dscratch("l3_proj", [5, D, Tt], BF16); kvrwu_b = Buf()
    rwkv_hbm = P.dscratch("l3_rwkv", [D, Tt], BF16); rwkv_b = Buf()
    gate_hbm = P.dscratch("l3_gate", [D, Tt], BF16); gate_b = Buf()
    a_hbm = P.dscratch("l3_abuf", [64, 32], BF16); a_hbm_b = Buf()
    zfm_hbm = P.dscratch("l3_zfm", [D, Tt], BF16); zfm_b = Buf()
    dfm_hbm = P.dscratch("l3_dfm", [D, Tt], BF16); dfm_b = Buf()
    with P.phase("l3shift"):
        mur = repl_row(P, ctx, "l3mu", wts["L3_attn_mu_x"], D)
        xt = P.sb([128, D], BF16); xtb = Buf()
        lt = P.sb([128, D], BF16); ltb = Buf()
        df = P.sb([128, D], F32); df_b = Buf()
        mt = P.sb([128, D], F32); mt_b = Buf()
        zt = P.sb([128, D], BF16); zt_b = Buf()
        dfb = P.sb([128, D], BF16); dfb_b = Buf()
        stg = P.sb([128, 128], BF16); stg_b = Buf()
        for i in range(NT):
            sy.dma(xt[:], xn_tok_ap[i * 128:(i + 1) * 128, :],
                   reads=[xn_tok_b], writes=[xtb])
            if i == 0:
                v.op(lambda bb: bb.memset(lt[0:64, :], 0.0), writes=[ltb])
                sy.dma(lt[1:128, :], xn_tok_ap[0:127, :],
                       reads=[xn_tok_b], writes=[ltb])
            else:
                sy.dma(lt[:], xn_tok_ap[i * 128 - 1:(i + 1) * 128 - 1, :],
                       reads=[xn_tok_b], writes=[ltb])
            v.op(lambda bb: bb.tensor_sub(df[:], lt[:], xt[:]),
                 reads=[xtb, ltb], writes=[df_b])
            v.op(lambda bb: bb.tensor_mul(mt[:], df[:], mur[0][:]),
                 reads=[df_b, mur[1]], writes=[mt_b])
            v.op(lambda bb: bb.tensor_add(mt[:], mt[:], xt[:]),
                 reads=[mt_b, xtb], writes=[mt_b])
            v.op(lambda bb: bb.tensor_mul(mt[:], mt[:], df[:]),
                 reads=[mt_b, df_b], writes=[mt_b])
            v.op(lambda bb: bb.tensor_add(zt[:], mt[:], xt[:]),
                 reads=[mt_b, xtb], writes=[zt_b])
            v.op(lambda bb: bb.tensor_copy(dfb[:], df[:]),
                 reads=[df_b], writes=[dfb_b])
            for src_t, src_b2, dsthbm, dsthbm_b in ((zt, zt_b, zfm_hbm, zfm_b),
                                                    (dfb, dfb_b, dfm_hbm, dfm_b)):
                for dc in range(8):
                    pt = pbk[dc % 2]
                    t.transpose(pt.ap[:, 0:128], src_t[:, dc * 128:(dc + 1) * 128],
                                idb_sb[:], reads=[src_b2, idb_buf], writes=[pt.buf])
                    v.op(lambda bb, pt=pt: bb.tensor_copy(stg[:], pt.ap[:, 0:128]),
                         reads=[pt.buf], writes=[stg_b])
                    sy.dma(dsthbm[dc * 128:(dc + 1) * 128, i * 128:(i + 1) * 128],
                           stg[:], reads=[stg_b], writes=[dsthbm_b])
    with P.phase("l3lora"):
        aw_sb = P.sb([128, 8, 6, 32], BF16); aw_b = Buf()
        for gi in range(6):
            g.dma(aw_sb[:, :, gi, :], wts["L3_attn_lora_A_w"][gi]
                  .rearrange("(c p) r -> p c r", p=128), writes=[aw_b])
        bw_sb = P.sb([32, 6, D], BF16); bw_b = Buf()
        g.dma(bw_sb[:], wts["L3_attn_lora_B_w"].rearrange("g r e -> r g e"),
              writes=[bw_b], allow_slow_non_contiguous=True)
        ab_sb = P.sb([32, 6], F32); ab_b = Buf()
        sy.dma(ab_sb[:], wts["L3_attn_lora_A_b"].rearrange("g r -> r g"),
               writes=[ab_b], allow_slow_non_contiguous=True)
        lam_sb = P.sb([128, 6, 8], F32); lam_b = Buf()
        sy.dma(lam_sb[:], wts["L3_attn_lora_lambda"].rearrange("g (c p) -> p g c", p=128),
               writes=[lam_b], allow_slow_non_contiguous=True)
        bb_sb = P.sb([128, 6, 8], F32); bbias_b = Buf()
        sy.dma(bb_sb[:], wts["L3_attn_lora_B_b"].rearrange("g (c p) -> p g c", p=128),
               writes=[bbias_b], allow_slow_non_contiguous=True)
        zch = P.sb([128, 8, 512], BF16); zch_b = Buf()
        dch = P.sb([128, 8, 512], BF16); dch_b = Buf()
        hdn = P.sb([32, 512], BF16); hdn_b = Buf()
        ddt = P.sb([128, 512], F32); ddt_b = Buf()
        for ts in range(Tt // 512):
            sy.dma(zch[:], zfm_hbm.rearrange("(c p) t -> p c t", p=128)
                   [:, :, ts * 512:(ts + 1) * 512], reads=[zfm_b], writes=[zch_b])
            sy.dma(dch[:], dfm_hbm.rearrange("(c p) t -> p c t", p=128)
                   [:, :, ts * 512:(ts + 1) * 512], reads=[dfm_b], writes=[dch_b])
            for gi in range(6):
                ph = pf[4]
                pairs = [(aw_sb[:, dc, gi, :], zch[:, dc, :])
                         for dc in range(8)]
                t.matmul_acc(ph.ap[0:32, 0:512], pairs, reads=[aw_b, zch_b],
                             writes=[ph.buf])
                s.op(lambda bb, ph=ph, gi=gi: bb.activation(
                    hdn[:], ph.ap[0:32, 0:512], AF.Tanh, bias=ab_sb[:, gi:gi + 1]),
                    reads=[ph.buf, ab_b], writes=[hdn_b])
                for ec in range(8):
                    pl = pf[ec % 2]
                    t.matmul_acc(pl.ap[:, 0:512],
                                 [(bw_sb[:, gi, ec * 128:(ec + 1) * 128], hdn[:])],
                                 reads=[bw_b, hdn_b], writes=[pl.buf])
                    v.op(lambda bb, pl=pl, gi=gi, ec=ec: bb.tensor_scalar(
                        ddt[:], pl.ap[:, 0:512], lam_sb[:, gi, ec:ec + 1],
                        bb_sb[:, gi, ec:ec + 1], ALU.add, ALU.add),
                        reads=[pl.buf, lam_b, bbias_b], writes=[ddt_b])
                    v.op(lambda bb, ec=ec: bb.tensor_mul(ddt[:], ddt[:], dch[:, ec, :]),
                         reads=[ddt_b, dch_b], writes=[ddt_b])
                    v.op(lambda bb, ec=ec, ts=ts: bb.tensor_tensor(
                        ddt[:], ddt[:], xn_fm_sb[:, ec, ts * 512:(ts + 1) * 512],
                        ALU.add),
                        reads=[ddt_b, xn_fm_buf], writes=[ddt_b])
                    g.dma(dd_hbm[gi, ec * 128:(ec + 1) * 128, ts * 512:(ts + 1) * 512],
                          ddt[:], reads=[ddt_b], writes=[dd_b])
    with P.phase("l3proj"):
        mhd_sb = P.sb([64, 5, 16, 64], BF16); mhd_b = Buf()
        g.dma(mhd_sb[:], wts["L3_attn_mhd_w"].rearrange("g h d e -> d g h e"),
              writes=[mhd_b], allow_slow_non_contiguous=True)
        ddh = P.sb([64, 512], BF16); ddh_b = Buf()
        pev = P.sb([64, 512], BF16); pev_b = Buf()
        for gi in range(5):
            for h in range(16):
                for ts in range(Tt // 512):
                    sy.dma(ddh[:], dd_hbm[gi, h * 64:(h + 1) * 64,
                                          ts * 512:(ts + 1) * 512],
                           reads=[dd_b], writes=[ddh_b])
                    pt = pf[ts % 2]
                    t.matmul_acc(pt.ap[0:64, 0:512],
                                 [(mhd_sb[:, gi, h, :], ddh[:])],
                                 reads=[mhd_b, ddh_b], writes=[pt.buf])
                    v.op(lambda bb, pt=pt: bb.tensor_copy(pev[:], pt.ap[0:64, 0:512]),
                         reads=[pt.buf], writes=[pev_b])
                    sy.dma(kvrwu[gi, h * 64:(h + 1) * 64, ts * 512:(ts + 1) * 512],
                           pev[:], reads=[pev_b], writes=[kvrwu_b])
        gw = WeightFM(P, "l3gw", D, D).load(P, wts["L3_attn_gate_w"])
        dch5 = P.sb([128, 8, 512], BF16); dch5_b = Buf()
        gev = P.sb([128, 512], BF16); gev_b = Buf()
        for ts in range(Tt // 512):
            sy.dma(dch5[:], dd_hbm[5].rearrange("(c p) t -> p c t", p=128)
                   [:, :, ts * 512:(ts + 1) * 512], reads=[dd_b], writes=[dch5_b])
            for oc in range(8):
                pt = pf[oc % 2]
                pairs = [(gw.lhsT(dc, oc * 128, (oc + 1) * 128), dch5[:, dc, :])
                         for dc in range(8)]
                t.matmul_acc(pt.ap[:, 0:512], pairs, reads=[gw.buf, dch5_b],
                             writes=[pt.buf])
                s.op(lambda bb, pt=pt: bb.activation(gev[:], pt.ap[:, 0:512], AF.Silu),
                     reads=[pt.buf], writes=[gev_b])
                sy.dma(gate_hbm[oc * 128:(oc + 1) * 128, ts * 512:(ts + 1) * 512],
                       gev[:], reads=[gev_b], writes=[gate_b])
    import os
    dbg = None
    if os.environ.get("L3_DBG"):
        dbg = {"Y": (P.dscratch("l3_Ydbg", [D, Tt], BF16), Buf()),
               "C": (P.dscratch("l3_Cdbg", [D, Tt], F32), Buf()),
               "G": (P.dscratch("l3_Gdbg", [128, 2080], F32), Buf())}
    for hp in range(8):
        _l3_scan_pair(P, ctx, wts, hp, kvrwu, kvrwu_b, rwkv_hbm, rwkv_b,
                      a_hbm, a_hbm_b, dbg=dbg)
    with P.phase("l3epi"):
        gng_sb, gng_b = bias_cols(P, ctx, "l3gng", wts["L3_attn_gn_gamma"], D)
        gnbt_sb, gnbt_b = bias_cols(P, ctx, "l3gnb", wts["L3_attn_gn_beta"], D)
        ow = WeightFM(P, "l3ow", D, D).load(P, wts["L3_attn_out_w"])
        ob_sb, ob_b = bias_cols(P, ctx, "l3ob", wts["L3_attn_out_b"], D)
        nch = P.sb([128, 8, 512], BF16); nch_b = Buf()
        gch = P.sb([128, 8, 512], BF16); gch_b = Buf()
        evq = P.sb([128, 512], F32); evq_b = Buf()
        ot = [P.sb([128, D], F32) for _ in range(4)]
        ot_b = [Buf() for _ in range(4)]
        for ts in range(Tt // 512):
            sy.dma(nch[:], rwkv_hbm.rearrange("(c p) t -> p c t", p=128)
                   [:, :, ts * 512:(ts + 1) * 512], reads=[rwkv_b], writes=[nch_b])
            sy.dma(gch[:], gate_hbm.rearrange("(c p) t -> p c t", p=128)
                   [:, :, ts * 512:(ts + 1) * 512], reads=[gate_b], writes=[gch_b])
            for dc in range(8):
                v.op(lambda bb, dc=dc: bb.tensor_scalar(
                    nch[:, dc, :], nch[:, dc, :], gng_sb[:, dc:dc + 1],
                    None, ALU.mult),
                    reads=[gng_b], writes=[nch_b])
                v.op(lambda bb, dc=dc: bb.tensor_scalar(
                    nch[:, dc, :], nch[:, dc, :], gnbt_sb[:, dc:dc + 1],
                    None, ALU.add),
                    reads=[gnbt_b], writes=[nch_b])
                v.op(lambda bb, dc=dc: bb.tensor_mul(nch[:, dc, :], nch[:, dc, :],
                                                     gch[:, dc, :]),
                     reads=[gch_b], writes=[nch_b])
            for oc in range(8):
                pt = pf[oc % 2]
                pairs = [(ow.lhsT(dc, oc * 128, (oc + 1) * 128), nch[:, dc, :])
                         for dc in range(8)]
                t.matmul_acc(pt.ap[:, 0:512], pairs, reads=[ow.buf, nch_b],
                             writes=[pt.buf])
                v.op(lambda bb, pt=pt, oc=oc: bb.tensor_scalar(
                    evq[:], pt.ap[:, 0:512], ob_sb[:, oc:oc + 1], None, ALU.add),
                    reads=[pt.buf, ob_b], writes=[evq_b])
                for tsub in range(4):
                    pt2 = pf[4]
                    t.transpose(pt2.ap[:, 0:128], evq[:, tsub * 128:(tsub + 1) * 128],
                                idf_sb[:], reads=[evq_b, idf_buf], writes=[pt2.buf])
                    v.op(lambda bb, pt2=pt2, oc=oc, tsub=tsub: bb.tensor_copy(
                        ot[tsub][:, oc * 128:(oc + 1) * 128], pt2.ap[:, 0:128]),
                        reads=[pt2.buf], writes=[ot_b[tsub]])
                    if oc == 7:
                        ti = ts * 4 + tsub
                        sy.dma(attn_out[ti * 128:(ti + 1) * 128, :], ot[tsub][:],
                               reads=[ot_b[tsub]], writes=[attn_out_b])


def _l3_scan_pair(P, ctx, wts, hp, kvrwu, kvrwu_b, rwkv_hbm, rwkv_b, a_hbm, a_hbm_b,
                  dbg=None):
    v, s, t, sy, g = P.vector, P.scalar, P.tensor, P.sync, P.gpsimd
    NT = NTILE
    Tt = NT * 128
    NC = Tt // 32
    NS = Tt // 512 if Tt >= 512 else 1
    SC = min(512, Tt)
    idf_sb, idf_buf = cref(ctx, "ident_f32")
    idb_sb, idb_buf = cref(ctx, "ident_bf")
    onesr_sb, onesr_buf = cref(ctx, "ones_row")
    onesc_sb, onesc_buf = cref(ctx, "ones_col")
    epssb, epsbuf = cref(ctx, "eps_cols")
    m32_sb, m32_buf = cref(ctx, "mask_strict32")
    hsel_sb, hsel_buf = cref(ctx, "headsel")
    pf = ctx["pf"]; pbk = ctx["pb"]
    r0, r1 = hp * 128, (hp + 1) * 128
    with P.phase(f"l3sc{hp}"):
        kf = P.sb([128, Tt], BF16); kf_b = Buf()
        vf = P.sb([128, Tt], BF16); vf_b = Buf()
        rf = P.sb([128, Tt], BF16); rf_b = Buf()
        wf = P.sb([128, Tt], BF16); wf_b = Buf()
        uf = P.sb([128, Tt], BF16); uf_b = Buf()
        for ap_sb, ap_b, gi in ((kf, kf_b, 0), (vf, vf_b, 1), (rf, rf_b, 2),
                                (wf, wf_b, 3), (uf, uf_b, 4)):
            sy.dma(ap_sb[:], kvrwu[gi, r0:r1, :], reads=[kvrwu_b], writes=[ap_b])
        wa_sb = P.sb([128, 64], BF16); wa_b = Buf()
        wbv_sb = P.sb([128, 64], BF16); wbv_b = Buf()
        g.dma(wa_sb[:], wts["L3_attn_wkv_A"][2 * hp:2 * hp + 2]
              .rearrange("h d e -> (h d) e"), writes=[wa_b])
        g.dma(wbv_sb[:], wts["L3_attn_wkv_B"][2 * hp:2 * hp + 2]
              .rearrange("h d e -> (h d) e"), writes=[wbv_b])
        lam_sb = P.sb([128, 1], F32); lam_b = Buf()
        sy.dma(lam_sb[:], wts["L3_attn_wkv_lamb"][2 * hp:2 * hp + 2]
               .rearrange("h d -> (h d)").rearrange("(p o) -> p o", o=1),
               writes=[lam_b])
        wh = P.sb([128, SC], BF16); wh_b = Buf()
        NSUP = Tt // SC
        cpad = P.sb([128, NSUP, SC + 1], F32); cpad_b = Buf()
        v.op(lambda bb: bb.memset(cpad[:, :, 0:1], 0.0), writes=[cpad_b])
        zrow = P.sb([128, SC], F32); zrow_b = Buf()
        v.op(lambda bb: bb.memset(zrow[:], 0.0), writes=[zrow_b])
        lwt = P.sb([128, SC], F32); lwt_b = Buf()
        for ts in range(Tt // SC):
            for h2 in range(2):
                hb = h2 * 64
                pw = pf[2]
                t.matmul_acc(pw.ap[0:64, 0:SC] if hb == 0 else pw.ap[64:128, 0:SC],
                             [(wa_sb[hb:hb + 64, :], wf[hb:hb + 64, ts * SC:(ts + 1) * SC])],
                             reads=[wa_b, wf_b], writes=[pw.buf],
                             flags=[(True, True)])
            s.op(lambda bb, pw=pf[2]: bb.activation(wh[:], pw.ap[:, 0:SC], AF.Tanh),
                 reads=[pf[2].buf], writes=[wh_b])
            for h2 in range(2):
                hb = h2 * 64
                pw = pf[3]
                t.matmul_acc(pw.ap[0:64, 0:SC] if hb == 0 else pw.ap[64:128, 0:SC],
                             [(wbv_sb[hb:hb + 64, :], wh[hb:hb + 64, :])],
                             reads=[wbv_b, wh_b], writes=[pw.buf],
                             flags=[(True, True)])
            s.op(lambda bb, pw=pf[3]: bb.activation(lwt[:], pw.ap[:, 0:SC], AF.Exp,
                                                    bias=lam_sb[:]),
                 reads=[pf[3].buf, lam_b], writes=[lwt_b])
            v.op(lambda bb: bb.tensor_scalar_mul(lwt[:], lwt[:], -1.0),
                 reads=[lwt_b], writes=[lwt_b])
            v.op(lambda bb, ts=ts: bb.tensor_tensor_scan(
                cpad[:, ts, 1:SC + 1], lwt[:], zrow[:],
                0.0, ALU.add, ALU.add),
                reads=[lwt_b, zrow_b], writes=[cpad_b])
        Y = P.sb([128, Tt], BF16); Y_b = Buf()
        vr = P.sb([128, SC], BF16); vr_b = Buf()
        ar2 = P.sb([2, Tt], BF16); ar2_b = Buf()
        arow1 = P.sb([1, SC], BF16); arow1_b = Buf()
        ar_hbm = P.dscratch(f"l3_ar{hp}", [2, Tt], BF16); ar_hbm_b = Buf()
        for ts in range(Tt // SC):
            v.op(lambda bb, ts=ts: bb.tensor_tensor(
                vr[:], vf[:, ts * SC:(ts + 1) * SC], rf[:, ts * SC:(ts + 1) * SC],
                ALU.mult),
                reads=[vf_b, rf_b], writes=[vr_b])
            for h2 in range(2):
                pa = pf[2]
                t.matmul_acc(pa.ap[0:1, 0:SC],
                             [(onesc_sb[h2 * 64:h2 * 64 + 64, :],
                               vr[h2 * 64:h2 * 64 + 64, :])],
                             reads=[vr_b, onesc_buf], writes=[pa.buf])
                v.op(lambda bb, pa=pa: bb.tensor_copy(arow1[:], pa.ap[0:1, 0:SC]),
                     reads=[pa.buf], writes=[arow1_b])
                sy.dma(ar_hbm[h2, ts * SC:(ts + 1) * SC].rearrange("(o f) -> o f", o=1),
                       arow1[:], reads=[arow1_b], writes=[ar_hbm_b])
        sy.dma(ar2[:], ar_hbm, reads=[ar_hbm_b], writes=[ar2_b])
        for ts in range(Tt // SC):
            pb2 = pf[3]
            t.matmul_acc(pb2.ap[:, 0:SC], [(hsel_sb[:], ar2[:, ts * SC:(ts + 1) * SC])],
                         reads=[ar2_b, hsel_buf], writes=[pb2.buf])
            v.op(lambda bb, ts=ts, pb2=pb2: bb.tensor_tensor(
                Y[:, ts * SC:(ts + 1) * SC], uf[:, ts * SC:(ts + 1) * SC],
                pb2.ap[:, 0:SC], ALU.mult),
                reads=[uf_b, pb2.buf], writes=[Y_b])
            v.op(lambda bb, ts=ts: bb.tensor_tensor(
                Y[:, ts * SC:(ts + 1) * SC], Y[:, ts * SC:(ts + 1) * SC],
                kf[:, ts * SC:(ts + 1) * SC], ALU.mult),
                reads=[kf_b], writes=[Y_b])
        Sst = P.sb([128, 64], F32); Sst_b = Buf()
        v.op(lambda bb: bb.memset(Sst[:], 0.0), writes=[Sst_b])
        abuf = P.sb([64, 32], BF16); abuf_b = Buf()
        aflat = P.sb([2, 1024], BF16); aflat_b = Buf()
        etile = P.sb([128, 512], BF16); etile_b = Buf()
        gtile = P.sb([128, 512], F32); gtile_b = Buf()
        tmp32 = P.sb([128, 32], F32); tmp32_b = Buf()
        ftile = P.sb([128, 32], F32); ftile_b = Buf()
        rscr = P.sb([64, 32], BF16); rscr_b = Buf()
        st_sb = P.sb([64, 128], BF16); st_sb_b = Buf()
        kbar = P.sb([128, 32], BF16); kbar_b = Buf()
        kbt = P.sb([32, 128], BF16); kbt_b = Buf()
        vtk = P.sb([32, 128], BF16); vtk_b = Buf()
        dcol = P.sb([128, 1], F32); dcol_b = Buf()
        for c in range(NC):
            c0 = c * 32
            sc = c0 // SC
            lo = c0 % SC
            ctm1 = cpad[:, sc, lo:lo + 32]
            csrow = cpad[:, sc, lo + 1:lo + 33]
            ccol = cpad[:, sc, lo:lo + 1]
            cnewc = cpad[:, sc, lo + 32:lo + 33]
            pA = pf[2]
            for h2 in range(2):
                hb = h2 * 64
                t.matmul_acc(pA.ap[h2 * 32:h2 * 32 + 32, 0:32],
                             [(rf[hb:hb + 64, c0:c0 + 32], vf[hb:hb + 64, c0:c0 + 32])],
                             reads=[rf_b, vf_b], writes=[pA.buf],
                             flags=[(True, True)])
            for h2 in range(2):
                v.op(lambda bb, h2=h2, pA=pA: bb.tensor_tensor(
                    abuf[h2 * 32:h2 * 32 + 32, :], pA.ap[h2 * 32:h2 * 32 + 32, 0:32],
                    m32_sb[:], ALU.mult),
                    reads=[pA.buf, m32_buf], writes=[abuf_b])
            sy.dma(a_hbm, abuf[:], reads=[abuf_b], writes=[a_hbm_b])
            sy.dma(aflat[:], a_hbm.rearrange("(h t) s -> h (t s)", h=2),
                   reads=[a_hbm_b], writes=[aflat_b])
            pb3, pb4 = pf[3], pf[4]
            t.matmul_acc(pb3.ap[:, 0:512], [(hsel_sb[:], aflat[:, 0:512])],
                         reads=[aflat_b, hsel_buf], writes=[pb3.buf])
            t.matmul_acc(pb4.ap[:, 0:512], [(hsel_sb[:], aflat[:, 512:1024])],
                         reads=[aflat_b, hsel_buf], writes=[pb4.buf])
            for half, pbx, pbx_b in ((0, pb3.ap, pb3.buf), (1, pb4.ap, pb4.buf)):
                t0 = half * 16
                cin0 = cpad[:, sc, lo + t0:lo + t0 + 16]
                v.op(lambda bb, cin0=cin0, csrow=csrow: bb.tensor_tensor(
                    gtile[:].rearrange("p (a b) -> p a b", a=16),
                    cin0.rearrange("p (a o) -> p a o", o=1).broadcast_to([128, 16, 32]),
                    csrow.rearrange("p (o b) -> p o b", o=1)
                    .broadcast_to([128, 16, 32]), ALU.subtract),
                    reads=[cpad_b], writes=[gtile_b])
                v.op(lambda bb: bb.tensor_scalar_min(gtile[:], gtile[:], 0.0),
                     reads=[gtile_b], writes=[gtile_b])
                s.op(lambda bb: bb.activation(etile[:], gtile[:], AF.Exp),
                     reads=[gtile_b], writes=[etile_b])
                v.op(lambda bb, c0=c0: bb.tensor_tensor(
                    etile[:].rearrange("p (a b) -> p a b", a=16), 
                    etile[:].rearrange("p (a b) -> p a b", a=16),
                    kf[:, c0:c0 + 32].rearrange("p (o b) -> p o b", o=1)
                    .broadcast_to([128, 16, 32]), ALU.mult),
                    reads=[kf_b], writes=[etile_b])
                v.op(lambda bb, pbx=pbx: bb.tensor_tensor(gtile[:], etile[:],
                                                          pbx[:, 0:512], ALU.mult),
                     reads=[etile_b, pbx_b], writes=[gtile_b])
                if dbg is not None and hp == 0 and c == 0 and half == 0:
                    sy.dma(dbg["G"][0][:, 0:512], gtile[:], reads=[gtile_b],
                           writes=[dbg["G"][1]])
                    g.dma(dbg["G"][0][:, 512:1024], etile[:], reads=[etile_b],
                           writes=[dbg["G"][1]])
                    g.dma(dbg["G"][0][0:64, 1024:1056],
                           abuf[:, 0:32], reads=[abuf_b], writes=[dbg["G"][1]])
                    g.dma(dbg["G"][0][0:2, 1056:2080], aflat[:], reads=[aflat_b],
                           writes=[dbg["G"][1]])
                v.op(lambda bb: bb.tensor_reduce(
                    tmp32[:, 0:16],
                    gtile[:].rearrange("p (a b) -> p a b", a=16), AX.X, ALU.add),
                    reads=[gtile_b], writes=[tmp32_b])
                v.op(lambda bb, t0=t0, c0=c0: bb.tensor_tensor(
                    Y[:, c0 + t0:c0 + t0 + 16], Y[:, c0 + t0:c0 + t0 + 16],
                    tmp32[:, 0:16], ALU.add),
                    reads=[tmp32_b], writes=[Y_b])
            for h2 in range(2):
                pt6 = pbk[0]
                psS = pf[5]
                t.transpose(psS.ap[0:64, 0:64], Sst[h2 * 64:h2 * 64 + 64, :],
                            idf_sb[h2 * 64:h2 * 64 + 64, h2 * 64:h2 * 64 + 64],
                            reads=[Sst_b, idf_buf], writes=[psS.buf])
                s.op(lambda bb, psS=psS, h2=h2: bb.copy(st_sb[:, h2 * 64:h2 * 64 + 64],
                                                        psS.ap[0:64, 0:64]),
                     reads=[psS.buf], writes=[st_sb_b])
            v.op(lambda bb, c0=c0: bb.tensor_copy(rscr[:, 0:32], rf[64:128, c0:c0 + 32]),
                 reads=[rf_b], writes=[rscr_b])
            pI = pf[5]
            t.matmul_acc(pI.ap[0:64, 0:32],
                         [(st_sb[:, 0:64], rf[0:64, c0:c0 + 32])],
                         reads=[st_sb_b, rf_b], writes=[pI.buf],
                         flags=[(True, True)])
            t.matmul_acc(pI.ap[64:128, 0:32],
                         [(st_sb[:, 64:128], rscr[:, 0:32])],
                         reads=[st_sb_b, rscr_b], writes=[pI.buf],
                         flags=[(True, True)])
            v.op(lambda bb, ctm1=ctm1, ccol=ccol: bb.tensor_scalar(
                ftile[:], ctm1, ccol, None, ALU.subtract),
                reads=[cpad_b], writes=[ftile_b])
            s.op(lambda bb: bb.activation(ftile[:], ftile[:], AF.Exp),
                 reads=[ftile_b], writes=[ftile_b])
            v.op(lambda bb, pI=pI: bb.tensor_tensor(ftile[:], ftile[:],
                                                    pI.ap[:, 0:32], ALU.mult),
                 reads=[ftile_b, pI.buf], writes=[ftile_b])
            v.op(lambda bb, c0=c0: bb.tensor_tensor(
                Y[:, c0:c0 + 32], Y[:, c0:c0 + 32], ftile[:], ALU.add),
                reads=[ftile_b], writes=[Y_b])
            v.op(lambda bb, csrow=csrow, cnewc=cnewc: bb.tensor_scalar(
                tmp32[:], csrow, cnewc, None, ALU.subtract),
                reads=[cpad_b], writes=[tmp32_b])
            v.op(lambda bb: bb.tensor_scalar_mul(tmp32[:], tmp32[:], -1.0),
                 reads=[tmp32_b], writes=[tmp32_b])
            s.op(lambda bb: bb.activation(tmp32[:], tmp32[:], AF.Exp),
                 reads=[tmp32_b], writes=[tmp32_b])
            v.op(lambda bb, c0=c0: bb.tensor_tensor(kbar[:], tmp32[:],
                                                    kf[:, c0:c0 + 32], ALU.mult),
                 reads=[tmp32_b, kf_b], writes=[kbar_b])
            pt7 = pbk[1]
            t.transpose(pt7.ap[0:32, 0:128], kbar[:], idb_sb[:],
                        reads=[kbar_b, idb_buf], writes=[pt7.buf])
            v.op(lambda bb, pt7=pt7: bb.tensor_copy(kbt[:], pt7.ap[0:32, 0:128]),
                 reads=[pt7.buf], writes=[kbt_b])
            pt8 = pbk[1]
            t.transpose(pt8.ap[0:32, 0:128], vf[:, c0:c0 + 32], idb_sb[:],
                        reads=[vf_b, idb_buf], writes=[pt8.buf])
            v.op(lambda bb, pt8=pt8: bb.tensor_copy(vtk[:], pt8.ap[0:32, 0:128]),
                 reads=[pt8.buf], writes=[vtk_b])
            v.op(lambda bb, cnewc=cnewc, ccol=ccol: bb.tensor_tensor(
                dcol[:], cnewc, ccol, ALU.subtract),
                reads=[cpad_b], writes=[dcol_b])
            s.op(lambda bb: bb.activation(dcol[:], dcol[:], AF.Exp),
                 reads=[dcol_b], writes=[dcol_b])
            v.op(lambda bb: bb.tensor_scalar(Sst[:], Sst[:], dcol[:], None, ALU.mult),
                 reads=[dcol_b], writes=[Sst_b])
            pD = pf[2]
            t.matmul_acc(pD.ap[0:64, 0:64], [(kbt[:, 0:64], vtk[:, 0:64])],
                         reads=[kbt_b, vtk_b], writes=[pD.buf], flags=[(True, True)])
            t.matmul_acc(pD.ap[64:128, 0:64], [(kbt[:, 64:128], vtk[:, 64:128])],
                         reads=[kbt_b, vtk_b], writes=[pD.buf], flags=[(True, True)])
            v.op(lambda bb, pD=pD: bb.tensor_tensor(Sst[:], Sst[:], pD.ap[:, 0:64],
                                                    ALU.add),
                 reads=[pD.buf, Sst_b], writes=[Sst_b])
        if dbg is not None:
            for ts in range(Tt // SC):
                sy.dma(dbg["Y"][0][r0:r1, ts * SC:(ts + 1) * SC],
                       Y[:, ts * SC:(ts + 1) * SC], reads=[Y_b], writes=[dbg["Y"][1]])
            for sc2 in range(Tt // SC):
                sy.dma(dbg["C"][0][r0:r1, sc2 * SC:(sc2 + 1) * SC],
                       cpad[:, sc2, 1:SC + 1], reads=[cpad_b], writes=[dbg["C"][1]])
        sqv = P.sb([128, SC], BF16); sqv_b = Buf()
        mrow = P.sb([1, SC], F32); mrow_b = Buf()
        vrow = P.sb([1, SC], F32); vrow_b = Buf()
        brow = P.sb([1, SC], BF16); brow_b = Buf()
        nout = P.sb([128, SC], BF16); nout_b = Buf()
        for ts in range(Tt // SC):
            ysl = Y[:, ts * SC:(ts + 1) * SC]
            s.op(lambda bb, ysl=ysl: bb.activation(sqv[:], ysl, AF.Square),
                 reads=[Y_b], writes=[sqv_b])
            for h2 in range(2):
                hb = h2 * 64
                pm = pf[2]
                pq = pf[3]
                t.matmul_acc(pm.ap[0:1, 0:SC],
                             [(onesc_sb[hb:hb + 64, :], ysl[hb:hb + 64, :]
                               if False else Y[hb:hb + 64, ts * SC:(ts + 1) * SC])],
                             reads=[Y_b, onesc_buf], writes=[pm.buf])
                t.matmul_acc(pq.ap[0:1, 0:SC],
                             [(onesc_sb[hb:hb + 64, :], sqv[hb:hb + 64, :])],
                             reads=[sqv_b, onesc_buf], writes=[pq.buf])
                v.op(lambda bb, pm=pm: bb.tensor_scalar_mul(mrow[:], pm.ap[0:1, 0:SC],
                                                            1.0 / 64),
                     reads=[pm.buf], writes=[mrow_b])
                v.op(lambda bb, pq=pq: bb.tensor_scalar_mul(vrow[:], pq.ap[0:1, 0:SC],
                                                            1.0 / 64),
                     reads=[pq.buf], writes=[vrow_b])
                v.op(lambda bb: bb.tensor_mul(zrow[0:1, :], mrow[:], mrow[:]),
                     reads=[mrow_b], writes=[zrow_b])
                v.op(lambda bb: bb.tensor_tensor(vrow[:], vrow[:], zrow[0:1, :],
                                                 ALU.subtract),
                     reads=[zrow_b, vrow_b], writes=[vrow_b])
                s.op(lambda bb: bb.activation(vrow[:], vrow[:], AF.Ln,
                                              bias=epssb[0:1, 1:2]),
                     reads=[vrow_b, epsbuf], writes=[vrow_b])
                s.op(lambda bb: bb.activation(vrow[:], vrow[:], AF.Exp, scale=-0.5),
                     reads=[vrow_b], writes=[vrow_b])
                v.op(lambda bb: bb.tensor_copy(brow[:], mrow[:]),
                     reads=[mrow_b], writes=[brow_b])
                pmb = pf[4]
                t.matmul_acc(pmb.ap[0:64, 0:SC], [(onesr_sb[:, 0:64], brow[:])],
                             reads=[brow_b, onesr_buf], writes=[pmb.buf])
                v.op(lambda bb: bb.tensor_copy(brow[:], vrow[:]),
                     reads=[vrow_b], writes=[brow_b])
                prb = pf[5]
                t.matmul_acc(prb.ap[0:64, 0:SC], [(onesr_sb[:, 0:64], brow[:])],
                             reads=[brow_b, onesr_buf], writes=[prb.buf])
                v.op(lambda bb, hb=hb, ts=ts, pmb=pmb: bb.tensor_tensor(
                    nout[hb:hb + 64, :], Y[hb:hb + 64, ts * SC:(ts + 1) * SC],
                    pmb.ap[0:64, 0:SC], ALU.subtract),
                    reads=[Y_b, pmb.buf], writes=[nout_b])
                v.op(lambda bb, hb=hb, prb=prb: bb.tensor_tensor(
                    nout[hb:hb + 64, :], nout[hb:hb + 64, :], prb.ap[0:64, 0:SC],
                    ALU.mult),
                    reads=[prb.buf], writes=[nout_b])
            sy.dma(rwkv_hbm[r0:r1, ts * SC:(ts + 1) * SC], nout[:],
                   reads=[nout_b], writes=[rwkv_b])


def emit_l3_ffn(P, ctx, wts, xn_fm_sb, xn_fm_buf, ffn_out, ffn_out_b):
    v, s, t, sy, g = P.vector, P.scalar, P.tensor, P.sync, P.gpsimd
    NT = NTILE
    Tt = NT * 128
    idf_sb, idf_buf = cref(ctx, "ident_f32")
    onesr_sb, onesr_buf = cref(ctx, "ones_row")
    pf = ctx["pf"]
    with P.phase("l3ffn"):
        wq = WeightFM(P, "l3wq", D, D)
        wraw = P.sb([128, 512], F32); wraw_b = Buf()
        wi = P.sb([128, 512], I32); wi_b = Buf()
        wf2 = P.sb([128, 512], F32); wf2_b = Buf()
        mcol = P.sb([128, 1], F32); mcol_b = Buf()
        macc = P.sb([128, 1], F32); macc_b = Buf()
        mpad = P.sb([128, 128], F32); mpad_b = Buf()
        srow = P.sb([1, 128], F32); srow_b = Buf()
        sval = P.sb([1, 1], BF16); sval_b = Buf()
        scol = P.sb([128, 1], F32); scol_b = Buf()
        rcol = P.sb([128, 1], F32); rcol_b = Buf()
        nt2 = D * D // (128 * 512)
        for i in range(nt2):
            a_, b_ = divmod(i, 2)
            sy.dma(wraw[:], wts["L3_ffn_kernel"][a_ * 128:(a_ + 1) * 128,
                                                 b_ * 512:(b_ + 1) * 512],
                   reads=[], writes=[wraw_b])
            v.op(lambda bb: bb.tensor_reduce(mcol[:], wraw[:], AX.X, ALU.max,
                                             apply_absolute_value=True),
                 reads=[wraw_b], writes=[mcol_b])
            if i == 0:
                v.op(lambda bb: bb.tensor_copy(macc[:], mcol[:]),
                     reads=[mcol_b], writes=[macc_b])
            else:
                v.op(lambda bb: bb.tensor_max(macc[:], macc[:], mcol[:]),
                     reads=[mcol_b, macc_b], writes=[macc_b])
        v.op(lambda bb: bb.memset(mpad[:], 0.0), writes=[mpad_b])
        v.op(lambda bb: bb.tensor_copy(mpad[:, 0:1], macc[:]),
             reads=[macc_b], writes=[mpad_b])
        pt = pf[0]
        t.transpose(pt.ap[:, 0:128], mpad[:], idf_sb[:],
                    reads=[mpad_b, idf_buf], writes=[pt.buf])
        s.op(lambda bb, pt=pt: bb.copy(srow[:], pt.ap[0:1, 0:128]),
             reads=[pt.buf], writes=[srow_b])
        v.op(lambda bb: bb.tensor_reduce(srow[:, 0:1], srow[:], AX.X, ALU.max),
             reads=[srow_b], writes=[srow_b])
        v.op(lambda bb: bb.tensor_scalar_mul(srow[:, 0:1], srow[:, 0:1], 1.0 / 7.0),
             reads=[srow_b], writes=[srow_b])
        sc_hbm = P.dscratch("l3_scale", [1, 1], F32)
        sc_hbm_b = Buf()
        sy.dma(sc_hbm, srow[0:1, 0:1], reads=[srow_b], writes=[sc_hbm_b])
        sy.dma(scol[:], sc_hbm.broadcast_to([128, 1]), reads=[sc_hbm_b],
               writes=[scol_b], allow_slow_non_contiguous=True)
        v.op(lambda bb: bb.reciprocal(rcol[:], scol[:]),
             reads=[scol_b], writes=[rcol_b])
        wqv = wq.sb[:].rearrange("p c o -> p (c o)")
        for i in range(nt2):
            a_, b_ = divmod(i, 2)
            sy.dma(wraw[:], wts["L3_ffn_kernel"][a_ * 128:(a_ + 1) * 128,
                                                 b_ * 512:(b_ + 1) * 512],
                   reads=[], writes=[wraw_b])
            v.op(lambda bb: bb.tensor_scalar(wf2[:], wraw[:], rcol[:], None, ALU.mult),
                 reads=[wraw_b, rcol_b], writes=[wf2_b])
            v.op(lambda bb: bb.tensor_copy(wi[:], wf2[:]),
                 reads=[wf2_b], writes=[wi_b])
            v.op(lambda bb: bb.tensor_copy(wf2[:], wi[:]),
                 reads=[wi_b], writes=[wf2_b])
            v.op(lambda bb: bb.tensor_scalar_min(wf2[:], wf2[:], 7.0),
                 reads=[wf2_b], writes=[wf2_b])
            v.op(lambda bb: bb.tensor_scalar_max(wf2[:], wf2[:], -8.0),
                 reads=[wf2_b], writes=[wf2_b])
            v.op(lambda bb, a_=a_, b_=b_: bb.tensor_scalar(
                wq.sb[:, a_, b_ * 512:(b_ + 1) * 512], wf2[:], scol[:], None, ALU.mult),
                reads=[wf2_b, scol_b], writes=[wq.buf])
        bias_sb, bias_b = bias_cols(P, ctx, "l3fb", wts["L3_ffn_bias"], D)
        a_sb = P.sb([128, 8, 16], BF16); a_b = Buf()
        g.dma(a_sb[:], wts["L3_ffn_a"].rearrange("(c p) r -> p c r", p=128),
              writes=[a_b])
        b_sb = P.sb([16, D], BF16); bmat_b = Buf()
        g.dma(b_sb[:], wts["L3_ffn_b"], writes=[bmat_b])
        xa = P.sb([16, 512], BF16); xa_b = Buf()
        evq = P.sb([128, 512], F32); evq_b = Buf()
        ot = [P.sb([128, D], F32) for _ in range(4)]
        ot_b = [Buf() for _ in range(4)]
        for ts in range(Tt // 512):
            pxa = pf[2]
            pairs = [(a_sb[:, dc, :], xn_fm_sb[:, dc, ts * 512:(ts + 1) * 512])
                     for dc in range(8)]
            t.matmul_acc(pxa.ap[0:16, 0:512], pairs, reads=[a_b, xn_fm_buf],
                         writes=[pxa.buf])
            v.op(lambda bb, pxa=pxa: bb.tensor_copy(xa[:], pxa.ap[0:16, 0:512]),
                 reads=[pxa.buf], writes=[xa_b])
            for oc in range(8):
                pt2 = pf[oc % 2]
                pairs = [(wq.lhsT(dc, oc * 128, (oc + 1) * 128),
                          xn_fm_sb[:, dc, ts * 512:(ts + 1) * 512]) for dc in range(8)]
                pairs.append((b_sb[:, oc * 128:(oc + 1) * 128], xa[:]))
                flags = [(i2 == 0, False) for i2 in range(8)] + [(False, True)]
                t.matmul_acc(pt2.ap[:, 0:512], pairs,
                             reads=[wq.buf, xn_fm_buf, bmat_b, xa_b],
                             writes=[pt2.buf], flags=flags)
                v.op(lambda bb, pt2=pt2, oc=oc: bb.tensor_scalar(
                    evq[:], pt2.ap[:, 0:512], bias_sb[:, oc:oc + 1], None, ALU.add),
                    reads=[pt2.buf, bias_b], writes=[evq_b])
                for tsub in range(4):
                    pt3 = pf[4]
                    t.transpose(pt3.ap[:, 0:128], evq[:, tsub * 128:(tsub + 1) * 128],
                                idf_sb[:], reads=[evq_b, idf_buf], writes=[pt3.buf])
                    v.op(lambda bb, pt3=pt3, oc=oc, tsub=tsub: bb.tensor_copy(
                        ot[tsub][:, oc * 128:(oc + 1) * 128], pt3.ap[:, 0:128]),
                        reads=[pt3.buf], writes=[ot_b[tsub]])
                    if oc == 7:
                        ti = ts * 4 + tsub
                        sy.dma(ffn_out[ti * 128:(ti + 1) * 128, :], ot[tsub][:],
                               reads=[ot_b[tsub]], writes=[ffn_out_b])


CAP = 5120
SEG0 = 768
SEG1 = 1536
NSLOT0 = 8 * SEG0
NSLOT1 = 8 * SEG1

K_L1ATTN = ["L1_attn_ln1_g", "L1_attn_ln1_b", "L1_attn_w_qkv", "L1_attn_b_qkv",
            "L1_attn_w_out", "L1_attn_b_out", "L1_attn_ln2_g", "L1_attn_ln2_b",
            "L1_attn_w1", "L1_attn_b1", "L1_attn_w2", "L1_attn_b2"]
K_L3 = ["L3_attn_mu_x", "L3_attn_lora_lambda", "L3_attn_lora_A_w",
        "L3_attn_lora_A_b", "L3_attn_lora_B_w", "L3_attn_lora_B_b",
        "L3_attn_mhd_w", "L3_attn_gate_w", "L3_attn_wkv_lamb", "L3_attn_wkv_A",
        "L3_attn_wkv_B", "L3_attn_gn_gamma", "L3_attn_gn_beta", "L3_attn_out_w",
        "L3_attn_out_b", "L3_ffn_kernel", "L3_ffn_bias", "L3_ffn_a", "L3_ffn_b"]


def _hbm_copy(P, src, src_b, dst, dst_b, dtype, rows, cols):
    sy = P.sync
    with P.phase("hcopy"):
        tb = [P.sb([128, cols], dtype) for _ in range(2)]
        tbb = [Buf() for _ in range(2)]
        for i in range(rows // 128):
            b = i % 2
            sy.dma(tb[b][:], src[i * 128:(i + 1) * 128, :], reads=[src_b],
                   writes=[tbb[b]])
            sy.dma(dst[i * 128:(i + 1) * 128, :], tb[b][:], reads=[tbb[b]],
                   writes=[dst_b])


def _wts_din(P, shapes, keys):
    return {k: P.din(k, list(shapes[k]), F32) for k in keys}


def _load_rlog(P, name):
    rlog_in = P.din(name, [128, NTILE * 8], F32)
    rl = P.sb([128, NTILE, 8], F32, name=name + "sb"); rl_b = Buf()
    P.sync.dma(rl[:], rlog_in.rearrange("p (n e) -> p n e", e=8), writes=[rl_b])
    return rl, rl_b


def _build_A(P, shapes):
    x_src = P.din("x_in", [T, D], F32)
    wts = _wts_din(P, shapes, ["L0_ffn_Wr", "L0_attn_wq", "L0_attn_wk",
                               "L0_attn_wv", "L0_attn_wo"])
    xnt0o = P.dout("xnt0", [T, D], BF16); xnt0o_b = Buf()
    attn0 = P.dout("attn0", [T, D], F32); attn0_b = Buf()
    rlog_out = P.dout("rlog0", [128, NTILE * 8], F32); rlog_out_b = Buf()
    disp0 = P.dout("disp0", [NSLOT0, D], BF16); disp0_b = Buf()
    ctx = make_ctx(P)
    xn_fm = P.sb([128, 8, T], BF16, name="xn_fm"); xn_fm_buf = Buf("xn_fm")
    wr_sb = P.sb([128, 8, 8], BF16, name="wr0bf"); wr_buf = Buf()
    P.gpsimd.dma(wr_sb[:], wts["L0_ffn_Wr"].rearrange("(c p) e -> p c e", p=128),
                 writes=[wr_buf])
    rlog0 = P.sb([128, NTILE, 8], F32, name="rlog0sb"); rlog0_b = Buf()
    xnt0 = P.dscratch("xn_tok0", [T, D], BF16); xnt0_b = Buf()
    with P.phase("pre0"):
        emit_pre_pass(P, ctx, (x_src, Buf()), [], None, (xnt0, xnt0_b),
                      xn_fm, xn_fm_buf, xn_tok2=(xnt0o, xnt0o_b))
    emit_rlog_fm(P, ctx, wr_sb, wr_buf, xn_fm, xn_fm_buf, rlog0, rlog0_b)
    P.sync.dma(rlog_out, rlog0[:].rearrange("p n e -> p (n e)"),
               reads=[rlog0_b], writes=[rlog_out_b])
    with P.phase("l0disp"):
        emit_l0_moe(P, ctx, wts, (xnt0, xnt0_b), None, rlog0, rlog0_b,
                    collective=False, cap=CAP, seg=SEG0, stage="disp",
                    disp_io=(disp0, disp0_b))
    emit_l0_attn(P, ctx, wts, xn_fm, xn_fm_buf, attn0, attn0_b)
    P.final_barrier()


def _build_B2(P, shapes):
    xnt0 = P.din("xnt0", [T, D], BF16); xnt0_b = Buf()
    disp0 = P.din("disp0", [NSLOT0, D], BF16); disp0_b = Buf()
    wts = _wts_din(P, shapes, ["L0_ffn_W1", "L0_ffn_W2"])
    ffn0 = P.dout("ffn0", [T, D], F32); ffn0_b = Buf()
    ctx = make_ctx(P)
    rlog0, rlog0_b = _load_rlog(P, "rlog0")
    with P.phase("l0moe"):
        emit_l0_moe(P, ctx, wts, (xnt0, xnt0_b), (ffn0, ffn0_b),
                    rlog0, rlog0_b, collective=True, cap=CAP, seg=SEG0,
                    stage="expcomb", disp_io=(disp0, disp0_b))
    P.final_barrier()


def _build_C(P, shapes):
    x_src = P.din("x_in", [T, D], F32)
    attn0 = P.din("attn0", [T, D], F32)
    ffn0 = P.din("ffn0", [T, D], F32)
    wts = _wts_din(P, shapes, ["L1_ffn_Wg"] + K_L1ATTN)
    xr0 = P.dout("xr0", [T, D], F32); xr0_b = Buf()
    xnt1o = P.dout("xnt1", [T, D], BF16); xnt1o_b = Buf()
    attn1 = P.dout("attn1", [T, D], F32); attn1_b = Buf()
    rlog_out = P.dout("rlog1", [128, NTILE * 8], F32); rlog_out_b = Buf()
    disp1 = P.dout("disp1", [NSLOT1, D], BF16); disp1_b = Buf()
    ctx = make_ctx(P)
    xn_fm = P.sb([128, 8, T], BF16, name="xn_fm"); xn_fm_buf = Buf("xn_fm")
    wg_sb = P.sb([128, 8, 8], BF16, name="wg1bf"); wg_buf = Buf()
    P.gpsimd.dma(wg_sb[:], wts["L1_ffn_Wg"].rearrange("(c p) e -> p c e", p=128),
                 writes=[wg_buf])
    rlog1 = P.sb([128, NTILE, 8], F32, name="rlog1sb"); rlog1_b = Buf()
    xnt1 = P.dscratch("xn_tok1", [T, D], BF16); xnt1_b = Buf()
    with P.phase("pre1"):
        emit_pre_pass(P, ctx, (x_src, Buf()), [(attn0, Buf()), (ffn0, Buf())],
                      (xr0, xr0_b), (xnt1, xnt1_b), xn_fm, xn_fm_buf,
                      xn_tok2=(xnt1o, xnt1o_b))
    emit_rlog_fm(P, ctx, wg_sb, wg_buf, xn_fm, xn_fm_buf, rlog1, rlog1_b)
    P.sync.dma(rlog_out, rlog1[:].rearrange("p n e -> p (n e)"),
               reads=[rlog1_b], writes=[rlog_out_b])
    emit_l1_moe(P, ctx, wts, None, None, (xnt1, xnt1_b), None, seg=SEG1,
                rlog=(rlog1, rlog1_b), stage="disp1", disp_io=(disp1, disp1_b))
    emit_l1_attn(P, ctx, wts, (xnt1, xnt1_b), attn1, attn1_b)
    P.final_barrier()


def _build_D2(P, shapes):
    xnt1 = P.din("xnt1", [T, D], BF16); xnt1_b = Buf()
    disp1i = P.din("disp1", [NSLOT1, D], BF16); disp1i_b = Buf()
    wts = _wts_din(P, shapes, ["L1_ffn_W1", "L1_ffn_W2", "L1_ffn_b1",
                               "L1_ffn_b2"])
    eout1 = P.dout("eout1", [NSLOT1, D], F32); eout1_b = Buf()
    ctx = make_ctx(P)
    rlog1, rlog1_b = _load_rlog(P, "rlog1")
    disp1 = P.dscratch("l1_dispS", [NSLOT1, D], BF16); disp1_b = Buf()
    emit_l1_moe(P, ctx, wts, None, None, (xnt1, xnt1_b), None, seg=SEG1,
                rlog=(rlog1, rlog1_b), stage="disp2exp",
                disp_io=(disp1, disp1_b), disp_in=(disp1i, disp1i_b),
                eout_io=(eout1, eout1_b))
    P.final_barrier()


def _build_D3(P, shapes):
    xnt1 = P.din("xnt1", [T, D], BF16); xnt1_b = Buf()
    eout1 = P.din("eout1", [NSLOT1, D], F32); eout1_b = Buf()
    wts = {}
    part1 = P.dout("part1", [T, D], F32); part1_b = Buf()
    ctx = make_ctx(P)
    rlog1, rlog1_b = _load_rlog(P, "rlog1")
    emit_l1_moe(P, ctx, wts, None, None, (xnt1, xnt1_b), None, seg=SEG1,
                rlog=(rlog1, rlog1_b), stage="comb1",
                eout_io=(eout1, eout1_b), part_io=(part1, part1_b))
    P.final_barrier()


def _build_D4(P, shapes):
    xnt1 = P.din("xnt1", [T, D], BF16); xnt1_b = Buf()
    eout1 = P.din("eout1", [NSLOT1, D], F32); eout1_b = Buf()
    part1 = P.din("part1", [T, D], F32); part1_b = Buf()
    wts = {}
    ffn1 = P.dout("ffn1", [T, D], F32); ffn1_b = Buf()
    ctx = make_ctx(P)
    rlog1, rlog1_b = _load_rlog(P, "rlog1")
    emit_l1_moe(P, ctx, wts, None, None, (xnt1, xnt1_b), (ffn1, ffn1_b),
                seg=SEG1, rlog=(rlog1, rlog1_b), stage="comb2",
                eout_io=(eout1, eout1_b), part_io=(part1, part1_b))
    P.final_barrier()


def _build_EF(P, shapes):
    xnt1 = P.din("xnt1", [T, D], BF16); xnt1_b = Buf()
    eout1 = P.din("eout1", [NSLOT1, D], F32); eout1_b = Buf()
    part1 = P.din("part1", [T, D], F32); part1_b = Buf()
    xr0 = P.din("xr0", [T, D], F32)
    attn1 = P.din("attn1", [T, D], F32)
    wts = _wts_din(P, shapes, ["L2_ffn_controller", "L2_attn_Wq", "L2_attn_Wk",
                               "L2_attn_Wv", "L2_attn_Wo", "L2_attn_gn_weight",
                               "L2_attn_gn_bias", "L2_ffn_f1", "L2_ffn_f2",
                               "L2_ffn_bias"] + K_L3)
    y_out = P.dout("y", [T, D], F32)
    ctx = make_ctx(P)
    rlog1, rlog1_b = _load_rlog(P, "rlog1")
    ffn1 = P.dscratch("ffn1", [T, D], F32); ffn1_b = Buf()
    emit_l1_moe(P, ctx, wts, None, None, (xnt1, xnt1_b), (ffn1, ffn1_b),
                seg=SEG1, rlog=(rlog1, rlog1_b), stage="comb2",
                eout_io=(eout1, eout1_b), part_io=(part1, part1_b))
    xn_fm = P.sb([128, 8, T], BF16, name="xn_fm"); xn_fm_buf = Buf("xn_fm")
    ctl_sb = P.sb([128, 8, 64], BF16, name="ctlbf"); ctl_buf = Buf()
    P.gpsimd.dma(ctl_sb[:], wts["L2_ffn_controller"]
                 .rearrange("(c p) e s -> p c (e s)", p=128), writes=[ctl_buf])
    xr1 = P.dscratch("x_res1", [T, D], F32); xr1_b = Buf()
    xnt2 = P.dscratch("xn_tok2", [T, D], BF16); xnt2_b = Buf()
    l2log = P.dscratch("l2log_h", [64, T], F32); l2log_b = Buf()
    with P.phase("pre2"):
        emit_pre_pass(P, ctx, (xr0, Buf()), [(attn1, Buf()), (ffn1, ffn1_b)],
                      (xr1, xr1_b), (xnt2, xnt2_b), xn_fm, xn_fm_buf)
    emit_l2log_fm(P, ctx, ctl_sb, ctl_buf, xn_fm, xn_fm_buf, l2log, l2log_b)
    attn2 = P.dscratch("attn2", [T, D], F32); attn2_b = Buf()
    emit_l2_attn(P, ctx, wts, xn_fm, xn_fm_buf, attn2, attn2_b)
    ffn2 = P.dscratch("ffn2", [T, D], F32); ffn2_b = Buf()
    emit_l2_ffn(P, ctx, wts, (xnt2, xnt2_b), (ffn2, ffn2_b), l2log, l2log_b)
    xr2 = P.dscratch("x_res2", [T, D], F32); xr2_b = Buf()
    xnt3 = P.dscratch("xn_tok3", [T, D], BF16); xnt3_b = Buf()
    with P.phase("pre3"):
        emit_pre_pass(P, ctx, (xr1, xr1_b), [(attn2, attn2_b), (ffn2, ffn2_b)],
                      (xr2, xr2_b), (xnt3, xnt3_b), xn_fm, xn_fm_buf)
    attn3 = P.dscratch("attn3", [T, D], F32); attn3_b = Buf()
    ffn3 = P.dscratch("ffn3", [T, D], F32); ffn3_b = Buf()
    emit_l3_attn(P, ctx, wts, xn_fm, xn_fm_buf, (xnt3, xnt3_b), attn3, attn3_b)
    emit_l3_ffn(P, ctx, wts, xn_fm, xn_fm_buf, ffn3, ffn3_b)
    with P.phase("final"):
        emit_pre_pass(P, ctx, (xr2, xr2_b), [(attn3, attn3_b), (ffn3, ffn3_b)],
                      (y_out, Buf()), None, None, None, final=True)
    P.final_barrier()


_LAUNCHES = (
    (_build_A, ("x_in",),
     ["L0_ffn_Wr", "L0_attn_wq", "L0_attn_wk", "L0_attn_wv", "L0_attn_wo"]),
    (_build_B2, ("xnt0", "rlog0", "disp0", "coremask"),
     ["L0_ffn_W1", "L0_ffn_W2"]),
    (_build_C, ("x_in", "attn0", "ffn0"), ["L1_ffn_Wg"] + K_L1ATTN),
    (_build_D2, ("xnt1", "rlog1", "disp1"),
     ["L1_ffn_W1", "L1_ffn_W2", "L1_ffn_b1", "L1_ffn_b2"]),
    (_build_D3, ("xnt1", "rlog1", "eout1"), []),
    (_build_EF, ("xnt1", "rlog1", "eout1", "part1", "xr0", "attn1"),
     ["L2_ffn_controller", "L2_attn_Wq", "L2_attn_Wk", "L2_attn_Wv",
      "L2_attn_Wo", "L2_attn_gn_weight", "L2_attn_gn_bias", "L2_ffn_f1",
      "L2_ffn_f2", "L2_ffn_bias"] + K_L3),
)


def kernel(**inputs):
    from concourse.bass_utils import run_bass_kernel_spmd
    x = np.ascontiguousarray(np.asarray(inputs["L0_attn_x"], dtype=np.float32))
    wkeys = [k for k in inputs if k != "L0_attn_x"]
    shapes = {k: np.asarray(inputs[k]).shape for k in wkeys}
    wmap = {k: np.ascontiguousarray(np.asarray(inputs[k], dtype=np.float32))
            for k in wkeys}
    cm = []
    for c in range(8):
        m = np.zeros((8, 1), dtype=np.dtype(ml_dtypes.bfloat16))
        m[:c] = 1
        cm.append(m)
    state = [{"x_in": x[c], "coremask": cm[c]} for c in range(8)]
    for build_fn, extra, kws in _LAUNCHES:
        P = Prog()
        with P.es:
            build_fn(P, shapes)
            P.finalize(sim=False)
            maps = []
            for c in range(8):
                m = {k: wmap[k] for k in kws}
                for k in extra:
                    m[k] = state[c][k]
                m.update(P._consts)
                maps.append(m)
            res = run_bass_kernel_spmd(P.nc, maps, core_ids=list(range(8)))
        for c in range(8):
            for k, v in res.results[c].items():
                state[c][k] = np.asarray(v)
    out = np.stack([np.asarray(state[c]["y"], dtype=np.float32)
                    for c in range(8)], axis=0)
    return out
